# version 01
# baseline (speedup 1.0000x reference)
import numpy as np
import ml_dtypes
from contextlib import ExitStack

import concourse.bass as bass
import concourse.mybir as mybir

F32 = mybir.dt.float32
BF16 = mybir.dt.bfloat16
AF = mybir.ActivationFunctionType
ALU = mybir.AluOpType
AXX = mybir.AxisListType.X

D = 1024
KC = 8
NTOK = 2048
TT = 16
TB = 4
NE = 32
EI = 768
EIC = 6
ISH = 2048
T = 4096
NH = 8
QB = 8
EPS = 1e-5

ENGS = ("te", "dv", "ac", "sp", "gp")
BF = ml_dtypes.bfloat16


class Prog:
    def __init__(self, nc):
        self.nc = nc
        self.q = {e: [] for e in ENGS}
        self.sems = {}
        self.cnt = {}
        self.ctx = ExitStack()
        for s in ("TE", "DV", "AC", "GPD", "SPD", "AGS", "RSS"):
            self.sems[s] = self.ctx.enter_context(nc.semaphore(s))
            self.cnt[s] = 0

    def op(self, eng, fn, inc=None, k=1):
        if inc is None:
            self.q[eng].append(lambda e, fn=fn: fn(e))
            return None
        self.cnt[inc] += k
        s = self.sems[inc]
        self.q[eng].append(lambda e, fn=fn, s=s, k=k: fn(e).then_inc(s, k))
        return self.cnt[inc]

    def wait(self, eng, sem, val):
        if val is None or val <= 0:
            return
        self.q[eng].append(lambda e, s=self.sems[sem], v=val: e.wait_ge(s, v))

    def fence_te(self):
        self.wait("te", "DV", self.cnt["DV"])
        self.wait("te", "AC", self.cnt["AC"])

    def fence_dma(self, eng):
        for s in ("TE", "DV", "AC"):
            self.wait(eng, s, self.cnt[s])

    def flush(self, block):
        def runner(fns):
            def go(e):
                for f in fns:
                    f(e)
            return go
        block.tensor(runner(self.q["te"]))
        block.vector(runner(self.q["dv"]))
        block.scalar(runner(self.q["ac"]))
        block.sync(runner(self.q["sp"]))
        block.gpsimd(runner(self.q["gp"]))


def r128(dram_ap):
    return dram_ap.rearrange("(c p) n -> p c n", p=128)


def emit_ln(p, x, ln, xlnT, ident, tpA, tpB, ln_fence=True):
    if ln_fence:
        p.wait("ac", "DV", p.cnt["DV"])
    for tt in range(TT):
        p.op("dv", lambda e, tt=tt: e.reduce_sum(
            ln["sums"][:, tt:tt+1], x[:, tt*1024:(tt+1)*1024], axis=AXX),
            inc="DV" if tt == TT - 1 else None)
    p.wait("dv", "DV", p.cnt["DV"])
    v_nm = p.op("dv", lambda e: e.tensor_scalar(
        ln["nm"][:, 0:TT], ln["sums"][:, 0:TT], -1.0 / D, None, ALU.mult), inc="DV")
    p.wait("ac", "DV", v_nm)
    v_red = {}
    for tt in range(TT):
        slot = tt % 2
        if tt >= 2:
            p.wait("ac", "DV", v_red[tt - 2])
        v_sq = p.op("ac", lambda e, tt=tt, slot=slot: e.activation(
            ln["sq"][:, slot*1024:(slot+1)*1024], x[:, tt*1024:(tt+1)*1024],
            AF.Square, bias=ln["nm"][:, tt:tt+1]), inc="AC")
        p.wait("dv", "AC", v_sq)
        v_red[tt] = p.op("dv", lambda e, tt=tt, slot=slot: e.reduce_sum(
            ln["vsums"][:, tt:tt+1], ln["sq"][:, slot*1024:(slot+1)*1024],
            axis=AXX), inc="DV")
    p.wait("ac", "DV", v_red[TT - 1])
    v_rstd = p.op("ac", lambda e: e.activation(
        ln["rstd"][:, 0:TT], ln["vsums"][:, 0:TT], AF.Sqrt,
        scale=1.0 / D, bias=ln["eps"][:, 0:1]), inc="AC")
    p.wait("dv", "AC", v_rstd)
    v_rcp = p.op("dv", lambda e: e.reciprocal(
        ln["rstd2"][:, 0:TT], ln["rstd"][:, 0:TT]), inc="DV")
    p.wait("dv", "DV", v_rcp)
    v_cp = {}
    v_tr_tile = {}
    for tt in range(TT):
        slot = tt % 2
        if tt >= 2:
            p.wait("dv", "TE", v_tr_tile[tt - 2])
        v_xln = p.op("dv", lambda e, tt=tt, slot=slot: e.tensor_scalar(
            ln["xln"][:, slot*1024:(slot+1)*1024], x[:, tt*1024:(tt+1)*1024],
            ln["nm"][:, tt:tt+1], ln["rstd2"][:, tt:tt+1], ALU.add, ALU.mult),
            inc="DV")
        p.wait("te", "DV", v_xln)
        for kc in range(KC):
            tp = tpA if kc % 2 == 0 else tpB
            prev = (tt, kc - 2) if kc >= 2 else ((tt - 1, KC + kc - 2) if tt > 0 else None)
            if prev is not None:
                p.wait("te", "DV", v_cp[prev])
            v_t = p.op("te", lambda e, tt=tt, kc=kc, slot=slot, tp=tp: e.transpose(
                tp[:], ln["xln"][:, slot*1024 + kc*128: slot*1024 + (kc+1)*128],
                ident[:]), inc="TE")
            p.wait("dv", "TE", v_t)
            v_cp[(tt, kc)] = p.op("dv", lambda e, tt=tt, kc=kc, tp=tp: e.tensor_copy(
                xlnT[:, kc*NTOK + tt*128: kc*NTOK + (tt+1)*128], tp[:]), inc="DV")
        v_tr_tile[tt] = p.cnt["TE"]
    return p.cnt["DV"]


def emit_router(p, b, v_xlnT_dv, v_w_gpd):
    nc = p.nc
    xlnT, rw, sgw = b["xlnT"], b["rw"], b["sgw"]
    with ExitStack() as ps:
        import itertools as _it
        _rid = getattr(emit_router, "_n", 0); emit_router._n = _rid + 1
        lg = ps.enter_context(nc.psum_tensor(f"lg{_rid}", [128, TT*32], F32))
        sg = ps.enter_context(nc.psum_tensor(f"sg{_rid}", [128, TT], F32))
        p.fence_te()
        p.wait("te", "DV", v_xlnT_dv)
        p.wait("te", "GPD", v_w_gpd)
        for tt in range(TT):
            for kc in range(KC):
                p.op("te", lambda e, tt=tt, kc=kc: e.matmul(
                    lg[:, tt*32:(tt+1)*32],
                    xlnT[:, kc*NTOK + tt*128: kc*NTOK + (tt+1)*128],
                    rw[:, kc*32:(kc+1)*32],
                    start=(tt == 0 and kc == 0), stop=(tt == TT-1 and kc == KC-1)))
        for tt in range(TT):
            for kc in range(KC):
                last = (tt == TT-1 and kc == KC-1)
                p.op("te", lambda e, tt=tt, kc=kc: e.matmul(
                    sg[:, tt:tt+1],
                    xlnT[:, kc*NTOK + tt*128: kc*NTOK + (tt+1)*128],
                    sgw[:, kc:kc+1],
                    start=(tt == 0 and kc == 0), stop=last),
                    inc="TE" if last else None)
        v_te = p.cnt["TE"]
        p.wait("dv", "TE", v_te)
        for tt in range(TT):
            p.op("dv", lambda e, tt=tt: e.reduce_max(
                b["mx"][:, tt:tt+1], lg[:, tt*32:(tt+1)*32], axis=AXX),
                inc="DV" if tt == TT - 1 else None)
        p.wait("dv", "DV", p.cnt["DV"])
        v_nmx = p.op("dv", lambda e: e.tensor_scalar(
            b["nmx"][:, 0:TT], b["mx"][:, 0:TT], -1.0, None, ALU.mult), inc="DV")
        p.wait("ac", "DV", v_nmx)
        for tt in range(TT):
            p.op("ac", lambda e, tt=tt: e.activation(
                b["expb"][:, tt*32:(tt+1)*32], lg[:, tt*32:(tt+1)*32], AF.Exp,
                bias=b["nmx"][:, tt:tt+1]))
        v_sig = p.op("ac", lambda e: e.activation(
            b["sig"][:, 0:TT], sg[:, 0:TT], AF.Sigmoid), inc="AC")
        p.wait("dv", "AC", v_sig)
        v_g2 = p.op("dv", lambda e: e.tensor_scalar(
            b["gate2"][:, 0:TT], b["sig"][:, 0:TT], 0.5, None, ALU.mult), inc="DV")
        expb, cur, wbuf, mb, comb = b["expb"], b["cur"], b["wbuf"], b["mb"], b["combine"]
        src = expb
        for it in range(4):
            for tt in range(TT):
                p.op("dv", lambda e, tt=tt, it=it, src=src: e.reduce_max(
                    mb[:, it*TT + tt: it*TT + tt + 1], src[:, tt*32:(tt+1)*32],
                    axis=AXX), inc="DV" if tt == TT - 1 else None)
            p.wait("dv", "DV", p.cnt["DV"])
            if it < 3:
                for tt in range(TT):
                    p.op("dv", lambda e, tt=tt, it=it, src=src: e.scalar_tensor_tensor(
                        cur[:, tt*32:(tt+1)*32], src[:, tt*32:(tt+1)*32],
                        mb[:, it*TT + tt: it*TT + tt + 1], src[:, tt*32:(tt+1)*32],
                        ALU.is_lt, ALU.mult), inc="DV" if tt == TT - 1 else None)
                p.wait("dv", "DV", p.cnt["DV"])
                src = cur
        for tt in range(TT):
            p.op("dv", lambda e, tt=tt: e.scalar_tensor_tensor(
                wbuf[:, tt*32:(tt+1)*32], expb[:, tt*32:(tt+1)*32],
                mb[:, 3*TT + tt: 3*TT + tt + 1], expb[:, tt*32:(tt+1)*32],
                ALU.is_ge, ALU.mult), inc="DV" if tt == TT - 1 else None)
        p.wait("dv", "DV", p.cnt["DV"])
        for tt in range(TT):
            p.op("dv", lambda e, tt=tt: e.reduce_sum(
                b["s4"][:, tt:tt+1], wbuf[:, tt*32:(tt+1)*32], axis=AXX),
                inc="DV" if tt == TT - 1 else None)
        p.wait("dv", "DV", p.cnt["DV"])
        v = p.op("dv", lambda e: e.tensor_scalar(
            b["s8"][:, 0:TT], b["s4"][:, 0:TT], 2.0, None, ALU.mult), inc="DV")
        p.wait("dv", "DV", v)
        v = p.op("dv", lambda e: e.reciprocal(b["r2"][:, 0:TT], b["s8"][:, 0:TT]),
                 inc="DV")
        p.wait("dv", "DV", v)
        for tt in range(TT):
            p.op("dv", lambda e, tt=tt: e.tensor_scalar(
                comb[:, tt*32:(tt+1)*32], wbuf[:, tt*32:(tt+1)*32],
                b["r2"][:, tt:tt+1], None, ALU.mult),
                inc="DV" if tt == TT - 1 else None)
    return {"combine_dv": p.cnt["DV"], "gate2_dv": v_g2,
            "lg_reader_ac": v_sig, "lg_reader_dv": v_nmx}


def emit_gu_act(p, st, lw_g, lw_u, xlnT, gps, ups, act_dst, silu_sb, tok_off):
    if st.get("g_free_ac"):
        p.wait("te", "AC", st["g_free_ac"])
    for kc in range(KC):
        p.op("te", lambda e, kc=kc: e.matmul(
            gps[:], lw_g(kc), xlnT[:, kc*NTOK + tok_off: kc*NTOK + tok_off + 512],
            start=(kc == 0), stop=(kc == KC - 1)),
            inc="TE" if kc == KC - 1 else None)
    v_g = p.cnt["TE"]
    if st.get("u_free_dv"):
        p.wait("te", "DV", st["u_free_dv"])
    for kc in range(KC):
        p.op("te", lambda e, kc=kc: e.matmul(
            ups[:], lw_u(kc), xlnT[:, kc*NTOK + tok_off: kc*NTOK + tok_off + 512],
            start=(kc == 0), stop=(kc == KC - 1)),
            inc="TE" if kc == KC - 1 else None)
    v_u = p.cnt["TE"]
    slot = st["silu_parity"]
    st["silu_parity"] ^= 1
    p.wait("ac", "TE", v_g)
    if st.get(("silu_free", slot)):
        p.wait("ac", "DV", st[("silu_free", slot)])
    v_silu = p.op("ac", lambda e, slot=slot: e.activation(
        silu_sb[:, slot*512:(slot+1)*512], gps[:], AF.Silu), inc="AC")
    st["g_free_ac"] = v_silu
    p.wait("dv", "AC", v_silu)
    p.wait("dv", "TE", v_u)
    v_act = p.op("dv", lambda e, slot=slot: e.tensor_tensor(
        act_dst, silu_sb[:, slot*512:(slot+1)*512], ups[:], ALU.mult), inc="DV")
    st["u_free_dv"] = v_act
    st[("silu_free", slot)] = v_act
    return v_act


def emit_ffn(p, b, drams, x, call):
    nc = p.nc
    xlnT = b["xlnT"]
    if call == 0:
        p.fence_dma("gp")
        p.op("gp", lambda e: e.dma_start(
            out=b["rw"].ap().rearrange("p (c n) -> p c n", c=KC),
            in_=r128(drams["router_w"].ap())), inc="GPD", k=16)
        p.op("gp", lambda e: e.dma_start(
            out=b["sgw"].ap(), in_=drams["sh_gate_w"][:, :]), inc="GPD", k=16)
        b["v_wsmall"] = p.cnt["GPD"]
    def load_sh(hf):
        for nm, buf in (("sh_gate", b["shg"]), ("sh_up", b["shu"])):
            p.op("gp", lambda e, nm=nm, buf=buf, hf=hf: e.dma_start(
                out=buf.ap().rearrange("p (c n) -> p c n", c=KC),
                in_=r128(drams[nm][:, hf*1024:(hf+1)*1024])), inc="GPD", k=16)
        vA = p.cnt["GPD"]
        p.op("gp", lambda e, hf=hf: e.dma_start(
            out=b["shd"].ap().rearrange("p (c n) -> p c n", c=KC),
            in_=r128(drams["sh_down"][hf*1024:(hf+1)*1024, :])), inc="GPD", k=16)
        return (vA, p.cnt["GPD"])

    def load_expert(ei, sl):
        for nm, buf, cs, w in (("w_gate", b["wg"], KC, EI),
                               ("w_up", b["wu"], KC, EI),
                               ("w_down", b["wd"], EIC, 1024)):
            p.op("gp", lambda e, nm=nm, buf=buf, ei=ei, sl=sl, cs=cs, w=w: e.dma_start(
                out=buf[:, sl*cs*w:(sl+1)*cs*w].rearrange("p (c n) -> p c n", c=cs),
                in_=r128(drams[nm][ei])), inc="GPD", k=16)
        return p.cnt["GPD"]

    with ExitStack() as ps0:
        tpA = ps0.enter_context(nc.psum_tensor(f"tpA{call}", [128, 128], BF16))
        tpB = ps0.enter_context(nc.psum_tensor(f"tpB{call}", [128, 128], BF16))
        p.fence_te()
        v_xlnT = emit_ln(p, x, b["ln"], xlnT, b["ident"], tpA, tpB)
    rinfo = emit_router(p, b, v_xlnT, b["v_wsmall"])
    x_writable_dv = max(rinfo["combine_dv"], rinfo["gate2_dv"])

    with ExitStack() as ps:
        gps = ps.enter_context(nc.psum_tensor(f"gps{call}", [128, 512], F32))
        ups = ps.enter_context(nc.psum_tensor(f"ups{call}", [128, 512], F32))
        routed = [ps.enter_context(nc.psum_tensor(f"rt{call}_{i}", [128, 1024], F32))
                  for i in range(2)]
        st = {"silu_parity": 0}
        pair_free_dv = {0: None, 1: None}
        ts_ctr = 0
        first_mm_fenced = [False]

        def down_group(lhs_fn, rhs_fn, tt, scalar_ap):
            nonlocal ts_ctr
            pr = ts_ctr % 2
            ts_ctr += 1
            if pair_free_dv[pr] is not None:
                p.wait("te", "DV", pair_free_dv[pr])
            nic = lhs_fn(None)
            for ic in range(nic):
                for nh2 in range(2):
                    last = (ic == nic - 1 and nh2 == 1)
                    p.op("te", lambda e, ic=ic, nh2=nh2, pr=pr: e.matmul(
                        routed[pr][:, nh2*512:(nh2+1)*512],
                        lhs_fn(ic), rhs_fn(ic, nh2),
                        start=(ic == 0), stop=(ic == nic - 1)),
                        inc="TE" if last else None)
            v_te = p.cnt["TE"]
            p.wait("dv", "TE", v_te)
            if not first_mm_fenced[0]:
                p.wait("dv", "DV", x_writable_dv)
                first_mm_fenced[0] = True
            pair_free_dv[pr] = p.op("dv", lambda e, tt=tt, pr=pr, s=scalar_ap: e.scalar_tensor_tensor(
                x[:, tt*1024:(tt+1)*1024], routed[pr][:], s,
                x[:, tt*1024:(tt+1)*1024], ALU.mult, ALU.add), inc="DV")

        with ExitStack() as shctx:
            alloc_ffn_shared(nc, shctx, b, f"f{call}")
            shg, shu, shd, actsh = b["shg"], b["shu"], b["shd"], b["actsh"]
            sh_gpd = [load_sh(0)]
            for hf in range(2):
                p.fence_te()
                p.wait("te", "GPD", sh_gpd[hf][0])
                for icc in range(KC):
                    for tb in range(TB):
                        emit_gu_act(
                            p, st,
                            (lambda kc, icc=icc: shg[:, kc*1024 + icc*128: kc*1024 + (icc+1)*128]),
                            (lambda kc, icc=icc: shu[:, kc*1024 + icc*128: kc*1024 + (icc+1)*128]),
                            xlnT, gps, ups,
                            actsh[:, icc*NTOK + tb*512: icc*NTOK + (tb+1)*512],
                            b["silu_sb"], tb*512)
                v_actsh_dv = p.cnt["DV"]
                p.wait("te", "DV", v_actsh_dv)
                p.wait("te", "GPD", sh_gpd[hf][1])
                for tt in range(TT):
                    down_group(
                        (lambda ic, tt=tt: KC if ic is None else
                         actsh[:, ic*NTOK + tt*128: ic*NTOK + (tt+1)*128]),
                        (lambda ic, nh2: shd[:, ic*1024 + nh2*512: ic*1024 + (nh2+1)*512]),
                        tt, b["gate2"][:, tt:tt+1])
                if hf == 0:
                    p.wait("gp", "TE", p.cnt["TE"])
                    sh_gpd.append(load_sh(1))
            v_te_sh, v_dv_sh = p.cnt["TE"], p.cnt["DV"]

        alloc_ffn_experts(nc, ps, b, f"f{call}")
        wg, wu, wd, act_sb = b["wg"], b["wu"], b["wd"], b["act_sb"]
        p.wait("gp", "TE", v_te_sh)
        p.wait("gp", "DV", v_dv_sh)
        for ei in range(2):
            b["ew_gpd"][ei] = load_expert(ei, ei % 2)
        act_slot_reader_te = {0: None, 1: None}
        for ei in range(NE):
            sl = ei % 2
            p.wait("te", "GPD", b["ew_gpd"][ei])
            for tb in range(TB):
                asl = tb % 2
                if act_slot_reader_te[asl] is not None:
                    p.wait("dv", "TE", act_slot_reader_te[asl])
                for ic in range(EIC):
                    emit_gu_act(
                        p, st,
                        (lambda kc, ic=ic, sl=sl: wg[:, sl*KC*EI + kc*EI + ic*128: sl*KC*EI + kc*EI + (ic+1)*128]),
                        (lambda kc, ic=ic, sl=sl: wu[:, sl*KC*EI + kc*EI + ic*128: sl*KC*EI + kc*EI + (ic+1)*128]),
                        xlnT, gps, ups,
                        act_sb[:, asl*EIC*512 + ic*512: asl*EIC*512 + (ic+1)*512],
                        b["silu_sb"], tb*512)
                v_act_dv = p.cnt["DV"]
                p.wait("te", "DV", v_act_dv)
                for ts4 in range(4):
                    tt = tb*4 + ts4
                    down_group(
                        (lambda ic, ts4=ts4, asl=asl: EIC if ic is None else
                         act_sb[:, asl*EIC*512 + ic*512 + ts4*128: asl*EIC*512 + ic*512 + (ts4+1)*128]),
                        (lambda ic, nh2, sl=sl: wd[:, sl*EIC*1024 + ic*1024 + nh2*512: sl*EIC*1024 + ic*1024 + (nh2+1)*512]),
                        tt, b["combine"][:, tt*32 + ei: tt*32 + ei + 1])
                act_slot_reader_te[asl] = p.cnt["TE"]
            if ei + 2 < NE:
                p.wait("gp", "TE", p.cnt["TE"])
                b["ew_gpd"][ei + 2] = load_expert(ei + 2, (ei + 2) % 2)
    return {"x_dv": p.cnt["DV"]}


def alloc_ffn_bufs(nc, ctx, b, tag=""):
    sb = lambda n, shape, dt=F32: ctx.enter_context(nc.sbuf_tensor(n + tag, shape, dt))
    b["xlnT"] = sb("xlnT", [128, KC*NTOK], BF16)
    ln = {}
    ln["sums"] = sb("ln_sums", [128, TT])
    ln["nm"] = sb("ln_nm", [128, TT])
    ln["vsums"] = sb("ln_vsums", [128, TT])
    ln["rstd"] = sb("ln_rstd", [128, TT])
    ln["rstd2"] = sb("ln_rstd2", [128, TT])
    ln["sq"] = sb("ln_sq", [128, 2*1024])
    ln["xln"] = sb("ln_xln", [128, 2*1024], BF16)
    ln["eps"] = b["eps"]
    b["ln"] = ln
    b["rw"] = sb("rw", [128, KC*32], BF16)
    b["sgw"] = sb("sgw", [128, KC], BF16)
    for n in ("mx", "nmx", "sig", "gate2", "s4", "s8", "r2"):
        b[n] = sb(n, [128, TT])
    for n in ("expb", "cur", "wbuf", "combine"):
        b[n] = sb(n, [128, TT*32])
    b["mb"] = sb("mb", [128, 4*TT])
    b["silu_sb"] = sb("silu_sb", [128, 2*512], BF16)
    return b


def alloc_ffn_shared(nc, ctx, b, tag):
    sb = lambda n, shape, dt=BF16: ctx.enter_context(nc.sbuf_tensor(n + tag, shape, dt))
    b["shg"] = sb("shg", [128, KC*1024])
    b["shu"] = sb("shu", [128, KC*1024])
    b["shd"] = sb("shd", [128, KC*1024])
    b["actsh"] = sb("actsh", [128, KC*NTOK])


def alloc_ffn_experts(nc, ctx, b, tag):
    sb = lambda n, shape, dt=BF16: ctx.enter_context(nc.sbuf_tensor(n + tag, shape, dt))
    b["wg"] = sb("wg", [128, 2*KC*EI])
    b["wu"] = sb("wu", [128, 2*KC*EI])
    b["wd"] = sb("wd", [128, 2*EIC*1024])
    b["act_sb"] = sb("act_sb", [128, 2*EIC*512])
    b["ew_gpd"] = [None]*NE


def emit_attention(p, nc, dr, ident, v_agin):
    GROUPS = [[0, 1], [2, 3], [4, 5], [6, 7]]
    with ExitStack() as actx:
        k_aug = actx.enter_context(nc.sbuf_tensor("k_aug", [128, NH*T], BF16))
        qT = actx.enter_context(nc.sbuf_tensor("qT", [128, 4*T], BF16))
        v_sb = actx.enter_context(nc.sbuf_tensor("v_sb", [128, 32*520], BF16))
        p.wait("gp", "SPD", v_agin)
        p.op("gp", lambda e: e.collective_compute(
            "AllGather", ALU.bypass, replica_groups=GROUPS,
            ins=[dr["ag_in"].ap().opt()], outs=[dr["ag_out"].ap().opt()]),
            inc="AGS", k=1)
        p.wait("dv", "TE", p.cnt["TE"])
        p.wait("dv", "AC", p.cnt["AC"])
        v_ones = p.op("dv", lambda e: e.memset(v_sb[:], 1.0), inc="DV")
        with ExitStack() as qctx:
            xch = qctx.enter_context(nc.sbuf_tensor("xch", [128, 2*KC*512], BF16))
            wqkv = qctx.enter_context(nc.sbuf_tensor("wqkv", [128, KC*1536], BF16))
            p.fence_dma("gp")
            p.op("gp", lambda e: e.dma_start(
                out=wqkv.ap().rearrange("p (c n) -> p c n", c=KC),
                in_=r128(dr["wqkvT"].ap())), inc="GPD", k=16)
            v_wqkv = p.cnt["GPD"]
            with ExitStack() as pctx:
                qkvps = [pctx.enter_context(nc.psum_tensor(f"qkvps{i}", [128, 512], F32))
                         for i in range(2)]
                p.fence_dma("sp")
                p.wait("sp", "AGS", 1)
                p.fence_te()
                p.wait("te", "GPD", v_wqkv)
                agv = dr["ag_out"].ap().rearrange("(r c p) t -> p r c t", p=128, c=KC)
                ps_free = {0: None, 1: None}
                chunk_reader_te = {0: None, 1: None}
                pp = [0]

                def grp(slx, lhs_fn, rhs_fn, evac_fn):
                    pi = pp[0] % 2
                    pp[0] += 1
                    ps = qkvps[pi]
                    if ps_free[pi] is not None:
                        p.wait("te", "DV", ps_free[pi])
                    for kc in range(KC):
                        p.op("te", lambda e, kc=kc, ps=ps: e.matmul(
                            ps[:], lhs_fn(kc), rhs_fn(kc),
                            start=(kc == 0), stop=(kc == KC - 1)),
                            inc="TE" if kc == KC - 1 else None)
                    p.wait("dv", "TE", p.cnt["TE"])
                    ps_free[pi] = evac_fn(ps)

                def xch_rhs(slx):
                    return lambda kc, slx=slx: xch[:, slx*KC*512 + kc*512: slx*KC*512 + (kc+1)*512]

                for tbg in range(QB):
                    slx = tbg % 2
                    if chunk_reader_te[slx] is not None:
                        p.wait("sp", "TE", chunk_reader_te[slx])
                    p.op("sp", lambda e, tbg=tbg, slx=slx: e.dma_start(
                        out=xch[:, slx*KC*512:(slx+1)*KC*512].rearrange(
                            "p (c n) -> p c n", c=KC),
                        in_=agv[:, tbg // TB, :, (tbg % TB)*512:(tbg % TB + 1)*512]),
                        inc="SPD", k=16)
                    p.wait("te", "SPD", p.cnt["SPD"])
                    for p4 in range(4):
                        grp(slx,
                            lambda kc, p4=p4: wqkv[:, kc*1536 + p4*128: kc*1536 + (p4+1)*128],
                            xch_rhs(slx),
                            lambda ps, p4=p4, tbg=tbg: p.op(
                                "dv", lambda e, ps=ps, p4=p4, tbg=tbg: e.tensor_scalar(
                                    qT[:, p4*T + tbg*512: p4*T + (tbg+1)*512],
                                    ps[:], 0.125, None, ALU.mult), inc="DV"))
                    for p4 in range(4):
                        def kev(ps, p4=p4, tbg=tbg):
                            p.op("dv", lambda e, ps=ps, p4=p4, tbg=tbg: e.tensor_copy(
                                k_aug[0:64, (2*p4)*T + tbg*512: (2*p4)*T + (tbg+1)*512],
                                ps[0:64, :]))
                            return p.op("dv", lambda e, ps=ps, p4=p4, tbg=tbg: e.tensor_copy(
                                k_aug[64:128, (2*p4+1)*T + tbg*512: (2*p4+1)*T + (tbg+1)*512],
                                ps[64:128, :]), inc="DV")
                        grp(slx,
                            lambda kc, p4=p4: wqkv[:, kc*1536 + 512 + p4*128: kc*1536 + 512 + (p4+1)*128],
                            xch_rhs(slx),
                            kev)
                    for ts4 in range(4):
                        kt = tbg*4 + ts4
                        def vev(ps, kt=kt):
                            p.wait("dv", "DV", v_ones)
                            return p.op("dv", lambda e, ps=ps, kt=kt: e.tensor_copy(
                                v_sb[:, kt*520: kt*520 + 520].rearrange(
                                    "p (h x) -> p h x", h=NH)[:, :, 0:64],
                                ps[:].rearrange("p (h x) -> p h x", h=NH)), inc="DV")
                        grp(slx,
                            lambda kc, ts4=ts4, slx=slx: xch[:, slx*KC*512 + kc*512 + ts4*128: slx*KC*512 + kc*512 + (ts4+1)*128],
                            lambda kc: wqkv[:, kc*1536 + 1024: kc*1536 + 1536],
                            vev)
                    chunk_reader_te[slx] = p.cnt["TE"]
        with ExitStack() as psictx:
            cs1 = psictx.enter_context(nc.sbuf_tensor("cs1sb", [128, T], BF16))
            cs2 = psictx.enter_context(nc.sbuf_tensor("cs2sb", [128, T], BF16))
            tpsi = psictx.enter_context(nc.sbuf_tensor("tpsi", [128, T], BF16))
            a4 = psictx.enter_context(nc.sbuf_tensor("a4sb", [128, NH], F32))
            b4 = psictx.enter_context(nc.sbuf_tensor("b4sb", [128, NH], F32))
            a2 = psictx.enter_context(nc.sbuf_tensor("a2sb", [128, NH], F32))
            sgn = psictx.enter_context(nc.sbuf_tensor("sgnsb", [128, 1], F32))
            p.fence_dma("sp")
            for nm, buf in (("cs1", cs1), ("cs2", cs2), ("a4", a4), ("b4", b4),
                            ("sgnv", sgn)):
                p.op("sp", lambda e, nm=nm, buf=buf: e.dma_start(
                    out=buf[:], in_=dr[nm][:, :]), inc="SPD", k=16)
            v_csld = p.cnt["SPD"]
            p.wait("dv", "SPD", v_csld)
            v_a2 = p.op("dv", lambda e: e.tensor_scalar(
                a2[:], b4[:], sgn[:, 0:1], None, ALU.mult), inc="DV")
            p.wait("dv", "DV", v_a2)
            for h in range(NH):
                Bb = (1 - (h % 2)) * 64
                v_t = p.op("dv", lambda e, h=h: e.tensor_scalar(
                    tpsi[:], cs1[:], a4[:, h:h+1], None, ALU.mult), inc="DV")
                p.wait("dv", "DV", v_t)
                p.op("dv", lambda e, h=h, Bb=Bb: e.scalar_tensor_tensor(
                    k_aug[Bb:Bb+64, h*T:(h+1)*T], cs2[Bb:Bb+64, :],
                    a2[Bb:Bb+64, h:h+1], tpsi[Bb:Bb+64, :], ALU.mult, ALU.add),
                    inc="DV")
            v_psi = p.cnt["DV"]
        with ExitStack() as fctx:
            phi2 = fctx.enter_context(nc.sbuf_tensor("phi2sb", [128, T], BF16))
            masks = fctx.enter_context(nc.sbuf_tensor("maskssb", [128, 4*512], BF16))
            wout = fctx.enter_context(nc.sbuf_tensor("woutsb", [128, 4*1024], BF16))
            q_aug = fctx.enter_context(nc.sbuf_tensor("q_aug", [128, 2*512], BF16))
            P_sb = fctx.enter_context(nc.sbuf_tensor("P_sb", [128, 2*512], BF16))
            O_sb = fctx.enter_context(nc.sbuf_tensor("O_sb", [128, 4*512], BF16))
            OT_sb = fctx.enter_context(nc.sbuf_tensor("OT_sb", [128, 4*128], BF16))
            r_sb = fctx.enter_context(nc.sbuf_tensor("r_sb", [128, 4], F32))
            yst = fctx.enter_context(nc.sbuf_tensor("yst", [128, 2*1024], F32))
            p.fence_dma("sp")
            for nm, buf in (("phi2", phi2), ("masks", masks)):
                p.op("sp", lambda e, nm=nm, buf=buf: e.dma_start(
                    out=buf[:], in_=dr[nm][:, :]), inc="SPD", k=16)
            v_fl = p.cnt["SPD"]
            p.fence_dma("gp")
            p.op("gp", lambda e: e.dma_start(
                out=wout.ap().rearrange("p (c n) -> p c n", c=4),
                in_=r128(dr["woutT"].ap())), inc="GPD", k=16)
            v_wout = p.cnt["GPD"]
            with ExitStack() as pctx:
                Sps = [pctx.enter_context(nc.psum_tensor(f"Sps{i}", [128, 512], F32))
                       for i in range(2)]
                Ops = [pctx.enter_context(nc.psum_tensor(f"Ops{i}", [128, 260], F32))
                       for i in range(2)]
                tpo = pctx.enter_context(nc.psum_tensor("tpo", [128, 128], BF16))
                yps = pctx.enter_context(nc.psum_tensor("yps", [128, 1024], F32))
                p.fence_te()
                p.wait("te", "DV", v_psi)
                p.wait("te", "SPD", v_fl)
                p.wait("te", "GPD", v_wout)
                q_free_te = {0: None, 1: None}
                S_free_ac = {0: None, 1: None}
                P_free_te = {0: None, 1: None}
                P_free_dv = {0: None, 1: None}
                O_free_dv = {0: None, 1: None}
                osb_reader_te = None
                yst_reader_spd = {0: None, 1: None}
                for qb in range(QB):
                    if osb_reader_te is not None:
                        p.wait("dv", "TE", osb_reader_te)
                    for h in range(NH):
                        qsl = h % 2
                        pi = h % 2
                        ob = h % 2
                        if q_free_te[qsl] is not None:
                            p.wait("dv", "TE", q_free_te[qsl])
                        p.op("dv", lambda e, h=h, qb=qb, qsl=qsl, pi=pi: e.tensor_copy(
                            q_aug[pi*64:(pi+1)*64, qsl*512:(qsl+1)*512],
                            qT[pi*64:(pi+1)*64, (h//2)*T + qb*512: (h//2)*T + (qb+1)*512]))
                        v_qa = p.op("dv", lambda e, qb=qb, qsl=qsl, pi=pi: e.tensor_copy(
                            q_aug[(1-pi)*64:(2-pi)*64, qsl*512:(qsl+1)*512],
                            phi2[(1-pi)*64:(2-pi)*64, qb*512:(qb+1)*512]), inc="DV")
                        p.wait("te", "DV", v_qa)
                        if O_free_dv[ob] is not None:
                            p.wait("te", "DV", O_free_dv[ob])
                        nkt = qb*4 + 4
                        for kt in range(nkt):
                            ssl = kt % 2
                            diag = kt >= qb*4
                            if S_free_ac[ssl] is not None:
                                p.wait("te", "AC", S_free_ac[ssl])
                            v_s = p.op("te", lambda e, h=h, kt=kt, ssl=ssl, qsl=qsl: e.matmul(
                                Sps[ssl][:], k_aug[:, h*T + kt*128: h*T + (kt+1)*128],
                                q_aug[:, qsl*512:(qsl+1)*512],
                                start=True, stop=True), inc="TE")
                            p.wait("ac", "TE", v_s)
                            if P_free_te[ssl] is not None:
                                p.wait("ac", "TE", P_free_te[ssl])
                            if P_free_dv[ssl] is not None:
                                p.wait("ac", "DV", P_free_dv[ssl])
                            v_p = p.op("ac", lambda e, ssl=ssl: e.activation(
                                P_sb[:, ssl*512:(ssl+1)*512], Sps[ssl][:], AF.Exp),
                                inc="AC")
                            S_free_ac[ssl] = v_p
                            if diag:
                                d = kt - qb*4
                                p.wait("dv", "AC", v_p)
                                v_m = p.op("dv", lambda e, ssl=ssl, d=d: e.tensor_tensor(
                                    P_sb[:, ssl*512:(ssl+1)*512],
                                    P_sb[:, ssl*512:(ssl+1)*512],
                                    masks[:, d*512:(d+1)*512], ALU.mult), inc="DV")
                                P_free_dv[ssl] = v_m
                                p.wait("te", "DV", v_m)
                            else:
                                P_free_dv[ssl] = None
                                p.wait("te", "AC", v_p)
                            for qs in range(4):
                                last = (kt == nkt - 1 and qs == 3)
                                p.op("te", lambda e, kt=kt, qs=qs, ssl=ssl, ob=ob, h=h: e.matmul(
                                    Ops[ob][:, qs*65:(qs+1)*65],
                                    P_sb[:, ssl*512 + qs*128: ssl*512 + (qs+1)*128],
                                    v_sb[:, kt*520 + h*65: kt*520 + (h+1)*65],
                                    start=(kt == 0 and qs == 0), stop=last),
                                    inc="TE" if last else None)
                            P_free_te[ssl] = p.cnt["TE"]
                        q_free_te[qsl] = p.cnt["TE"]
                        v_av = p.cnt["TE"]
                        p.wait("dv", "TE", v_av)
                        for qs in range(4):
                            p.op("dv", lambda e, qs=qs, ob=ob: e.reciprocal(
                                r_sb[:, qs:qs+1], Ops[ob][:, qs*65 + 64: qs*65 + 65]),
                                inc="DV" if qs == 3 else None)
                        p.wait("dv", "DV", p.cnt["DV"])
                        for qs in range(4):
                            p.op("dv", lambda e, qs=qs, h=h, ob=ob: e.tensor_scalar(
                                O_sb[:, qs*512 + h*64: qs*512 + (h+1)*64],
                                Ops[ob][:, qs*65: qs*65 + 64],
                                r_sb[:, qs:qs+1], None, ALU.mult),
                                inc="DV" if qs == 3 else None)
                        O_free_dv[ob] = p.cnt["DV"]
                    v_osb = p.cnt["DV"]
                    p.wait("te", "DV", v_osb)
                    for qs in range(4):
                        ysl = qs % 2
                        for c4 in range(4):
                            v_t = p.op("te", lambda e, qs=qs, c4=c4: e.transpose(
                                tpo[:], O_sb[:, qs*512 + c4*128: qs*512 + (c4+1)*128],
                                ident[:]), inc="TE")
                            p.wait("dv", "TE", v_t)
                            v_otc = p.op("dv", lambda e, c4=c4: e.tensor_copy(
                                OT_sb[:, c4*128:(c4+1)*128], tpo[:]), inc="DV")
                            p.wait("te", "DV", v_otc)
                        for c4 in range(4):
                            for nh2 in range(2):
                                last = (c4 == 3 and nh2 == 1)
                                p.op("te", lambda e, c4=c4, nh2=nh2: e.matmul(
                                    yps[:, nh2*512:(nh2+1)*512],
                                    OT_sb[:, c4*128:(c4+1)*128],
                                    wout[:, c4*1024 + nh2*512: c4*1024 + (nh2+1)*512],
                                    start=(c4 == 0), stop=(c4 == 3)),
                                    inc="TE" if last else None)
                        v_y = p.cnt["TE"]
                        p.wait("dv", "TE", v_y)
                        if yst_reader_spd[ysl] is not None:
                            p.wait("dv", "SPD", yst_reader_spd[ysl])
                        v_ye = p.op("dv", lambda e, ysl=ysl: e.tensor_copy(
                            yst[:, ysl*1024:(ysl+1)*1024], yps[:]), inc="DV")
                        p.wait("sp", "DV", v_ye)
                        p.op("sp", lambda e, qb=qb, qs=qs, ysl=ysl: e.dma_start(
                            out=dr["y_in"].ap().rearrange("(t p) n -> p t n", p=128)[:, qb*4+qs],
                            in_=yst[:, ysl*1024:(ysl+1)*1024]), inc="SPD", k=16)
                        yst_reader_spd[ysl] = p.cnt["SPD"]
                    osb_reader_te = p.cnt["TE"]
                v_yall = p.cnt["SPD"]
        p.wait("gp", "SPD", v_yall)
        p.op("gp", lambda e: e.collective_compute(
            "ReduceScatter", ALU.add, replica_groups=GROUPS,
            ins=[dr["y_in"].ap().opt()], outs=[dr["y_out"].ap().opt()]),
            inc="RSS", k=1)


def build_program():
    nc = bass.Bass("TRN2", target_bir_lowering=False)
    dr = {}
    dr["x_own"] = nc.dram_tensor("x_own", [NTOK, D], F32, kind="ExternalInput")
    dr["router_w"] = nc.dram_tensor("router_w", [D, 32], F32, kind="ExternalInput")
    dr["sh_gate_w"] = nc.dram_tensor("sh_gate_w", [128, KC], F32, kind="ExternalInput")
    dr["sh_gate"] = nc.dram_tensor("sh_gate", [D, ISH], F32, kind="ExternalInput")
    dr["sh_up"] = nc.dram_tensor("sh_up", [D, ISH], F32, kind="ExternalInput")
    dr["sh_down"] = nc.dram_tensor("sh_down", [ISH, D], F32, kind="ExternalInput")
    dr["w_gate"] = nc.dram_tensor("w_gate", [NE, D, EI], F32, kind="ExternalInput")
    dr["w_up"] = nc.dram_tensor("w_up", [NE, D, EI], F32, kind="ExternalInput")
    dr["w_down"] = nc.dram_tensor("w_down", [NE, EI, D], F32, kind="ExternalInput")
    dr["wqkvT"] = nc.dram_tensor("wqkvT", [D, 1536], F32, kind="ExternalInput")
    dr["woutT"] = nc.dram_tensor("woutT", [512, D], F32, kind="ExternalInput")
    dr["ident"] = nc.dram_tensor("ident", [128, 128], BF16, kind="ExternalInput")
    dr["cs1"] = nc.dram_tensor("cs1", [128, T], BF16, kind="ExternalInput")
    dr["cs2"] = nc.dram_tensor("cs2", [128, T], BF16, kind="ExternalInput")
    dr["phi2"] = nc.dram_tensor("phi2", [128, T], BF16, kind="ExternalInput")
    dr["a4"] = nc.dram_tensor("a4", [128, NH], F32, kind="ExternalInput")
    dr["b4"] = nc.dram_tensor("b4", [128, NH], F32, kind="ExternalInput")
    dr["sgnv"] = nc.dram_tensor("sgnv", [128, 1], F32, kind="ExternalInput")
    dr["masks"] = nc.dram_tensor("masks", [128, 4*512], BF16, kind="ExternalInput")
    dr["out"] = nc.dram_tensor("out", [NTOK, D], F32, kind="ExternalOutput")
    dr["xspill"] = nc.dram_tensor("xspill", [NTOK, D], F32)
    dr["ag_in"] = nc.dram_tensor("ag_in", [D, NTOK], BF16)
    dr["ag_out"] = nc.dram_tensor("ag_out", [2*D, NTOK], BF16)
    dr["y_in"] = nc.dram_tensor("y_in", [T, D], F32)
    dr["y_out"] = nc.dram_tensor("y_out", [NTOK, D], F32)

    with ExitStack() as gctx:
        p = Prog(nc)
        gctx.enter_context(p.ctx)
        ident = gctx.enter_context(nc.sbuf_tensor("identsb", [128, 128], BF16))
        epscol = gctx.enter_context(nc.sbuf_tensor("epscol", [128, 1], F32))
        blk = gctx.enter_context(nc.Block())
        p.op("sp", lambda e: e.dma_start(out=ident[:], in_=dr["ident"][:, :]),
             inc="SPD", k=16)
        v_ident = p.cnt["SPD"]
        v_eps = p.op("dv", lambda e: e.memset(epscol[:], EPS), inc="DV")

        with ExitStack() as f1ctx:
            b = {"ident": ident, "eps": epscol}
            x1 = f1ctx.enter_context(nc.sbuf_tensor("x1", [128, TT*1024], F32))
            alloc_ffn_bufs(nc, f1ctx, b, "c0")
            b["x"] = x1
            p.op("sp", lambda e: e.dma_start(
                out=x1.ap().rearrange("p (t n) -> p t n", t=TT),
                in_=dr["x_own"].ap().rearrange("(t p) n -> p t n", p=128)),
                inc="SPD", k=16)
            v_x = p.cnt["SPD"]
            p.wait("te", "SPD", v_ident)
            p.wait("dv", "SPD", v_x)
            p.wait("ac", "SPD", v_x)
            if True:
                info1 = emit_ffn(p, b, dr, x1, 0)
                with ExitStack() as ps0:
                    tpA = ps0.enter_context(nc.psum_tensor("tpA_l2", [128, 128], BF16))
                    tpB = ps0.enter_context(nc.psum_tensor("tpB_l2", [128, 128], BF16))
                    p.fence_te()
                    v_xln2 = emit_ln(p, x1, b["ln"], b["xlnT"], ident, tpA, tpB)
                p.wait("sp", "DV", v_xln2)
                p.op("sp", lambda e: e.dma_start(
                    out=dr["xspill"].ap().rearrange("(t p) n -> p t n", p=128),
                    in_=x1.ap().rearrange("p (t n) -> p t n", t=TT)),
                    inc="SPD", k=16)
                p.op("sp", lambda e, _t=b["xlnT"]: e.dma_start(
                    out=r128(dr["ag_in"].ap()),
                    in_=_t.ap().rearrange("p (c n) -> p c n", c=KC)),
                    inc="SPD", k=16)
                v_agin = p.cnt["SPD"]
        emit_attention(p, nc, dr, ident, v_agin)
        with ExitStack() as f2ctx:
            b = {"ident": ident, "eps": epscol}
            x2 = f2ctx.enter_context(nc.sbuf_tensor("x2", [128, TT*1024], F32))
            alloc_ffn_bufs(nc, f2ctx, b, "c1")
            b["x"] = x2
            p.fence_dma("sp")
            p.wait("sp", "RSS", 1)
            p.op("sp", lambda e: e.dma_start(
                out=x2.ap().rearrange("p (t n) -> p t n", t=TT),
                in_=dr["y_out"].ap().rearrange("(t p) n -> p t n", p=128)),
                inc="SPD", k=16)
            v_y = p.cnt["SPD"]
            with nc.sbuf_tensor("stg", [128, 2*1024], F32) as stg:
                for tt in range(TT):
                    sl = tt % 2
                    p.op("sp", lambda e, tt=tt, sl=sl, stg=stg: e.dma_start(
                        out=stg[:, sl*1024:(sl+1)*1024],
                        in_=dr["xspill"].ap().rearrange("(t p) n -> p t n", p=128)[:, tt]),
                        inc="SPD", k=16)
                    v_s = p.cnt["SPD"]
                    p.wait("dv", "SPD", max(v_s, v_y))
                    v_add = p.op("dv", lambda e, tt=tt, sl=sl, stg=stg: e.tensor_tensor(
                        x2[:, tt*1024:(tt+1)*1024], x2[:, tt*1024:(tt+1)*1024],
                        stg[:, sl*1024:(sl+1)*1024], ALU.add), inc="DV")
                    p.wait("sp", "DV", v_add)
            p.fence_dma("gp")
            p.op("gp", lambda e: e.dma_start(
                out=b["rw"].ap().rearrange("p (c n) -> p c n", c=KC),
                in_=r128(dr["router_w"].ap())), inc="GPD", k=16)
            p.op("gp", lambda e: e.dma_start(
                out=b["sgw"].ap(), in_=dr["sh_gate_w"][:, :]), inc="GPD", k=16)
            b["v_wsmall"] = p.cnt["GPD"]
            info2 = emit_ffn(p, b, dr, x2, 1)
            p.wait("sp", "DV", info2["x_dv"])
            p.op("sp", lambda e: e.dma_start(
                out=dr["out"].ap().rearrange("(t p) n -> p t n", p=128),
                in_=x2.ap().rearrange("p (t n) -> p t n", t=TT)),
                inc="SPD", k=16)
            p.wait("sp", "SPD", p.cnt["SPD"])
        p.flush(blk)
    return nc


def make_core_inputs(core, full):
    b, hg = core // 2, core % 2
    X = np.asarray(full["L0_attn_x"], np.float32).reshape(4*T, D)
    rows = slice(b*T + hg*NTOK, b*T + (hg+1)*NTOK)
    Wqkv = np.asarray(full["L0_attn_Wqkv"], np.float32)
    heads = range(hg*NH, (hg+1)*NH)
    qrows = np.concatenate([np.arange(h*64, (h+1)*64) for h in heads])
    sel = np.concatenate([qrows, D + qrows, 2*D + qrows])
    wqkvT = np.ascontiguousarray(Wqkv[sel].T)
    Wout = np.asarray(full["L0_attn_Wout"], np.float32)
    woutT = np.ascontiguousarray(Wout[:, hg*512:(hg+1)*512].T)
    alpha = np.asarray(full["L0_attn_alpha"], np.float32)[hg*NH:(hg+1)*NH]
    beta = np.asarray(full["L0_attn_beta"], np.float32)[hg*NH:(hg+1)*NH]
    a4 = np.ascontiguousarray(np.tile(alpha.T, (4, 1)))
    b4 = np.ascontiguousarray(np.tile(beta.T, (4, 1)))
    consts = make_consts()
    return {
        "x_own": np.ascontiguousarray(X[rows]),
        "router_w": np.asarray(full["L0_ffn_router_w"], np.float32),
        "sh_gate_w": np.ascontiguousarray(
            np.asarray(full["L0_ffn_sh_gate_w"], np.float32).reshape(KC, 128).T),
        "sh_gate": np.asarray(full["L0_ffn_sh_gate"], np.float32),
        "sh_up": np.asarray(full["L0_ffn_sh_up"], np.float32),
        "sh_down": np.asarray(full["L0_ffn_sh_down"], np.float32),
        "w_gate": np.asarray(full["L0_ffn_w_gate"], np.float32),
        "w_up": np.asarray(full["L0_ffn_w_up"], np.float32),
        "w_down": np.asarray(full["L0_ffn_w_down"], np.float32),
        "wqkvT": wqkvT, "woutT": woutT, "a4": a4, "b4": b4,
        **consts,
    }


_CONSTS = None

def make_consts():
    global _CONSTS
    if _CONSTS is not None:
        return _CONSTS
    R = 32
    om = (10000.0 ** (-2.0 * np.arange(R, dtype=np.float64) / 64.0))
    k = np.arange(T, dtype=np.float64)
    C = np.cos(om[:, None] * k[None, :])
    S = np.sin(om[:, None] * k[None, :])
    cs1 = np.concatenate([C, S, C, S], axis=0).astype(BF)
    cs2 = np.concatenate([S, C, S, C], axis=0).astype(BF)
    phi = np.concatenate([C, S], axis=0)
    phi2 = np.concatenate([phi, phi], axis=0).astype(BF)
    sgn = np.concatenate([-np.ones(32), np.ones(32), -np.ones(32),
                          np.ones(32)]).astype(np.float32)[:, None]
    i = np.arange(128)[:, None]
    j = np.arange(512)[None, :]
    masks = np.concatenate(
        [(j >= i + d*128).astype(np.float32) for d in range(4)], axis=1).astype(BF)
    _CONSTS = {
        "ident": np.eye(128).astype(BF),
        "cs1": np.ascontiguousarray(cs1), "cs2": np.ascontiguousarray(cs2),
        "phi2": np.ascontiguousarray(phi2), "sgnv": sgn, "masks": masks,
    }
    return _CONSTS


def kernel(**inputs):
    from concourse.bass_utils import run_bass_kernel_spmd
    nc = build_program()
    in_maps = [make_core_inputs(c, inputs) for c in range(8)]
    res = run_bass_kernel_spmd(nc, in_maps, core_ids=list(range(8)))
    out = np.empty((4 * T, D), dtype=np.float32)
    for c in range(8):
        b, hg = c // 2, c % 2
        out[b*T + hg*NTOK: b*T + (hg+1)*NTOK] = res.results[c]["out"]
    return out.reshape(4, T, D)


# version 02
# speedup vs baseline: 1.0293x; 1.0293x over previous
import numpy as np
import ml_dtypes
from contextlib import ExitStack

import concourse.bass as bass
import concourse.mybir as mybir

F32 = mybir.dt.float32
BF16 = mybir.dt.bfloat16
AF = mybir.ActivationFunctionType
ALU = mybir.AluOpType
AXX = mybir.AxisListType.X

D = 1024
KC = 8
NTOK = 2048
TT = 16
TB = 4
NE = 32
EI = 768
EIC = 6
ISH = 2048
T = 4096
NH = 8
QB = 8
EPS = 1e-5

ENGS = ("te", "dv", "ac", "sp", "gp")
BF = ml_dtypes.bfloat16


class Prog:
    def __init__(self, nc):
        self.nc = nc
        self.q = {e: [] for e in ENGS}
        self.sems = {}
        self.cnt = {}
        self.ctx = ExitStack()
        for s in ("TE", "DV", "AC", "GPD", "SPD", "AGS", "RSS"):
            self.sems[s] = self.ctx.enter_context(nc.semaphore(s))
            self.cnt[s] = 0

    def op(self, eng, fn, inc=None, k=1):
        if inc is None:
            self.q[eng].append(lambda e, fn=fn: fn(e))
            return None
        self.cnt[inc] += k
        s = self.sems[inc]
        self.q[eng].append(lambda e, fn=fn, s=s, k=k: fn(e).then_inc(s, k))
        return self.cnt[inc]

    def wait(self, eng, sem, val):
        if val is None or val <= 0:
            return
        self.q[eng].append(lambda e, s=self.sems[sem], v=val: e.wait_ge(s, v))

    def fence_te(self):
        self.wait("te", "DV", self.cnt["DV"])
        self.wait("te", "AC", self.cnt["AC"])

    def fence_dma(self, eng):
        for s in ("TE", "DV", "AC"):
            self.wait(eng, s, self.cnt[s])

    def flush(self, block):
        def runner(fns):
            def go(e):
                for f in fns:
                    f(e)
            return go
        block.tensor(runner(self.q["te"]))
        block.vector(runner(self.q["dv"]))
        block.scalar(runner(self.q["ac"]))
        block.sync(runner(self.q["sp"]))
        block.gpsimd(runner(self.q["gp"]))


def r128(dram_ap):
    return dram_ap.rearrange("(c p) n -> p c n", p=128)


def emit_ln(p, x, ln, xlnT, ident, tpA, tpB, ln_fence=True):
    if ln_fence:
        p.wait("ac", "DV", p.cnt["DV"])
    for tt in range(TT):
        p.op("dv", lambda e, tt=tt: e.reduce_sum(
            ln["sums"][:, tt:tt+1], x[:, tt*1024:(tt+1)*1024], axis=AXX),
            inc="DV" if tt == TT - 1 else None)
    p.wait("dv", "DV", p.cnt["DV"])
    v_nm = p.op("dv", lambda e: e.tensor_scalar(
        ln["nm"][:, 0:TT], ln["sums"][:, 0:TT], -1.0 / D, None, ALU.mult), inc="DV")
    p.wait("ac", "DV", v_nm)
    v_red = {}
    for tt in range(TT):
        slot = tt % 2
        if tt >= 2:
            p.wait("ac", "DV", v_red[tt - 2])
        v_sq = p.op("ac", lambda e, tt=tt, slot=slot: e.activation(
            ln["sq"][:, slot*1024:(slot+1)*1024], x[:, tt*1024:(tt+1)*1024],
            AF.Square, bias=ln["nm"][:, tt:tt+1]), inc="AC")
        p.wait("dv", "AC", v_sq)
        v_red[tt] = p.op("dv", lambda e, tt=tt, slot=slot: e.reduce_sum(
            ln["vsums"][:, tt:tt+1], ln["sq"][:, slot*1024:(slot+1)*1024],
            axis=AXX), inc="DV")
    p.wait("ac", "DV", v_red[TT - 1])
    v_rstd = p.op("ac", lambda e: e.activation(
        ln["rstd"][:, 0:TT], ln["vsums"][:, 0:TT], AF.Sqrt,
        scale=1.0 / D, bias=ln["eps"][:, 0:1]), inc="AC")
    p.wait("dv", "AC", v_rstd)
    v_rcp = p.op("dv", lambda e: e.reciprocal(
        ln["rstd2"][:, 0:TT], ln["rstd"][:, 0:TT]), inc="DV")
    p.wait("dv", "DV", v_rcp)
    v_cp = {}
    v_tr_tile = {}
    for tt in range(TT):
        slot = tt % 2
        if tt >= 2:
            p.wait("dv", "TE", v_tr_tile[tt - 2])
        v_xln = p.op("dv", lambda e, tt=tt, slot=slot: e.tensor_scalar(
            ln["xln"][:, slot*1024:(slot+1)*1024], x[:, tt*1024:(tt+1)*1024],
            ln["nm"][:, tt:tt+1], ln["rstd2"][:, tt:tt+1], ALU.add, ALU.mult),
            inc="DV")
        p.wait("te", "DV", v_xln)
        for kc in range(KC):
            tp = tpA if kc % 2 == 0 else tpB
            prev = (tt, kc - 2) if kc >= 2 else ((tt - 1, KC + kc - 2) if tt > 0 else None)
            if prev is not None:
                p.wait("te", "DV", v_cp[prev])
            v_t = p.op("te", lambda e, tt=tt, kc=kc, slot=slot, tp=tp: e.transpose(
                tp[:], ln["xln"][:, slot*1024 + kc*128: slot*1024 + (kc+1)*128],
                ident[:]), inc="TE")
            p.wait("dv", "TE", v_t)
            v_cp[(tt, kc)] = p.op("dv", lambda e, tt=tt, kc=kc, tp=tp: e.tensor_copy(
                xlnT[:, kc*NTOK + tt*128: kc*NTOK + (tt+1)*128], tp[:]), inc="DV")
        v_tr_tile[tt] = p.cnt["TE"]
    return p.cnt["DV"]


def emit_router(p, b, v_xlnT_dv, v_w_gpd):
    nc = p.nc
    xlnT, rw, sgw = b["xlnT"], b["rw"], b["sgw"]
    with ExitStack() as ps:
        import itertools as _it
        _rid = getattr(emit_router, "_n", 0); emit_router._n = _rid + 1
        lg = ps.enter_context(nc.psum_tensor(f"lg{_rid}", [128, TT*32], F32))
        sg = ps.enter_context(nc.psum_tensor(f"sg{_rid}", [128, TT], F32))
        p.fence_te()
        p.wait("te", "DV", v_xlnT_dv)
        p.wait("te", "GPD", v_w_gpd)
        for tt in range(TT):
            for kc in range(KC):
                p.op("te", lambda e, tt=tt, kc=kc: e.matmul(
                    lg[:, tt*32:(tt+1)*32],
                    xlnT[:, kc*NTOK + tt*128: kc*NTOK + (tt+1)*128],
                    rw[:, kc*32:(kc+1)*32],
                    start=(tt == 0 and kc == 0), stop=(tt == TT-1 and kc == KC-1)))
        for tt in range(TT):
            for kc in range(KC):
                last = (tt == TT-1 and kc == KC-1)
                p.op("te", lambda e, tt=tt, kc=kc: e.matmul(
                    sg[:, tt:tt+1],
                    xlnT[:, kc*NTOK + tt*128: kc*NTOK + (tt+1)*128],
                    sgw[:, kc:kc+1],
                    start=(tt == 0 and kc == 0), stop=last),
                    inc="TE" if last else None)
        v_te = p.cnt["TE"]
        p.wait("dv", "TE", v_te)
        for tt in range(TT):
            p.op("dv", lambda e, tt=tt: e.reduce_max(
                b["mx"][:, tt:tt+1], lg[:, tt*32:(tt+1)*32], axis=AXX),
                inc="DV" if tt == TT - 1 else None)
        p.wait("dv", "DV", p.cnt["DV"])
        v_nmx = p.op("dv", lambda e: e.tensor_scalar(
            b["nmx"][:, 0:TT], b["mx"][:, 0:TT], -1.0, None, ALU.mult), inc="DV")
        p.wait("ac", "DV", v_nmx)
        for tt in range(TT):
            p.op("ac", lambda e, tt=tt: e.activation(
                b["expb"][:, tt*32:(tt+1)*32], lg[:, tt*32:(tt+1)*32], AF.Exp,
                bias=b["nmx"][:, tt:tt+1]))
        v_sig = p.op("ac", lambda e: e.activation(
            b["sig"][:, 0:TT], sg[:, 0:TT], AF.Sigmoid), inc="AC")
        p.wait("dv", "AC", v_sig)
        v_g2 = p.op("dv", lambda e: e.tensor_scalar(
            b["gate2"][:, 0:TT], b["sig"][:, 0:TT], 0.5, None, ALU.mult), inc="DV")
        expb, cur, wbuf, mb, comb = b["expb"], b["cur"], b["wbuf"], b["mb"], b["combine"]
        src = expb
        for it in range(4):
            for tt in range(TT):
                p.op("dv", lambda e, tt=tt, it=it, src=src: e.reduce_max(
                    mb[:, it*TT + tt: it*TT + tt + 1], src[:, tt*32:(tt+1)*32],
                    axis=AXX), inc="DV" if tt == TT - 1 else None)
            p.wait("dv", "DV", p.cnt["DV"])
            if it < 3:
                for tt in range(TT):
                    p.op("dv", lambda e, tt=tt, it=it, src=src: e.scalar_tensor_tensor(
                        cur[:, tt*32:(tt+1)*32], src[:, tt*32:(tt+1)*32],
                        mb[:, it*TT + tt: it*TT + tt + 1], src[:, tt*32:(tt+1)*32],
                        ALU.is_lt, ALU.mult), inc="DV" if tt == TT - 1 else None)
                p.wait("dv", "DV", p.cnt["DV"])
                src = cur
        for tt in range(TT):
            p.op("dv", lambda e, tt=tt: e.scalar_tensor_tensor(
                wbuf[:, tt*32:(tt+1)*32], expb[:, tt*32:(tt+1)*32],
                mb[:, 3*TT + tt: 3*TT + tt + 1], expb[:, tt*32:(tt+1)*32],
                ALU.is_ge, ALU.mult), inc="DV" if tt == TT - 1 else None)
        p.wait("dv", "DV", p.cnt["DV"])
        for tt in range(TT):
            p.op("dv", lambda e, tt=tt: e.reduce_sum(
                b["s4"][:, tt:tt+1], wbuf[:, tt*32:(tt+1)*32], axis=AXX),
                inc="DV" if tt == TT - 1 else None)
        p.wait("dv", "DV", p.cnt["DV"])
        v = p.op("dv", lambda e: e.tensor_scalar(
            b["s8"][:, 0:TT], b["s4"][:, 0:TT], 2.0, None, ALU.mult), inc="DV")
        p.wait("dv", "DV", v)
        v = p.op("dv", lambda e: e.reciprocal(b["r2"][:, 0:TT], b["s8"][:, 0:TT]),
                 inc="DV")
        p.wait("dv", "DV", v)
        for tt in range(TT):
            p.op("dv", lambda e, tt=tt: e.tensor_scalar(
                comb[:, tt*32:(tt+1)*32], wbuf[:, tt*32:(tt+1)*32],
                b["r2"][:, tt:tt+1], None, ALU.mult),
                inc="DV" if tt == TT - 1 else None)
    return {"combine_dv": p.cnt["DV"], "gate2_dv": v_g2,
            "lg_reader_ac": v_sig, "lg_reader_dv": v_nmx}


def emit_gu_act(p, st, lw_g, lw_u, xlnT, gpsl, upsl, act_dst, silu_sb, tok_off):
    i = st.get("gu_rot", 0) & 1
    st["gu_rot"] = i ^ 1
    gps, ups = gpsl[i], upsl[i]
    busy = st.get(("gu_busy", i))
    if busy is not None:
        p.wait("te", "AC", busy[0])
        p.wait("te", "DV", busy[1])
    for kc in range(KC):
        p.op("te", lambda e, kc=kc, gps=gps: e.matmul(
            gps[:], lw_g(kc), xlnT[:, kc*NTOK + tok_off: kc*NTOK + tok_off + 512],
            start=(kc == 0), stop=(kc == KC - 1)),
            inc="TE" if kc == KC - 1 else None)
    v_g = p.cnt["TE"]
    for kc in range(KC):
        p.op("te", lambda e, kc=kc, ups=ups: e.matmul(
            ups[:], lw_u(kc), xlnT[:, kc*NTOK + tok_off: kc*NTOK + tok_off + 512],
            start=(kc == 0), stop=(kc == KC - 1)),
            inc="TE" if kc == KC - 1 else None)
    v_u = p.cnt["TE"]
    slot = st["silu_parity"]
    st["silu_parity"] ^= 1
    p.wait("ac", "TE", v_g)
    if st.get(("silu_free", slot)):
        p.wait("ac", "DV", st[("silu_free", slot)])
    v_silu = p.op("ac", lambda e, slot=slot, gps=gps: e.activation(
        silu_sb[:, slot*512:(slot+1)*512], gps[:], AF.Silu), inc="AC")
    p.wait("dv", "AC", v_silu)
    p.wait("dv", "TE", v_u)
    v_act = p.op("dv", lambda e, slot=slot, ups=ups: e.tensor_tensor(
        act_dst, silu_sb[:, slot*512:(slot+1)*512], ups[:], ALU.mult), inc="DV")
    st[("gu_busy", i)] = (v_silu, v_act)
    st[("silu_free", slot)] = v_act
    return v_act


def emit_ffn(p, b, drams, x, call):
    nc = p.nc
    xlnT = b["xlnT"]
    if call == 0:
        p.fence_dma("gp")
        p.op("gp", lambda e: e.dma_start(
            out=b["rw"].ap().rearrange("p (c n) -> p c n", c=KC),
            in_=r128(drams["router_w"].ap())), inc="GPD", k=16)
        p.op("gp", lambda e: e.dma_start(
            out=b["sgw"].ap(), in_=drams["sh_gate_w"][:, :]), inc="GPD", k=16)
        b["v_wsmall"] = p.cnt["GPD"]
    def load_sh(hf):
        for nm, buf in (("sh_gate", b["shg"]), ("sh_up", b["shu"])):
            p.op("gp", lambda e, nm=nm, buf=buf, hf=hf: e.dma_start(
                out=buf.ap().rearrange("p (c n) -> p c n", c=KC),
                in_=r128(drams[nm][:, hf*1024:(hf+1)*1024])), inc="GPD", k=16)
        vA = p.cnt["GPD"]
        p.op("gp", lambda e, hf=hf: e.dma_start(
            out=b["shd"].ap().rearrange("p (c n) -> p c n", c=KC),
            in_=r128(drams["sh_down"][hf*1024:(hf+1)*1024, :])), inc="GPD", k=16)
        return (vA, p.cnt["GPD"])

    def load_expert(ei, sl):
        for nm, buf, cs, w in (("w_gate", b["wg"], KC, EI),
                               ("w_up", b["wu"], KC, EI),
                               ("w_down", b["wd"], EIC, 1024)):
            p.op("gp", lambda e, nm=nm, buf=buf, ei=ei, sl=sl, cs=cs, w=w: e.dma_start(
                out=buf[:, sl*cs*w:(sl+1)*cs*w].rearrange("p (c n) -> p c n", c=cs),
                in_=r128(drams[nm][ei])), inc="GPD", k=16)
        return p.cnt["GPD"]

    with ExitStack() as ps0:
        tpA = ps0.enter_context(nc.psum_tensor(f"tpA{call}", [128, 128], BF16))
        tpB = ps0.enter_context(nc.psum_tensor(f"tpB{call}", [128, 128], BF16))
        p.fence_te()
        v_xlnT = emit_ln(p, x, b["ln"], xlnT, b["ident"], tpA, tpB)
    rinfo = emit_router(p, b, v_xlnT, b["v_wsmall"])
    x_writable_dv = max(rinfo["combine_dv"], rinfo["gate2_dv"])

    with ExitStack() as ps:
        gpsl = [ps.enter_context(nc.psum_tensor(f"gps{call}_{i}", [128, 512], F32))
                for i in range(2)]
        upsl = [ps.enter_context(nc.psum_tensor(f"ups{call}_{i}", [128, 512], F32))
                for i in range(2)]
        routed = [ps.enter_context(nc.psum_tensor(f"rt{call}_{i}", [128, 1024], F32))
                  for i in range(2)]
        st = {"silu_parity": 0}
        pair_free_dv = {0: None, 1: None}
        ts_ctr = 0
        first_mm_fenced = [False]

        def down_group(lhs_fn, rhs_fn, tt, scalar_ap):
            nonlocal ts_ctr
            pr = ts_ctr % 2
            ts_ctr += 1
            if pair_free_dv[pr] is not None:
                p.wait("te", "DV", pair_free_dv[pr])
            nic = lhs_fn(None)
            for ic in range(nic):
                for nh2 in range(2):
                    last = (ic == nic - 1 and nh2 == 1)
                    p.op("te", lambda e, ic=ic, nh2=nh2, pr=pr: e.matmul(
                        routed[pr][:, nh2*512:(nh2+1)*512],
                        lhs_fn(ic), rhs_fn(ic, nh2),
                        start=(ic == 0), stop=(ic == nic - 1)),
                        inc="TE" if last else None)
            v_te = p.cnt["TE"]
            p.wait("dv", "TE", v_te)
            if not first_mm_fenced[0]:
                p.wait("dv", "DV", x_writable_dv)
                first_mm_fenced[0] = True
            pair_free_dv[pr] = p.op("dv", lambda e, tt=tt, pr=pr, s=scalar_ap: e.scalar_tensor_tensor(
                x[:, tt*1024:(tt+1)*1024], routed[pr][:], s,
                x[:, tt*1024:(tt+1)*1024], ALU.mult, ALU.add), inc="DV")

        with ExitStack() as shctx:
            alloc_ffn_shared(nc, shctx, b, f"f{call}")
            shg, shu, shd, actsh = b["shg"], b["shu"], b["shd"], b["actsh"]
            sh_gpd = [load_sh(0)]
            for hf in range(2):
                p.fence_te()
                p.wait("te", "GPD", sh_gpd[hf][0])
                for icc in range(KC):
                    for tb in range(TB):
                        emit_gu_act(
                            p, st,
                            (lambda kc, icc=icc: shg[:, kc*1024 + icc*128: kc*1024 + (icc+1)*128]),
                            (lambda kc, icc=icc: shu[:, kc*1024 + icc*128: kc*1024 + (icc+1)*128]),
                            xlnT, gpsl, upsl,
                            actsh[:, icc*NTOK + tb*512: icc*NTOK + (tb+1)*512],
                            b["silu_sb"], tb*512)
                v_actsh_dv = p.cnt["DV"]
                p.wait("te", "DV", v_actsh_dv)
                p.wait("te", "GPD", sh_gpd[hf][1])
                for tt in range(TT):
                    down_group(
                        (lambda ic, tt=tt: KC if ic is None else
                         actsh[:, ic*NTOK + tt*128: ic*NTOK + (tt+1)*128]),
                        (lambda ic, nh2: shd[:, ic*1024 + nh2*512: ic*1024 + (nh2+1)*512]),
                        tt, b["gate2"][:, tt:tt+1])
                if hf == 0:
                    p.wait("gp", "TE", p.cnt["TE"])
                    sh_gpd.append(load_sh(1))
            v_te_sh, v_dv_sh = p.cnt["TE"], p.cnt["DV"]

        alloc_ffn_experts(nc, ps, b, f"f{call}")
        wg, wu, wd, act_sb = b["wg"], b["wu"], b["wd"], b["act_sb"]
        p.wait("gp", "TE", v_te_sh)
        p.wait("gp", "DV", v_dv_sh)
        for ei in range(2):
            b["ew_gpd"][ei] = load_expert(ei, ei % 2)
        act_slot_reader_te = {0: None, 1: None}
        for ei in range(NE):
            sl = ei % 2
            p.wait("te", "GPD", b["ew_gpd"][ei])
            for tb in range(TB):
                asl = tb % 2
                if act_slot_reader_te[asl] is not None:
                    p.wait("dv", "TE", act_slot_reader_te[asl])
                for ic in range(EIC):
                    emit_gu_act(
                        p, st,
                        (lambda kc, ic=ic, sl=sl: wg[:, sl*KC*EI + kc*EI + ic*128: sl*KC*EI + kc*EI + (ic+1)*128]),
                        (lambda kc, ic=ic, sl=sl: wu[:, sl*KC*EI + kc*EI + ic*128: sl*KC*EI + kc*EI + (ic+1)*128]),
                        xlnT, gpsl, upsl,
                        act_sb[:, asl*EIC*512 + ic*512: asl*EIC*512 + (ic+1)*512],
                        b["silu_sb"], tb*512)
                v_act_dv = p.cnt["DV"]
                p.wait("te", "DV", v_act_dv)
                for ts4 in range(4):
                    tt = tb*4 + ts4
                    down_group(
                        (lambda ic, ts4=ts4, asl=asl: EIC if ic is None else
                         act_sb[:, asl*EIC*512 + ic*512 + ts4*128: asl*EIC*512 + ic*512 + (ts4+1)*128]),
                        (lambda ic, nh2, sl=sl: wd[:, sl*EIC*1024 + ic*1024 + nh2*512: sl*EIC*1024 + ic*1024 + (nh2+1)*512]),
                        tt, b["combine"][:, tt*32 + ei: tt*32 + ei + 1])
                act_slot_reader_te[asl] = p.cnt["TE"]
            if ei + 2 < NE:
                p.wait("gp", "TE", p.cnt["TE"])
                b["ew_gpd"][ei + 2] = load_expert(ei + 2, (ei + 2) % 2)
    return {"x_dv": p.cnt["DV"]}


def alloc_ffn_bufs(nc, ctx, b, tag=""):
    sb = lambda n, shape, dt=F32: ctx.enter_context(nc.sbuf_tensor(n + tag, shape, dt))
    b["xlnT"] = sb("xlnT", [128, KC*NTOK], BF16)
    ln = {}
    ln["sums"] = sb("ln_sums", [128, TT])
    ln["nm"] = sb("ln_nm", [128, TT])
    ln["vsums"] = sb("ln_vsums", [128, TT])
    ln["rstd"] = sb("ln_rstd", [128, TT])
    ln["rstd2"] = sb("ln_rstd2", [128, TT])
    ln["sq"] = sb("ln_sq", [128, 2*1024])
    ln["xln"] = sb("ln_xln", [128, 2*1024], BF16)
    ln["eps"] = b["eps"]
    b["ln"] = ln
    b["rw"] = sb("rw", [128, KC*32], BF16)
    b["sgw"] = sb("sgw", [128, KC], BF16)
    for n in ("mx", "nmx", "sig", "gate2", "s4", "s8", "r2"):
        b[n] = sb(n, [128, TT])
    for n in ("expb", "cur", "wbuf", "combine"):
        b[n] = sb(n, [128, TT*32])
    b["mb"] = sb("mb", [128, 4*TT])
    b["silu_sb"] = sb("silu_sb", [128, 2*512], BF16)
    return b


def alloc_ffn_shared(nc, ctx, b, tag):
    sb = lambda n, shape, dt=BF16: ctx.enter_context(nc.sbuf_tensor(n + tag, shape, dt))
    b["shg"] = sb("shg", [128, KC*1024])
    b["shu"] = sb("shu", [128, KC*1024])
    b["shd"] = sb("shd", [128, KC*1024])
    b["actsh"] = sb("actsh", [128, KC*NTOK])


def alloc_ffn_experts(nc, ctx, b, tag):
    sb = lambda n, shape, dt=BF16: ctx.enter_context(nc.sbuf_tensor(n + tag, shape, dt))
    b["wg"] = sb("wg", [128, 2*KC*EI])
    b["wu"] = sb("wu", [128, 2*KC*EI])
    b["wd"] = sb("wd", [128, 2*EIC*1024])
    b["act_sb"] = sb("act_sb", [128, 2*EIC*512])
    b["ew_gpd"] = [None]*NE


def emit_attention(p, nc, dr, ident, v_agin):
    GROUPS = [[0, 1], [2, 3], [4, 5], [6, 7]]
    with ExitStack() as actx:
        k_aug = actx.enter_context(nc.sbuf_tensor("k_aug", [128, NH*T], BF16))
        qT = actx.enter_context(nc.sbuf_tensor("qT", [128, 4*T], BF16))
        v_sb = actx.enter_context(nc.sbuf_tensor("v_sb", [128, 32*520], BF16))
        p.wait("gp", "SPD", v_agin)
        p.op("gp", lambda e: e.collective_compute(
            "AllGather", ALU.bypass, replica_groups=GROUPS,
            ins=[dr["ag_in"].ap().opt()], outs=[dr["ag_out"].ap().opt()]),
            inc="AGS", k=1)
        p.wait("dv", "TE", p.cnt["TE"])
        p.wait("dv", "AC", p.cnt["AC"])
        v_ones = p.op("dv", lambda e: e.memset(v_sb[:], 1.0), inc="DV")
        with ExitStack() as qctx:
            xch = qctx.enter_context(nc.sbuf_tensor("xch", [128, 2*KC*512], BF16))
            wqkv = qctx.enter_context(nc.sbuf_tensor("wqkv", [128, KC*1536], BF16))
            p.fence_dma("gp")
            p.op("gp", lambda e: e.dma_start(
                out=wqkv.ap().rearrange("p (c n) -> p c n", c=KC),
                in_=r128(dr["wqkvT"].ap())), inc="GPD", k=16)
            v_wqkv = p.cnt["GPD"]
            with ExitStack() as pctx:
                qkvps = [pctx.enter_context(nc.psum_tensor(f"qkvps{i}", [128, 512], F32))
                         for i in range(2)]
                p.fence_dma("sp")
                p.wait("sp", "AGS", 1)
                p.fence_te()
                p.wait("te", "GPD", v_wqkv)
                agv = dr["ag_out"].ap().rearrange("(r c p) t -> p r c t", p=128, c=KC)
                ps_free = {0: None, 1: None}
                chunk_reader_te = {0: None, 1: None}
                pp = [0]

                def grp(slx, lhs_fn, rhs_fn, evac_fn):
                    pi = pp[0] % 2
                    pp[0] += 1
                    ps = qkvps[pi]
                    if ps_free[pi] is not None:
                        p.wait("te", "DV", ps_free[pi])
                    for kc in range(KC):
                        p.op("te", lambda e, kc=kc, ps=ps: e.matmul(
                            ps[:], lhs_fn(kc), rhs_fn(kc),
                            start=(kc == 0), stop=(kc == KC - 1)),
                            inc="TE" if kc == KC - 1 else None)
                    p.wait("dv", "TE", p.cnt["TE"])
                    ps_free[pi] = evac_fn(ps)

                def xch_rhs(slx):
                    return lambda kc, slx=slx: xch[:, slx*KC*512 + kc*512: slx*KC*512 + (kc+1)*512]

                for tbg in range(QB):
                    slx = tbg % 2
                    if chunk_reader_te[slx] is not None:
                        p.wait("sp", "TE", chunk_reader_te[slx])
                    p.op("sp", lambda e, tbg=tbg, slx=slx: e.dma_start(
                        out=xch[:, slx*KC*512:(slx+1)*KC*512].rearrange(
                            "p (c n) -> p c n", c=KC),
                        in_=agv[:, tbg // TB, :, (tbg % TB)*512:(tbg % TB + 1)*512]),
                        inc="SPD", k=16)
                    p.wait("te", "SPD", p.cnt["SPD"])
                    for p4 in range(4):
                        grp(slx,
                            lambda kc, p4=p4: wqkv[:, kc*1536 + p4*128: kc*1536 + (p4+1)*128],
                            xch_rhs(slx),
                            lambda ps, p4=p4, tbg=tbg: p.op(
                                "dv", lambda e, ps=ps, p4=p4, tbg=tbg: e.tensor_scalar(
                                    qT[:, p4*T + tbg*512: p4*T + (tbg+1)*512],
                                    ps[:], 0.125, None, ALU.mult), inc="DV"))
                    for p4 in range(4):
                        def kev(ps, p4=p4, tbg=tbg):
                            p.op("dv", lambda e, ps=ps, p4=p4, tbg=tbg: e.tensor_copy(
                                k_aug[0:64, (2*p4)*T + tbg*512: (2*p4)*T + (tbg+1)*512],
                                ps[0:64, :]))
                            return p.op("dv", lambda e, ps=ps, p4=p4, tbg=tbg: e.tensor_copy(
                                k_aug[64:128, (2*p4+1)*T + tbg*512: (2*p4+1)*T + (tbg+1)*512],
                                ps[64:128, :]), inc="DV")
                        grp(slx,
                            lambda kc, p4=p4: wqkv[:, kc*1536 + 512 + p4*128: kc*1536 + 512 + (p4+1)*128],
                            xch_rhs(slx),
                            kev)
                    for ts4 in range(4):
                        kt = tbg*4 + ts4
                        def vev(ps, kt=kt):
                            p.wait("dv", "DV", v_ones)
                            return p.op("dv", lambda e, ps=ps, kt=kt: e.tensor_copy(
                                v_sb[:, kt*520: kt*520 + 520].rearrange(
                                    "p (h x) -> p h x", h=NH)[:, :, 0:64],
                                ps[:].rearrange("p (h x) -> p h x", h=NH)), inc="DV")
                        grp(slx,
                            lambda kc, ts4=ts4, slx=slx: xch[:, slx*KC*512 + kc*512 + ts4*128: slx*KC*512 + kc*512 + (ts4+1)*128],
                            lambda kc: wqkv[:, kc*1536 + 1024: kc*1536 + 1536],
                            vev)
                    chunk_reader_te[slx] = p.cnt["TE"]
        with ExitStack() as psictx:
            cs1 = psictx.enter_context(nc.sbuf_tensor("cs1sb", [128, T], BF16))
            cs2 = psictx.enter_context(nc.sbuf_tensor("cs2sb", [128, T], BF16))
            tpsi = psictx.enter_context(nc.sbuf_tensor("tpsi", [128, T], BF16))
            a4 = psictx.enter_context(nc.sbuf_tensor("a4sb", [128, NH], F32))
            b4 = psictx.enter_context(nc.sbuf_tensor("b4sb", [128, NH], F32))
            a2 = psictx.enter_context(nc.sbuf_tensor("a2sb", [128, NH], F32))
            sgn = psictx.enter_context(nc.sbuf_tensor("sgnsb", [128, 1], F32))
            p.fence_dma("sp")
            for nm, buf in (("cs1", cs1), ("cs2", cs2), ("a4", a4), ("b4", b4),
                            ("sgnv", sgn)):
                p.op("sp", lambda e, nm=nm, buf=buf: e.dma_start(
                    out=buf[:], in_=dr[nm][:, :]), inc="SPD", k=16)
            v_csld = p.cnt["SPD"]
            p.wait("dv", "SPD", v_csld)
            v_a2 = p.op("dv", lambda e: e.tensor_scalar(
                a2[:], b4[:], sgn[:, 0:1], None, ALU.mult), inc="DV")
            p.wait("dv", "DV", v_a2)
            for h in range(NH):
                Bb = (1 - (h % 2)) * 64
                v_t = p.op("dv", lambda e, h=h: e.tensor_scalar(
                    tpsi[:], cs1[:], a4[:, h:h+1], None, ALU.mult), inc="DV")
                p.wait("dv", "DV", v_t)
                p.op("dv", lambda e, h=h, Bb=Bb: e.scalar_tensor_tensor(
                    k_aug[Bb:Bb+64, h*T:(h+1)*T], cs2[Bb:Bb+64, :],
                    a2[Bb:Bb+64, h:h+1], tpsi[Bb:Bb+64, :], ALU.mult, ALU.add),
                    inc="DV")
            v_psi = p.cnt["DV"]
        with ExitStack() as fctx:
            phi2 = fctx.enter_context(nc.sbuf_tensor("phi2sb", [128, T], BF16))
            masks = fctx.enter_context(nc.sbuf_tensor("maskssb", [128, 4*512], BF16))
            wout = fctx.enter_context(nc.sbuf_tensor("woutsb", [128, 4*1024], BF16))
            q_aug = fctx.enter_context(nc.sbuf_tensor("q_aug", [128, 2*512], BF16))
            P_sb = fctx.enter_context(nc.sbuf_tensor("P_sb", [128, 2*512], BF16))
            O_sb = fctx.enter_context(nc.sbuf_tensor("O_sb", [128, 4*512], BF16))
            OT_sb = fctx.enter_context(nc.sbuf_tensor("OT_sb", [128, 4*128], BF16))
            r_sb = fctx.enter_context(nc.sbuf_tensor("r_sb", [128, 4], F32))
            yst = fctx.enter_context(nc.sbuf_tensor("yst", [128, 2*1024], F32))
            p.fence_dma("sp")
            for nm, buf in (("phi2", phi2), ("masks", masks)):
                p.op("sp", lambda e, nm=nm, buf=buf: e.dma_start(
                    out=buf[:], in_=dr[nm][:, :]), inc="SPD", k=16)
            v_fl = p.cnt["SPD"]
            p.fence_dma("gp")
            p.op("gp", lambda e: e.dma_start(
                out=wout.ap().rearrange("p (c n) -> p c n", c=4),
                in_=r128(dr["woutT"].ap())), inc="GPD", k=16)
            v_wout = p.cnt["GPD"]
            with ExitStack() as pctx:
                Sps = [pctx.enter_context(nc.psum_tensor(f"Sps{i}", [128, 512], F32))
                       for i in range(2)]
                Ops = [pctx.enter_context(nc.psum_tensor(f"Ops{i}", [128, 260], F32))
                       for i in range(2)]
                tpo = pctx.enter_context(nc.psum_tensor("tpo", [128, 128], BF16))
                yps = pctx.enter_context(nc.psum_tensor("yps", [128, 1024], F32))
                p.fence_te()
                p.wait("te", "DV", v_psi)
                p.wait("te", "SPD", v_fl)
                p.wait("te", "GPD", v_wout)
                q_free_te = {0: None, 1: None}
                S_free_ac = {0: None, 1: None}
                P_free_te = {0: None, 1: None}
                P_free_dv = {0: None, 1: None}
                O_free_dv = {0: None, 1: None}
                osb_reader_te = None
                yst_reader_spd = {0: None, 1: None}
                for qb in range(QB):
                    if osb_reader_te is not None:
                        p.wait("dv", "TE", osb_reader_te)
                    for h in range(NH):
                        qsl = h % 2
                        pi = h % 2
                        ob = h % 2
                        if q_free_te[qsl] is not None:
                            p.wait("dv", "TE", q_free_te[qsl])
                        p.op("dv", lambda e, h=h, qb=qb, qsl=qsl, pi=pi: e.tensor_copy(
                            q_aug[pi*64:(pi+1)*64, qsl*512:(qsl+1)*512],
                            qT[pi*64:(pi+1)*64, (h//2)*T + qb*512: (h//2)*T + (qb+1)*512]))
                        v_qa = p.op("dv", lambda e, qb=qb, qsl=qsl, pi=pi: e.tensor_copy(
                            q_aug[(1-pi)*64:(2-pi)*64, qsl*512:(qsl+1)*512],
                            phi2[(1-pi)*64:(2-pi)*64, qb*512:(qb+1)*512]), inc="DV")
                        p.wait("te", "DV", v_qa)
                        if O_free_dv[ob] is not None:
                            p.wait("te", "DV", O_free_dv[ob])
                        nkt = qb*4 + 4
                        for kt in range(nkt):
                            ssl = kt % 2
                            diag = kt >= qb*4
                            if S_free_ac[ssl] is not None:
                                p.wait("te", "AC", S_free_ac[ssl])
                            v_s = p.op("te", lambda e, h=h, kt=kt, ssl=ssl, qsl=qsl: e.matmul(
                                Sps[ssl][:], k_aug[:, h*T + kt*128: h*T + (kt+1)*128],
                                q_aug[:, qsl*512:(qsl+1)*512],
                                start=True, stop=True), inc="TE")
                            p.wait("ac", "TE", v_s)
                            if P_free_te[ssl] is not None:
                                p.wait("ac", "TE", P_free_te[ssl])
                            if P_free_dv[ssl] is not None:
                                p.wait("ac", "DV", P_free_dv[ssl])
                            v_p = p.op("ac", lambda e, ssl=ssl: e.activation(
                                P_sb[:, ssl*512:(ssl+1)*512], Sps[ssl][:], AF.Exp),
                                inc="AC")
                            S_free_ac[ssl] = v_p
                            if diag:
                                d = kt - qb*4
                                p.wait("dv", "AC", v_p)
                                v_m = p.op("dv", lambda e, ssl=ssl, d=d: e.tensor_tensor(
                                    P_sb[:, ssl*512:(ssl+1)*512],
                                    P_sb[:, ssl*512:(ssl+1)*512],
                                    masks[:, d*512:(d+1)*512], ALU.mult), inc="DV")
                                P_free_dv[ssl] = v_m
                                p.wait("te", "DV", v_m)
                            else:
                                P_free_dv[ssl] = None
                                p.wait("te", "AC", v_p)
                            for qs in range(4):
                                last = (kt == nkt - 1 and qs == 3)
                                p.op("te", lambda e, kt=kt, qs=qs, ssl=ssl, ob=ob, h=h: e.matmul(
                                    Ops[ob][:, qs*65:(qs+1)*65],
                                    P_sb[:, ssl*512 + qs*128: ssl*512 + (qs+1)*128],
                                    v_sb[:, kt*520 + h*65: kt*520 + (h+1)*65],
                                    start=(kt == 0 and qs == 0), stop=last),
                                    inc="TE" if last else None)
                            P_free_te[ssl] = p.cnt["TE"]
                        q_free_te[qsl] = p.cnt["TE"]
                        v_av = p.cnt["TE"]
                        p.wait("dv", "TE", v_av)
                        for qs in range(4):
                            p.op("dv", lambda e, qs=qs, ob=ob: e.reciprocal(
                                r_sb[:, qs:qs+1], Ops[ob][:, qs*65 + 64: qs*65 + 65]),
                                inc="DV" if qs == 3 else None)
                        p.wait("dv", "DV", p.cnt["DV"])
                        for qs in range(4):
                            p.op("dv", lambda e, qs=qs, h=h, ob=ob: e.tensor_scalar(
                                O_sb[:, qs*512 + h*64: qs*512 + (h+1)*64],
                                Ops[ob][:, qs*65: qs*65 + 64],
                                r_sb[:, qs:qs+1], None, ALU.mult),
                                inc="DV" if qs == 3 else None)
                        O_free_dv[ob] = p.cnt["DV"]
                    v_osb = p.cnt["DV"]
                    p.wait("te", "DV", v_osb)
                    for qs in range(4):
                        ysl = qs % 2
                        for c4 in range(4):
                            v_t = p.op("te", lambda e, qs=qs, c4=c4: e.transpose(
                                tpo[:], O_sb[:, qs*512 + c4*128: qs*512 + (c4+1)*128],
                                ident[:]), inc="TE")
                            p.wait("dv", "TE", v_t)
                            v_otc = p.op("dv", lambda e, c4=c4: e.tensor_copy(
                                OT_sb[:, c4*128:(c4+1)*128], tpo[:]), inc="DV")
                            p.wait("te", "DV", v_otc)
                        for c4 in range(4):
                            for nh2 in range(2):
                                last = (c4 == 3 and nh2 == 1)
                                p.op("te", lambda e, c4=c4, nh2=nh2: e.matmul(
                                    yps[:, nh2*512:(nh2+1)*512],
                                    OT_sb[:, c4*128:(c4+1)*128],
                                    wout[:, c4*1024 + nh2*512: c4*1024 + (nh2+1)*512],
                                    start=(c4 == 0), stop=(c4 == 3)),
                                    inc="TE" if last else None)
                        v_y = p.cnt["TE"]
                        p.wait("dv", "TE", v_y)
                        if yst_reader_spd[ysl] is not None:
                            p.wait("dv", "SPD", yst_reader_spd[ysl])
                        v_ye = p.op("dv", lambda e, ysl=ysl: e.tensor_copy(
                            yst[:, ysl*1024:(ysl+1)*1024], yps[:]), inc="DV")
                        p.wait("sp", "DV", v_ye)
                        p.op("sp", lambda e, qb=qb, qs=qs, ysl=ysl: e.dma_start(
                            out=dr["y_in"].ap().rearrange("(t p) n -> p t n", p=128)[:, qb*4+qs],
                            in_=yst[:, ysl*1024:(ysl+1)*1024]), inc="SPD", k=16)
                        yst_reader_spd[ysl] = p.cnt["SPD"]
                    osb_reader_te = p.cnt["TE"]
                v_yall = p.cnt["SPD"]
        p.wait("gp", "SPD", v_yall)
        p.op("gp", lambda e: e.collective_compute(
            "ReduceScatter", ALU.add, replica_groups=GROUPS,
            ins=[dr["y_in"].ap().opt()], outs=[dr["y_out"].ap().opt()]),
            inc="RSS", k=1)


def build_program():
    nc = bass.Bass("TRN2", target_bir_lowering=False)
    dr = {}
    dr["x_own"] = nc.dram_tensor("x_own", [NTOK, D], F32, kind="ExternalInput")
    dr["router_w"] = nc.dram_tensor("router_w", [D, 32], F32, kind="ExternalInput")
    dr["sh_gate_w"] = nc.dram_tensor("sh_gate_w", [128, KC], F32, kind="ExternalInput")
    dr["sh_gate"] = nc.dram_tensor("sh_gate", [D, ISH], F32, kind="ExternalInput")
    dr["sh_up"] = nc.dram_tensor("sh_up", [D, ISH], F32, kind="ExternalInput")
    dr["sh_down"] = nc.dram_tensor("sh_down", [ISH, D], F32, kind="ExternalInput")
    dr["w_gate"] = nc.dram_tensor("w_gate", [NE, D, EI], F32, kind="ExternalInput")
    dr["w_up"] = nc.dram_tensor("w_up", [NE, D, EI], F32, kind="ExternalInput")
    dr["w_down"] = nc.dram_tensor("w_down", [NE, EI, D], F32, kind="ExternalInput")
    dr["wqkvT"] = nc.dram_tensor("wqkvT", [D, 1536], F32, kind="ExternalInput")
    dr["woutT"] = nc.dram_tensor("woutT", [512, D], F32, kind="ExternalInput")
    dr["ident"] = nc.dram_tensor("ident", [128, 128], BF16, kind="ExternalInput")
    dr["cs1"] = nc.dram_tensor("cs1", [128, T], BF16, kind="ExternalInput")
    dr["cs2"] = nc.dram_tensor("cs2", [128, T], BF16, kind="ExternalInput")
    dr["phi2"] = nc.dram_tensor("phi2", [128, T], BF16, kind="ExternalInput")
    dr["a4"] = nc.dram_tensor("a4", [128, NH], F32, kind="ExternalInput")
    dr["b4"] = nc.dram_tensor("b4", [128, NH], F32, kind="ExternalInput")
    dr["sgnv"] = nc.dram_tensor("sgnv", [128, 1], F32, kind="ExternalInput")
    dr["masks"] = nc.dram_tensor("masks", [128, 4*512], BF16, kind="ExternalInput")
    dr["out"] = nc.dram_tensor("out", [NTOK, D], F32, kind="ExternalOutput")
    dr["xspill"] = nc.dram_tensor("xspill", [NTOK, D], F32)
    dr["ag_in"] = nc.dram_tensor("ag_in", [D, NTOK], BF16)
    dr["ag_out"] = nc.dram_tensor("ag_out", [2*D, NTOK], BF16)
    dr["y_in"] = nc.dram_tensor("y_in", [T, D], F32)
    dr["y_out"] = nc.dram_tensor("y_out", [NTOK, D], F32)

    with ExitStack() as gctx:
        p = Prog(nc)
        gctx.enter_context(p.ctx)
        ident = gctx.enter_context(nc.sbuf_tensor("identsb", [128, 128], BF16))
        epscol = gctx.enter_context(nc.sbuf_tensor("epscol", [128, 1], F32))
        blk = gctx.enter_context(nc.Block())
        p.op("sp", lambda e: e.dma_start(out=ident[:], in_=dr["ident"][:, :]),
             inc="SPD", k=16)
        v_ident = p.cnt["SPD"]
        v_eps = p.op("dv", lambda e: e.memset(epscol[:], EPS), inc="DV")

        with ExitStack() as f1ctx:
            b = {"ident": ident, "eps": epscol}
            x1 = f1ctx.enter_context(nc.sbuf_tensor("x1", [128, TT*1024], F32))
            alloc_ffn_bufs(nc, f1ctx, b, "c0")
            b["x"] = x1
            p.op("sp", lambda e: e.dma_start(
                out=x1.ap().rearrange("p (t n) -> p t n", t=TT),
                in_=dr["x_own"].ap().rearrange("(t p) n -> p t n", p=128)),
                inc="SPD", k=16)
            v_x = p.cnt["SPD"]
            p.wait("te", "SPD", v_ident)
            p.wait("dv", "SPD", v_x)
            p.wait("ac", "SPD", v_x)
            if True:
                info1 = emit_ffn(p, b, dr, x1, 0)
                with ExitStack() as ps0:
                    tpA = ps0.enter_context(nc.psum_tensor("tpA_l2", [128, 128], BF16))
                    tpB = ps0.enter_context(nc.psum_tensor("tpB_l2", [128, 128], BF16))
                    p.fence_te()
                    v_xln2 = emit_ln(p, x1, b["ln"], b["xlnT"], ident, tpA, tpB)
                p.wait("sp", "DV", v_xln2)
                p.op("sp", lambda e: e.dma_start(
                    out=dr["xspill"].ap().rearrange("(t p) n -> p t n", p=128),
                    in_=x1.ap().rearrange("p (t n) -> p t n", t=TT)),
                    inc="SPD", k=16)
                p.op("sp", lambda e, _t=b["xlnT"]: e.dma_start(
                    out=r128(dr["ag_in"].ap()),
                    in_=_t.ap().rearrange("p (c n) -> p c n", c=KC)),
                    inc="SPD", k=16)
                v_agin = p.cnt["SPD"]
        emit_attention(p, nc, dr, ident, v_agin)
        with ExitStack() as f2ctx:
            b = {"ident": ident, "eps": epscol}
            x2 = f2ctx.enter_context(nc.sbuf_tensor("x2", [128, TT*1024], F32))
            alloc_ffn_bufs(nc, f2ctx, b, "c1")
            b["x"] = x2
            p.fence_dma("sp")
            p.wait("sp", "RSS", 1)
            p.op("sp", lambda e: e.dma_start(
                out=x2.ap().rearrange("p (t n) -> p t n", t=TT),
                in_=dr["y_out"].ap().rearrange("(t p) n -> p t n", p=128)),
                inc="SPD", k=16)
            v_y = p.cnt["SPD"]
            with nc.sbuf_tensor("stg", [128, 2*1024], F32) as stg:
                for tt in range(TT):
                    sl = tt % 2
                    p.op("sp", lambda e, tt=tt, sl=sl, stg=stg: e.dma_start(
                        out=stg[:, sl*1024:(sl+1)*1024],
                        in_=dr["xspill"].ap().rearrange("(t p) n -> p t n", p=128)[:, tt]),
                        inc="SPD", k=16)
                    v_s = p.cnt["SPD"]
                    p.wait("dv", "SPD", max(v_s, v_y))
                    v_add = p.op("dv", lambda e, tt=tt, sl=sl, stg=stg: e.tensor_tensor(
                        x2[:, tt*1024:(tt+1)*1024], x2[:, tt*1024:(tt+1)*1024],
                        stg[:, sl*1024:(sl+1)*1024], ALU.add), inc="DV")
                    p.wait("sp", "DV", v_add)
            p.wait("dv", "DV", p.cnt["DV"])
            p.fence_dma("gp")
            p.op("gp", lambda e: e.dma_start(
                out=b["rw"].ap().rearrange("p (c n) -> p c n", c=KC),
                in_=r128(dr["router_w"].ap())), inc="GPD", k=16)
            p.op("gp", lambda e: e.dma_start(
                out=b["sgw"].ap(), in_=dr["sh_gate_w"][:, :]), inc="GPD", k=16)
            b["v_wsmall"] = p.cnt["GPD"]
            info2 = emit_ffn(p, b, dr, x2, 1)
            p.wait("sp", "DV", info2["x_dv"])
            p.op("sp", lambda e: e.dma_start(
                out=dr["out"].ap().rearrange("(t p) n -> p t n", p=128),
                in_=x2.ap().rearrange("p (t n) -> p t n", t=TT)),
                inc="SPD", k=16)
            p.wait("sp", "SPD", p.cnt["SPD"])
        p.flush(blk)
    return nc


def make_core_inputs(core, full):
    b, hg = core // 2, core % 2
    X = np.asarray(full["L0_attn_x"], np.float32).reshape(4*T, D)
    rows = slice(b*T + hg*NTOK, b*T + (hg+1)*NTOK)
    Wqkv = np.asarray(full["L0_attn_Wqkv"], np.float32)
    heads = range(hg*NH, (hg+1)*NH)
    qrows = np.concatenate([np.arange(h*64, (h+1)*64) for h in heads])
    sel = np.concatenate([qrows, D + qrows, 2*D + qrows])
    wqkvT = np.ascontiguousarray(Wqkv[sel].T)
    Wout = np.asarray(full["L0_attn_Wout"], np.float32)
    woutT = np.ascontiguousarray(Wout[:, hg*512:(hg+1)*512].T)
    alpha = np.asarray(full["L0_attn_alpha"], np.float32)[hg*NH:(hg+1)*NH]
    beta = np.asarray(full["L0_attn_beta"], np.float32)[hg*NH:(hg+1)*NH]
    a4 = np.ascontiguousarray(np.tile(alpha.T, (4, 1)))
    b4 = np.ascontiguousarray(np.tile(beta.T, (4, 1)))
    consts = make_consts()
    return {
        "x_own": np.ascontiguousarray(X[rows]),
        "router_w": np.asarray(full["L0_ffn_router_w"], np.float32),
        "sh_gate_w": np.ascontiguousarray(
            np.asarray(full["L0_ffn_sh_gate_w"], np.float32).reshape(KC, 128).T),
        "sh_gate": np.asarray(full["L0_ffn_sh_gate"], np.float32),
        "sh_up": np.asarray(full["L0_ffn_sh_up"], np.float32),
        "sh_down": np.asarray(full["L0_ffn_sh_down"], np.float32),
        "w_gate": np.asarray(full["L0_ffn_w_gate"], np.float32),
        "w_up": np.asarray(full["L0_ffn_w_up"], np.float32),
        "w_down": np.asarray(full["L0_ffn_w_down"], np.float32),
        "wqkvT": wqkvT, "woutT": woutT, "a4": a4, "b4": b4,
        **consts,
    }


_CONSTS = None

def make_consts():
    global _CONSTS
    if _CONSTS is not None:
        return _CONSTS
    R = 32
    om = (10000.0 ** (-2.0 * np.arange(R, dtype=np.float64) / 64.0))
    k = np.arange(T, dtype=np.float64)
    C = np.cos(om[:, None] * k[None, :])
    S = np.sin(om[:, None] * k[None, :])
    cs1 = np.concatenate([C, S, C, S], axis=0).astype(BF)
    cs2 = np.concatenate([S, C, S, C], axis=0).astype(BF)
    phi = np.concatenate([C, S], axis=0)
    phi2 = np.concatenate([phi, phi], axis=0).astype(BF)
    sgn = np.concatenate([-np.ones(32), np.ones(32), -np.ones(32),
                          np.ones(32)]).astype(np.float32)[:, None]
    i = np.arange(128)[:, None]
    j = np.arange(512)[None, :]
    masks = np.concatenate(
        [(j >= i + d*128).astype(np.float32) for d in range(4)], axis=1).astype(BF)
    _CONSTS = {
        "ident": np.eye(128).astype(BF),
        "cs1": np.ascontiguousarray(cs1), "cs2": np.ascontiguousarray(cs2),
        "phi2": np.ascontiguousarray(phi2), "sgnv": sgn, "masks": masks,
    }
    return _CONSTS


def kernel(**inputs):
    from concourse.bass_utils import run_bass_kernel_spmd
    nc = build_program()
    in_maps = [make_core_inputs(c, inputs) for c in range(8)]
    res = run_bass_kernel_spmd(nc, in_maps, core_ids=list(range(8)))
    out = np.empty((4 * T, D), dtype=np.float32)
    for c in range(8):
        b, hg = c // 2, c % 2
        out[b*T + hg*NTOK: b*T + (hg+1)*NTOK] = res.results[c]["out"]
    return out.reshape(4, T, D)


# version 03
# speedup vs baseline: 1.0845x; 1.0536x over previous
import numpy as np
import ml_dtypes
from contextlib import ExitStack

import concourse.bass as bass
import concourse.mybir as mybir

F32 = mybir.dt.float32
BF16 = mybir.dt.bfloat16
AF = mybir.ActivationFunctionType
ALU = mybir.AluOpType
AXX = mybir.AxisListType.X

D = 1024
KC = 8
NTOK = 2048
TT = 16
TB = 4
NE = 32
EI = 768
EIC = 6
ISH = 2048
T = 4096
NH = 8
QB = 8
EPS = 1e-5

ENGS = ("te", "dv", "ac", "sp", "gp")
BF = ml_dtypes.bfloat16


class Prog:
    def __init__(self, nc):
        self.nc = nc
        self.q = {e: [] for e in ENGS}
        self.sems = {}
        self.cnt = {}
        self.ctx = ExitStack()
        for s in ("TE", "DV", "AC", "GPD", "SPD", "AGS", "RSS"):
            self.sems[s] = self.ctx.enter_context(nc.semaphore(s))
            self.cnt[s] = 0

    def op(self, eng, fn, inc=None, k=1):
        if inc is None:
            self.q[eng].append(lambda e, fn=fn: fn(e))
            return None
        self.cnt[inc] += k
        s = self.sems[inc]
        self.q[eng].append(lambda e, fn=fn, s=s, k=k: fn(e).then_inc(s, k))
        return self.cnt[inc]

    def wait(self, eng, sem, val):
        if val is None or val <= 0:
            return
        self.q[eng].append(lambda e, s=self.sems[sem], v=val: e.wait_ge(s, v))

    def fence_te(self):
        self.wait("te", "DV", self.cnt["DV"])
        self.wait("te", "AC", self.cnt["AC"])

    def fence_dma(self, eng):
        for s in ("TE", "DV", "AC"):
            self.wait(eng, s, self.cnt[s])

    def flush(self, block):
        def runner(fns):
            def go(e):
                for f in fns:
                    f(e)
            return go
        block.tensor(runner(self.q["te"]))
        block.vector(runner(self.q["dv"]))
        block.scalar(runner(self.q["ac"]))
        block.sync(runner(self.q["sp"]))
        block.gpsimd(runner(self.q["gp"]))


def r128(dram_ap):
    return dram_ap.rearrange("(c p) n -> p c n", p=128)


def emit_ln(p, x, ln, xlnT, ident, tpA, tpB, ln_fence=True):
    if ln_fence:
        p.wait("ac", "DV", p.cnt["DV"])
    for tt in range(TT):
        p.op("dv", lambda e, tt=tt: e.reduce_sum(
            ln["sums"][:, tt:tt+1], x[:, tt*1024:(tt+1)*1024], axis=AXX),
            inc="DV" if tt == TT - 1 else None)
    p.wait("dv", "DV", p.cnt["DV"])
    v_nm = p.op("dv", lambda e: e.tensor_scalar(
        ln["nm"][:, 0:TT], ln["sums"][:, 0:TT], -1.0 / D, None, ALU.mult), inc="DV")
    p.wait("ac", "DV", v_nm)
    v_red = {}
    for tt in range(TT):
        slot = tt % 2
        if tt >= 2:
            p.wait("ac", "DV", v_red[tt - 2])
        v_sq = p.op("ac", lambda e, tt=tt, slot=slot: e.activation(
            ln["sq"][:, slot*1024:(slot+1)*1024], x[:, tt*1024:(tt+1)*1024],
            AF.Square, bias=ln["nm"][:, tt:tt+1]), inc="AC")
        p.wait("dv", "AC", v_sq)
        v_red[tt] = p.op("dv", lambda e, tt=tt, slot=slot: e.reduce_sum(
            ln["vsums"][:, tt:tt+1], ln["sq"][:, slot*1024:(slot+1)*1024],
            axis=AXX), inc="DV")
    p.wait("ac", "DV", v_red[TT - 1])
    v_rstd = p.op("ac", lambda e: e.activation(
        ln["rstd"][:, 0:TT], ln["vsums"][:, 0:TT], AF.Sqrt,
        scale=1.0 / D, bias=ln["eps"][:, 0:1]), inc="AC")
    p.wait("dv", "AC", v_rstd)
    v_rcp = p.op("dv", lambda e: e.reciprocal(
        ln["rstd2"][:, 0:TT], ln["rstd"][:, 0:TT]), inc="DV")
    p.wait("dv", "DV", v_rcp)
    v_cp = {}
    v_tr_tile = {}
    for tt in range(TT):
        slot = tt % 2
        if tt >= 2:
            p.wait("dv", "TE", v_tr_tile[tt - 2])
        v_xln = p.op("dv", lambda e, tt=tt, slot=slot: e.tensor_scalar(
            ln["xln"][:, slot*1024:(slot+1)*1024], x[:, tt*1024:(tt+1)*1024],
            ln["nm"][:, tt:tt+1], ln["rstd2"][:, tt:tt+1], ALU.add, ALU.mult),
            inc="DV")
        p.wait("te", "DV", v_xln)
        for kc in range(KC):
            tp = tpA if kc % 2 == 0 else tpB
            prev = (tt, kc - 2) if kc >= 2 else ((tt - 1, KC + kc - 2) if tt > 0 else None)
            if prev is not None:
                p.wait("te", "DV", v_cp[prev])
            v_t = p.op("te", lambda e, tt=tt, kc=kc, slot=slot, tp=tp: e.transpose(
                tp[:], ln["xln"][:, slot*1024 + kc*128: slot*1024 + (kc+1)*128],
                ident[:]), inc="TE")
            p.wait("dv", "TE", v_t)
            v_cp[(tt, kc)] = p.op("dv", lambda e, tt=tt, kc=kc, tp=tp: e.tensor_copy(
                xlnT[:, kc*NTOK + tt*128: kc*NTOK + (tt+1)*128], tp[:]), inc="DV")
        v_tr_tile[tt] = p.cnt["TE"]
    return p.cnt["DV"]


def emit_router(p, b, v_xlnT_dv, v_w_gpd):
    nc = p.nc
    xlnT, rw, sgw = b["xlnT"], b["rw"], b["sgw"]
    with ExitStack() as ps:
        import itertools as _it
        _rid = getattr(emit_router, "_n", 0); emit_router._n = _rid + 1
        lg = ps.enter_context(nc.psum_tensor(f"lg{_rid}", [128, TT*32], F32))
        sg = ps.enter_context(nc.psum_tensor(f"sg{_rid}", [128, TT], F32))
        p.fence_te()
        p.wait("te", "DV", v_xlnT_dv)
        p.wait("te", "GPD", v_w_gpd)
        for tt in range(TT):
            for kc in range(KC):
                p.op("te", lambda e, tt=tt, kc=kc: e.matmul(
                    lg[:, tt*32:(tt+1)*32],
                    xlnT[:, kc*NTOK + tt*128: kc*NTOK + (tt+1)*128],
                    rw[:, kc*32:(kc+1)*32],
                    start=(tt == 0 and kc == 0), stop=(tt == TT-1 and kc == KC-1)))
        for tt in range(TT):
            for kc in range(KC):
                last = (tt == TT-1 and kc == KC-1)
                p.op("te", lambda e, tt=tt, kc=kc: e.matmul(
                    sg[:, tt:tt+1],
                    xlnT[:, kc*NTOK + tt*128: kc*NTOK + (tt+1)*128],
                    sgw[:, kc:kc+1],
                    start=(tt == 0 and kc == 0), stop=last),
                    inc="TE" if last else None)
        v_te = p.cnt["TE"]
        p.wait("dv", "TE", v_te)
        for tt in range(TT):
            p.op("dv", lambda e, tt=tt: e.reduce_max(
                b["mx"][:, tt:tt+1], lg[:, tt*32:(tt+1)*32], axis=AXX),
                inc="DV" if tt == TT - 1 else None)
        p.wait("dv", "DV", p.cnt["DV"])
        v_nmx = p.op("dv", lambda e: e.tensor_scalar(
            b["nmx"][:, 0:TT], b["mx"][:, 0:TT], -1.0, None, ALU.mult), inc="DV")
        p.wait("ac", "DV", v_nmx)
        for tt in range(TT):
            p.op("ac", lambda e, tt=tt: e.activation(
                b["expb"][:, tt*32:(tt+1)*32], lg[:, tt*32:(tt+1)*32], AF.Exp,
                bias=b["nmx"][:, tt:tt+1]))
        v_sig = p.op("ac", lambda e: e.activation(
            b["sig"][:, 0:TT], sg[:, 0:TT], AF.Sigmoid), inc="AC")
        p.wait("dv", "AC", v_sig)
        v_g2 = p.op("dv", lambda e: e.tensor_scalar(
            b["gate2"][:, 0:TT], b["sig"][:, 0:TT], 0.5, None, ALU.mult), inc="DV")
        expb, cur, wbuf, mb, comb = b["expb"], b["cur"], b["wbuf"], b["mb"], b["combine"]
        src = expb
        for it in range(4):
            for tt in range(TT):
                p.op("dv", lambda e, tt=tt, it=it, src=src: e.reduce_max(
                    mb[:, it*TT + tt: it*TT + tt + 1], src[:, tt*32:(tt+1)*32],
                    axis=AXX), inc="DV" if tt == TT - 1 else None)
            p.wait("dv", "DV", p.cnt["DV"])
            if it < 3:
                for tt in range(TT):
                    p.op("dv", lambda e, tt=tt, it=it, src=src: e.scalar_tensor_tensor(
                        cur[:, tt*32:(tt+1)*32], src[:, tt*32:(tt+1)*32],
                        mb[:, it*TT + tt: it*TT + tt + 1], src[:, tt*32:(tt+1)*32],
                        ALU.is_lt, ALU.mult), inc="DV" if tt == TT - 1 else None)
                p.wait("dv", "DV", p.cnt["DV"])
                src = cur
        for tt in range(TT):
            p.op("dv", lambda e, tt=tt: e.scalar_tensor_tensor(
                wbuf[:, tt*32:(tt+1)*32], expb[:, tt*32:(tt+1)*32],
                mb[:, 3*TT + tt: 3*TT + tt + 1], expb[:, tt*32:(tt+1)*32],
                ALU.is_ge, ALU.mult), inc="DV" if tt == TT - 1 else None)
        p.wait("dv", "DV", p.cnt["DV"])
        for tt in range(TT):
            p.op("dv", lambda e, tt=tt: e.reduce_sum(
                b["s4"][:, tt:tt+1], wbuf[:, tt*32:(tt+1)*32], axis=AXX),
                inc="DV" if tt == TT - 1 else None)
        p.wait("dv", "DV", p.cnt["DV"])
        v = p.op("dv", lambda e: e.tensor_scalar(
            b["s8"][:, 0:TT], b["s4"][:, 0:TT], 2.0, None, ALU.mult), inc="DV")
        p.wait("dv", "DV", v)
        v = p.op("dv", lambda e: e.reciprocal(b["r2"][:, 0:TT], b["s8"][:, 0:TT]),
                 inc="DV")
        p.wait("dv", "DV", v)
        for tt in range(TT):
            p.op("dv", lambda e, tt=tt: e.tensor_scalar(
                comb[:, tt*32:(tt+1)*32], wbuf[:, tt*32:(tt+1)*32],
                b["r2"][:, tt:tt+1], None, ALU.mult),
                inc="DV" if tt == TT - 1 else None)
    return {"combine_dv": p.cnt["DV"], "gate2_dv": v_g2,
            "lg_reader_ac": v_sig, "lg_reader_dv": v_nmx}


def emit_gu_act(p, st, lw_g, lw_u, xlnT, gpsl, upsl, act_dst, silu_sb, tok_off):
    i = st.get("gu_rot", 0) & 1
    st["gu_rot"] = i ^ 1
    gps, ups = gpsl[i], upsl[i]
    busy = st.get(("gu_busy", i))
    if busy is not None:
        p.wait("te", "AC", busy[0])
        p.wait("te", "DV", busy[1])
    for kc in range(KC):
        p.op("te", lambda e, kc=kc, gps=gps: e.matmul(
            gps[:], lw_g(kc), xlnT[:, kc*NTOK + tok_off: kc*NTOK + tok_off + 512],
            start=(kc == 0), stop=(kc == KC - 1)),
            inc="TE" if kc == KC - 1 else None)
    v_g = p.cnt["TE"]
    for kc in range(KC):
        p.op("te", lambda e, kc=kc, ups=ups: e.matmul(
            ups[:], lw_u(kc), xlnT[:, kc*NTOK + tok_off: kc*NTOK + tok_off + 512],
            start=(kc == 0), stop=(kc == KC - 1)),
            inc="TE" if kc == KC - 1 else None)
    v_u = p.cnt["TE"]
    slot = st["silu_parity"]
    st["silu_parity"] ^= 1
    p.wait("ac", "TE", v_g)
    if st.get(("silu_free", slot)):
        p.wait("ac", "DV", st[("silu_free", slot)])
    v_silu = p.op("ac", lambda e, slot=slot, gps=gps: e.activation(
        silu_sb[:, slot*512:(slot+1)*512], gps[:], AF.Silu), inc="AC")
    p.wait("dv", "AC", v_silu)
    p.wait("dv", "TE", v_u)
    v_act = p.op("dv", lambda e, slot=slot, ups=ups: e.tensor_tensor(
        act_dst, silu_sb[:, slot*512:(slot+1)*512], ups[:], ALU.mult), inc="DV")
    st[("gu_busy", i)] = (v_silu, v_act)
    st[("silu_free", slot)] = v_act
    return v_act


def emit_ffn(p, b, drams, x, call):
    nc = p.nc
    xlnT = b["xlnT"]
    if call == 0:
        p.fence_dma("gp")
        p.op("gp", lambda e: e.dma_start(
            out=b["rw"].ap().rearrange("p (c n) -> p c n", c=KC),
            in_=r128(drams["router_w"].ap())), inc="GPD", k=16)
        p.op("gp", lambda e: e.dma_start(
            out=b["sgw"].ap(), in_=drams["sh_gate_w"][:, :]), inc="GPD", k=16)
        b["v_wsmall"] = p.cnt["GPD"]
    def load_sh(hf):
        for nm, buf in (("sh_gate", b["shg"]), ("sh_up", b["shu"])):
            p.op("gp", lambda e, nm=nm, buf=buf, hf=hf: e.dma_start(
                out=buf.ap().rearrange("p (c n) -> p c n", c=KC),
                in_=r128(drams[nm][:, hf*1024:(hf+1)*1024])), inc="GPD", k=16)
        vA = p.cnt["GPD"]
        p.op("gp", lambda e, hf=hf: e.dma_start(
            out=b["shd"].ap().rearrange("p (c n) -> p c n", c=KC),
            in_=r128(drams["sh_down"][hf*1024:(hf+1)*1024, :])), inc="GPD", k=16)
        return (vA, p.cnt["GPD"])

    def load_expert(ei, sl):
        for nm, buf, cs, w in (("w_gate", b["wg"], KC, EI),
                               ("w_up", b["wu"], KC, EI),
                               ("w_down", b["wd"], EIC, 1024)):
            p.op("gp", lambda e, nm=nm, buf=buf, ei=ei, sl=sl, cs=cs, w=w: e.dma_start(
                out=buf[:, sl*cs*w:(sl+1)*cs*w].rearrange("p (c n) -> p c n", c=cs),
                in_=r128(drams[nm][ei])), inc="GPD", k=16)
        return p.cnt["GPD"]

    with ExitStack() as ps0:
        tpA = ps0.enter_context(nc.psum_tensor(f"tpA{call}", [128, 128], BF16))
        tpB = ps0.enter_context(nc.psum_tensor(f"tpB{call}", [128, 128], BF16))
        p.fence_te()
        v_xlnT = emit_ln(p, x, b["ln"], xlnT, b["ident"], tpA, tpB)
    rinfo = emit_router(p, b, v_xlnT, b["v_wsmall"])
    x_writable_dv = max(rinfo["combine_dv"], rinfo["gate2_dv"])

    with ExitStack() as ps:
        gpsl = [ps.enter_context(nc.psum_tensor(f"gps{call}_{i}", [128, 512], F32))
                for i in range(2)]
        upsl = [ps.enter_context(nc.psum_tensor(f"ups{call}_{i}", [128, 512], F32))
                for i in range(2)]
        routed = [ps.enter_context(nc.psum_tensor(f"rt{call}_{i}", [128, 1024], F32))
                  for i in range(2)]
        st = {"silu_parity": 0}
        pair_free_dv = {0: None, 1: None}
        ts_ctr = 0
        first_mm_fenced = [False]

        def down_group(lhs_fn, rhs_fn, tt, scalar_ap):
            nonlocal ts_ctr
            pr = ts_ctr % 2
            ts_ctr += 1
            if pair_free_dv[pr] is not None:
                p.wait("te", "DV", pair_free_dv[pr])
            nic = lhs_fn(None)
            for ic in range(nic):
                for nh2 in range(2):
                    last = (ic == nic - 1 and nh2 == 1)
                    p.op("te", lambda e, ic=ic, nh2=nh2, pr=pr: e.matmul(
                        routed[pr][:, nh2*512:(nh2+1)*512],
                        lhs_fn(ic), rhs_fn(ic, nh2),
                        start=(ic == 0), stop=(ic == nic - 1)),
                        inc="TE" if last else None)
            v_te = p.cnt["TE"]
            p.wait("dv", "TE", v_te)
            if not first_mm_fenced[0]:
                p.wait("dv", "DV", x_writable_dv)
                first_mm_fenced[0] = True
            pair_free_dv[pr] = p.op("dv", lambda e, tt=tt, pr=pr, s=scalar_ap: e.scalar_tensor_tensor(
                x[:, tt*1024:(tt+1)*1024], routed[pr][:], s,
                x[:, tt*1024:(tt+1)*1024], ALU.mult, ALU.add), inc="DV")

        with ExitStack() as shctx:
            alloc_ffn_shared(nc, shctx, b, f"f{call}")
            shg, shu, shd, actsh = b["shg"], b["shu"], b["shd"], b["actsh"]
            sh_gpd = [load_sh(0)]
            for hf in range(2):
                p.fence_te()
                p.wait("te", "GPD", sh_gpd[hf][0])
                for icc in range(KC):
                    for tb in range(TB):
                        emit_gu_act(
                            p, st,
                            (lambda kc, icc=icc: shg[:, kc*1024 + icc*128: kc*1024 + (icc+1)*128]),
                            (lambda kc, icc=icc: shu[:, kc*1024 + icc*128: kc*1024 + (icc+1)*128]),
                            xlnT, gpsl, upsl,
                            actsh[:, icc*NTOK + tb*512: icc*NTOK + (tb+1)*512],
                            b["silu_sb"], tb*512)
                v_actsh_dv = p.cnt["DV"]
                p.wait("te", "DV", v_actsh_dv)
                p.wait("te", "GPD", sh_gpd[hf][1])
                for tt in range(TT):
                    down_group(
                        (lambda ic, tt=tt: KC if ic is None else
                         actsh[:, ic*NTOK + tt*128: ic*NTOK + (tt+1)*128]),
                        (lambda ic, nh2: shd[:, ic*1024 + nh2*512: ic*1024 + (nh2+1)*512]),
                        tt, b["gate2"][:, tt:tt+1])
                if hf == 0:
                    p.wait("gp", "TE", p.cnt["TE"])
                    sh_gpd.append(load_sh(1))
            v_te_sh, v_dv_sh = p.cnt["TE"], p.cnt["DV"]

        alloc_ffn_experts(nc, ps, b, f"f{call}")
        wg, wu, wd, act_sb = b["wg"], b["wu"], b["wd"], b["act_sb"]
        p.wait("gp", "TE", v_te_sh)
        p.wait("gp", "DV", v_dv_sh)
        for ei in range(2):
            b["ew_gpd"][ei] = load_expert(ei, ei % 2)
        act_slot_reader_te = {0: None, 1: None}
        for ei in range(NE):
            sl = ei % 2
            p.wait("te", "GPD", b["ew_gpd"][ei])
            for tb in range(TB):
                asl = tb % 2
                if act_slot_reader_te[asl] is not None:
                    p.wait("dv", "TE", act_slot_reader_te[asl])
                for ic in range(EIC):
                    emit_gu_act(
                        p, st,
                        (lambda kc, ic=ic, sl=sl: wg[:, sl*KC*EI + kc*EI + ic*128: sl*KC*EI + kc*EI + (ic+1)*128]),
                        (lambda kc, ic=ic, sl=sl: wu[:, sl*KC*EI + kc*EI + ic*128: sl*KC*EI + kc*EI + (ic+1)*128]),
                        xlnT, gpsl, upsl,
                        act_sb[:, asl*EIC*512 + ic*512: asl*EIC*512 + (ic+1)*512],
                        b["silu_sb"], tb*512)
                v_act_dv = p.cnt["DV"]
                p.wait("te", "DV", v_act_dv)
                for ts4 in range(4):
                    tt = tb*4 + ts4
                    down_group(
                        (lambda ic, ts4=ts4, asl=asl: EIC if ic is None else
                         act_sb[:, asl*EIC*512 + ic*512 + ts4*128: asl*EIC*512 + ic*512 + (ts4+1)*128]),
                        (lambda ic, nh2, sl=sl: wd[:, sl*EIC*1024 + ic*1024 + nh2*512: sl*EIC*1024 + ic*1024 + (nh2+1)*512]),
                        tt, b["combine"][:, tt*32 + ei: tt*32 + ei + 1])
                act_slot_reader_te[asl] = p.cnt["TE"]
            if ei + 2 < NE:
                p.wait("gp", "TE", p.cnt["TE"])
                b["ew_gpd"][ei + 2] = load_expert(ei + 2, (ei + 2) % 2)
    return {"x_dv": p.cnt["DV"]}


def alloc_ffn_bufs(nc, ctx, b, tag=""):
    sb = lambda n, shape, dt=F32: ctx.enter_context(nc.sbuf_tensor(n + tag, shape, dt))
    b["xlnT"] = sb("xlnT", [128, KC*NTOK], BF16)
    ln = {}
    ln["sums"] = sb("ln_sums", [128, TT])
    ln["nm"] = sb("ln_nm", [128, TT])
    ln["vsums"] = sb("ln_vsums", [128, TT])
    ln["rstd"] = sb("ln_rstd", [128, TT])
    ln["rstd2"] = sb("ln_rstd2", [128, TT])
    ln["sq"] = sb("ln_sq", [128, 2*1024])
    ln["xln"] = sb("ln_xln", [128, 2*1024], BF16)
    ln["eps"] = b["eps"]
    b["ln"] = ln
    b["rw"] = sb("rw", [128, KC*32], BF16)
    b["sgw"] = sb("sgw", [128, KC], BF16)
    for n in ("mx", "nmx", "sig", "gate2", "s4", "s8", "r2"):
        b[n] = sb(n, [128, TT])
    for n in ("expb", "cur", "wbuf", "combine"):
        b[n] = sb(n, [128, TT*32])
    b["mb"] = sb("mb", [128, 4*TT])
    b["silu_sb"] = sb("silu_sb", [128, 2*512], BF16)
    return b


def alloc_ffn_shared(nc, ctx, b, tag):
    sb = lambda n, shape, dt=BF16: ctx.enter_context(nc.sbuf_tensor(n + tag, shape, dt))
    b["shg"] = sb("shg", [128, KC*1024])
    b["shu"] = sb("shu", [128, KC*1024])
    b["shd"] = sb("shd", [128, KC*1024])
    b["actsh"] = sb("actsh", [128, KC*NTOK])


def alloc_ffn_experts(nc, ctx, b, tag):
    sb = lambda n, shape, dt=BF16: ctx.enter_context(nc.sbuf_tensor(n + tag, shape, dt))
    b["wg"] = sb("wg", [128, 2*KC*EI])
    b["wu"] = sb("wu", [128, 2*KC*EI])
    b["wd"] = sb("wd", [128, 2*EIC*1024])
    b["act_sb"] = sb("act_sb", [128, 2*EIC*512])
    b["ew_gpd"] = [None]*NE


def emit_attention(p, nc, dr, ident, v_agin):
    GROUPS = [[0, 1], [2, 3], [4, 5], [6, 7]]
    with ExitStack() as actx:
        k_aug = actx.enter_context(nc.sbuf_tensor("k_aug", [128, NH*T], BF16))
        qT = actx.enter_context(nc.sbuf_tensor("qT", [128, 4*T], BF16))
        v_sb = actx.enter_context(nc.sbuf_tensor("v_sb", [128, 32*520], BF16))
        p.wait("gp", "SPD", v_agin)
        p.op("gp", lambda e: e.collective_compute(
            "AllGather", ALU.bypass, replica_groups=GROUPS,
            ins=[dr["ag_in"].ap().opt()], outs=[dr["ag_out"].ap().opt()]),
            inc="AGS", k=1)
        p.wait("dv", "TE", p.cnt["TE"])
        p.wait("dv", "AC", p.cnt["AC"])
        v_ones = p.op("dv", lambda e: e.memset(v_sb[:], 1.0), inc="DV")
        with ExitStack() as qctx:
            xch = qctx.enter_context(nc.sbuf_tensor("xch", [128, 2*KC*512], BF16))
            wqkv = qctx.enter_context(nc.sbuf_tensor("wqkv", [128, KC*1536], BF16))
            p.fence_dma("gp")
            p.op("gp", lambda e: e.dma_start(
                out=wqkv.ap().rearrange("p (c n) -> p c n", c=KC),
                in_=r128(dr["wqkvT"].ap())), inc="GPD", k=16)
            v_wqkv = p.cnt["GPD"]
            with ExitStack() as pctx:
                qkvps = [pctx.enter_context(nc.psum_tensor(f"qkvps{i}", [128, 512], F32))
                         for i in range(2)]
                p.fence_dma("sp")
                p.wait("sp", "AGS", 1)
                p.fence_te()
                p.wait("te", "GPD", v_wqkv)
                agv = dr["ag_out"].ap().rearrange("(r c p) t -> p r c t", p=128, c=KC)
                ps_free = {0: None, 1: None}
                chunk_reader_te = {0: None, 1: None}
                pp = [0]

                def grp(slx, lhs_fn, rhs_fn, evac_fn):
                    pi = pp[0] % 2
                    pp[0] += 1
                    ps = qkvps[pi]
                    if ps_free[pi] is not None:
                        p.wait("te", "DV", ps_free[pi])
                    for kc in range(KC):
                        p.op("te", lambda e, kc=kc, ps=ps: e.matmul(
                            ps[:], lhs_fn(kc), rhs_fn(kc),
                            start=(kc == 0), stop=(kc == KC - 1)),
                            inc="TE" if kc == KC - 1 else None)
                    p.wait("dv", "TE", p.cnt["TE"])
                    ps_free[pi] = evac_fn(ps)

                def xch_rhs(slx):
                    return lambda kc, slx=slx: xch[:, slx*KC*512 + kc*512: slx*KC*512 + (kc+1)*512]

                for tbg in range(QB):
                    slx = tbg % 2
                    if chunk_reader_te[slx] is not None:
                        p.wait("sp", "TE", chunk_reader_te[slx])
                    p.op("sp", lambda e, tbg=tbg, slx=slx: e.dma_start(
                        out=xch[:, slx*KC*512:(slx+1)*KC*512].rearrange(
                            "p (c n) -> p c n", c=KC),
                        in_=agv[:, tbg // TB, :, (tbg % TB)*512:(tbg % TB + 1)*512]),
                        inc="SPD", k=16)
                    p.wait("te", "SPD", p.cnt["SPD"])
                    for p4 in range(4):
                        grp(slx,
                            lambda kc, p4=p4: wqkv[:, kc*1536 + p4*128: kc*1536 + (p4+1)*128],
                            xch_rhs(slx),
                            lambda ps, p4=p4, tbg=tbg: p.op(
                                "dv", lambda e, ps=ps, p4=p4, tbg=tbg: e.tensor_scalar(
                                    qT[:, p4*T + tbg*512: p4*T + (tbg+1)*512],
                                    ps[:], 0.125, None, ALU.mult), inc="DV"))
                    for p4 in range(4):
                        def kev(ps, p4=p4, tbg=tbg):
                            p.op("dv", lambda e, ps=ps, p4=p4, tbg=tbg: e.tensor_copy(
                                k_aug[0:64, (2*p4)*T + tbg*512: (2*p4)*T + (tbg+1)*512],
                                ps[0:64, :]))
                            return p.op("dv", lambda e, ps=ps, p4=p4, tbg=tbg: e.tensor_copy(
                                k_aug[64:128, (2*p4+1)*T + tbg*512: (2*p4+1)*T + (tbg+1)*512],
                                ps[64:128, :]), inc="DV")
                        grp(slx,
                            lambda kc, p4=p4: wqkv[:, kc*1536 + 512 + p4*128: kc*1536 + 512 + (p4+1)*128],
                            xch_rhs(slx),
                            kev)
                    for ts4 in range(4):
                        kt = tbg*4 + ts4
                        def vev(ps, kt=kt):
                            p.wait("dv", "DV", v_ones)
                            return p.op("dv", lambda e, ps=ps, kt=kt: e.tensor_copy(
                                v_sb[:, kt*520: kt*520 + 520].rearrange(
                                    "p (h x) -> p h x", h=NH)[:, :, 0:64],
                                ps[:].rearrange("p (h x) -> p h x", h=NH)), inc="DV")
                        grp(slx,
                            lambda kc, ts4=ts4, slx=slx: xch[:, slx*KC*512 + kc*512 + ts4*128: slx*KC*512 + kc*512 + (ts4+1)*128],
                            lambda kc: wqkv[:, kc*1536 + 1024: kc*1536 + 1536],
                            vev)
                    chunk_reader_te[slx] = p.cnt["TE"]
        with ExitStack() as psictx:
            cs1 = psictx.enter_context(nc.sbuf_tensor("cs1sb", [128, T], BF16))
            cs2 = psictx.enter_context(nc.sbuf_tensor("cs2sb", [128, T], BF16))
            tpsi = psictx.enter_context(nc.sbuf_tensor("tpsi", [128, T], BF16))
            a4 = psictx.enter_context(nc.sbuf_tensor("a4sb", [128, NH], F32))
            b4 = psictx.enter_context(nc.sbuf_tensor("b4sb", [128, NH], F32))
            a2 = psictx.enter_context(nc.sbuf_tensor("a2sb", [128, NH], F32))
            sgn = psictx.enter_context(nc.sbuf_tensor("sgnsb", [128, 1], F32))
            p.fence_dma("sp")
            for nm, buf in (("cs1", cs1), ("cs2", cs2), ("a4", a4), ("b4", b4),
                            ("sgnv", sgn)):
                p.op("sp", lambda e, nm=nm, buf=buf: e.dma_start(
                    out=buf[:], in_=dr[nm][:, :]), inc="SPD", k=16)
            v_csld = p.cnt["SPD"]
            p.wait("dv", "SPD", v_csld)
            v_a2 = p.op("dv", lambda e: e.tensor_scalar(
                a2[:], b4[:], sgn[:, 0:1], None, ALU.mult), inc="DV")
            p.wait("dv", "DV", v_a2)
            for h in range(NH):
                Bb = (1 - (h % 2)) * 64
                v_t = p.op("dv", lambda e, h=h: e.tensor_scalar(
                    tpsi[:], cs1[:], a4[:, h:h+1], None, ALU.mult), inc="DV")
                p.wait("dv", "DV", v_t)
                p.op("dv", lambda e, h=h, Bb=Bb: e.scalar_tensor_tensor(
                    k_aug[Bb:Bb+64, h*T:(h+1)*T], cs2[Bb:Bb+64, :],
                    a2[Bb:Bb+64, h:h+1], tpsi[Bb:Bb+64, :], ALU.mult, ALU.add),
                    inc="DV")
            v_psi = p.cnt["DV"]
        with ExitStack() as fctx:
            phi2 = fctx.enter_context(nc.sbuf_tensor("phi2sb", [128, T], BF16))
            masks = fctx.enter_context(nc.sbuf_tensor("maskssb", [128, 4*512], BF16))
            wout = fctx.enter_context(nc.sbuf_tensor("woutsb", [128, 4*1024], BF16))
            q_aug = fctx.enter_context(nc.sbuf_tensor("q_aug", [128, 3*512], BF16))
            P_sb = fctx.enter_context(nc.sbuf_tensor("P_sb", [128, 3*512], BF16))
            O_sb = fctx.enter_context(nc.sbuf_tensor("O_sb", [128, 4*512], BF16))
            OT_sb = fctx.enter_context(nc.sbuf_tensor("OT_sb", [128, 4*128], BF16))
            r_sb = fctx.enter_context(nc.sbuf_tensor("r_sb", [128, 4], F32))
            yst = fctx.enter_context(nc.sbuf_tensor("yst", [128, 2*1024], F32))
            p.fence_dma("sp")
            for nm, buf in (("phi2", phi2), ("masks", masks)):
                p.op("sp", lambda e, nm=nm, buf=buf: e.dma_start(
                    out=buf[:], in_=dr[nm][:, :]), inc="SPD", k=16)
            v_fl = p.cnt["SPD"]
            p.fence_dma("gp")
            p.op("gp", lambda e: e.dma_start(
                out=wout.ap().rearrange("p (c n) -> p c n", c=4),
                in_=r128(dr["woutT"].ap())), inc="GPD", k=16)
            v_wout = p.cnt["GPD"]
            with ExitStack() as pctx:
                Sps = [pctx.enter_context(nc.psum_tensor(f"Sps{i}", [128, 512], F32))
                       for i in range(3)]
                Ops = [pctx.enter_context(nc.psum_tensor(f"Ops{i}", [128, 260], F32))
                       for i in range(2)]
                tpo = pctx.enter_context(nc.psum_tensor("tpo", [128, 128], BF16))
                yps = pctx.enter_context(nc.psum_tensor("yps", [128, 1024], F32))
                p.fence_te()
                p.wait("te", "DV", v_psi)
                p.wait("te", "SPD", v_fl)
                p.wait("te", "GPD", v_wout)
                q_free_te = {0: None, 1: None, 2: None}
                S_free_ac = {0: None, 1: None, 2: None}
                P_free_te = {0: None, 1: None, 2: None}
                P_free_dv = {0: None, 1: None, 2: None}
                O_free_dv = {0: None, 1: None}
                osb_reader_te = None
                yst_reader_spd = {0: None, 1: None}
                v_qa = {}

                def emit_qa(qb, h):
                    qsl = h % 3
                    pi = h % 2
                    if q_free_te[qsl] is not None:
                        p.wait("dv", "TE", q_free_te[qsl])
                    p.op("dv", lambda e, h=h, qb=qb, qsl=qsl, pi=pi: e.tensor_copy(
                        q_aug[pi*64:(pi+1)*64, qsl*512:(qsl+1)*512],
                        qT[pi*64:(pi+1)*64, (h//2)*T + qb*512: (h//2)*T + (qb+1)*512]))
                    v_qa[(qb, h)] = p.op("dv", lambda e, qb=qb, qsl=qsl, pi=pi: e.tensor_copy(
                        q_aug[(1-pi)*64:(2-pi)*64, qsl*512:(qsl+1)*512],
                        phi2[(1-pi)*64:(2-pi)*64, qb*512:(qb+1)*512]), inc="DV")

                for qb in range(QB):
                    if osb_reader_te is not None:
                        p.wait("dv", "TE", osb_reader_te)
                    emit_qa(qb, 0)
                    emit_qa(qb, 1)
                    for h in range(NH):
                        qsl = h % 3
                        ob = h % 2
                        if h + 2 < NH:
                            emit_qa(qb, h + 2)
                        p.wait("te", "DV", v_qa[(qb, h)])
                        if O_free_dv[ob] is not None:
                            p.wait("te", "DV", O_free_dv[ob])
                        nkt = qb*4 + 4
                        for kt in range(nkt):
                            ssl = kt % 3
                            diag = kt >= qb*4
                            if S_free_ac[ssl] is not None:
                                p.wait("te", "AC", S_free_ac[ssl])
                            v_s = p.op("te", lambda e, h=h, kt=kt, ssl=ssl, qsl=qsl: e.matmul(
                                Sps[ssl][:], k_aug[:, h*T + kt*128: h*T + (kt+1)*128],
                                q_aug[:, qsl*512:(qsl+1)*512],
                                start=True, stop=True), inc="TE")
                            p.wait("ac", "TE", v_s)
                            if P_free_te[ssl] is not None:
                                p.wait("ac", "TE", P_free_te[ssl])
                            if P_free_dv[ssl] is not None:
                                p.wait("ac", "DV", P_free_dv[ssl])
                            v_p = p.op("ac", lambda e, ssl=ssl: e.activation(
                                P_sb[:, ssl*512:(ssl+1)*512], Sps[ssl][:], AF.Exp),
                                inc="AC")
                            S_free_ac[ssl] = v_p
                            if diag:
                                d = kt - qb*4
                                p.wait("dv", "AC", v_p)
                                v_m = p.op("dv", lambda e, ssl=ssl, d=d: e.tensor_tensor(
                                    P_sb[:, ssl*512:(ssl+1)*512],
                                    P_sb[:, ssl*512:(ssl+1)*512],
                                    masks[:, d*512:(d+1)*512], ALU.mult), inc="DV")
                                P_free_dv[ssl] = v_m
                                p.wait("te", "DV", v_m)
                            else:
                                P_free_dv[ssl] = None
                                p.wait("te", "AC", v_p)
                            for qs in range(4):
                                last = (kt == nkt - 1 and qs == 3)
                                p.op("te", lambda e, kt=kt, qs=qs, ssl=ssl, ob=ob, h=h: e.matmul(
                                    Ops[ob][:, qs*65:(qs+1)*65],
                                    P_sb[:, ssl*512 + qs*128: ssl*512 + (qs+1)*128],
                                    v_sb[:, kt*520 + h*65: kt*520 + (h+1)*65],
                                    start=(kt == 0 and qs == 0), stop=last),
                                    inc="TE" if last else None)
                            P_free_te[ssl] = p.cnt["TE"]
                        q_free_te[qsl] = p.cnt["TE"]
                        v_av = p.cnt["TE"]
                        p.wait("dv", "TE", v_av)
                        for qs in range(4):
                            p.op("dv", lambda e, qs=qs, ob=ob: e.reciprocal(
                                r_sb[:, qs:qs+1], Ops[ob][:, qs*65 + 64: qs*65 + 65]),
                                inc="DV" if qs == 3 else None)
                        p.wait("dv", "DV", p.cnt["DV"])
                        for qs in range(4):
                            p.op("dv", lambda e, qs=qs, h=h, ob=ob: e.tensor_scalar(
                                O_sb[:, qs*512 + h*64: qs*512 + (h+1)*64],
                                Ops[ob][:, qs*65: qs*65 + 64],
                                r_sb[:, qs:qs+1], None, ALU.mult),
                                inc="DV" if qs == 3 else None)
                        O_free_dv[ob] = p.cnt["DV"]
                    v_osb = p.cnt["DV"]
                    p.wait("te", "DV", v_osb)
                    for qs in range(4):
                        ysl = qs % 2
                        for c4 in range(4):
                            v_t = p.op("te", lambda e, qs=qs, c4=c4: e.transpose(
                                tpo[:], O_sb[:, qs*512 + c4*128: qs*512 + (c4+1)*128],
                                ident[:]), inc="TE")
                            p.wait("dv", "TE", v_t)
                            v_otc = p.op("dv", lambda e, c4=c4: e.tensor_copy(
                                OT_sb[:, c4*128:(c4+1)*128], tpo[:]), inc="DV")
                            p.wait("te", "DV", v_otc)
                        for c4 in range(4):
                            for nh2 in range(2):
                                last = (c4 == 3 and nh2 == 1)
                                p.op("te", lambda e, c4=c4, nh2=nh2: e.matmul(
                                    yps[:, nh2*512:(nh2+1)*512],
                                    OT_sb[:, c4*128:(c4+1)*128],
                                    wout[:, c4*1024 + nh2*512: c4*1024 + (nh2+1)*512],
                                    start=(c4 == 0), stop=(c4 == 3)),
                                    inc="TE" if last else None)
                        v_y = p.cnt["TE"]
                        p.wait("dv", "TE", v_y)
                        if yst_reader_spd[ysl] is not None:
                            p.wait("dv", "SPD", yst_reader_spd[ysl])
                        v_ye = p.op("dv", lambda e, ysl=ysl: e.tensor_copy(
                            yst[:, ysl*1024:(ysl+1)*1024], yps[:]), inc="DV")
                        p.wait("sp", "DV", v_ye)
                        p.op("sp", lambda e, qb=qb, qs=qs, ysl=ysl: e.dma_start(
                            out=dr["y_in"].ap().rearrange("(t p) n -> p t n", p=128)[:, qb*4+qs],
                            in_=yst[:, ysl*1024:(ysl+1)*1024]), inc="SPD", k=16)
                        yst_reader_spd[ysl] = p.cnt["SPD"]
                    osb_reader_te = p.cnt["TE"]
                v_yall = p.cnt["SPD"]
        p.wait("gp", "SPD", v_yall)
        p.op("gp", lambda e: e.collective_compute(
            "ReduceScatter", ALU.add, replica_groups=GROUPS,
            ins=[dr["y_in"].ap().opt()], outs=[dr["y_out"].ap().opt()]),
            inc="RSS", k=1)


def build_program():
    nc = bass.Bass("TRN2", target_bir_lowering=False)
    dr = {}
    dr["x_own"] = nc.dram_tensor("x_own", [NTOK, D], F32, kind="ExternalInput")
    dr["router_w"] = nc.dram_tensor("router_w", [D, 32], F32, kind="ExternalInput")
    dr["sh_gate_w"] = nc.dram_tensor("sh_gate_w", [128, KC], F32, kind="ExternalInput")
    dr["sh_gate"] = nc.dram_tensor("sh_gate", [D, ISH], F32, kind="ExternalInput")
    dr["sh_up"] = nc.dram_tensor("sh_up", [D, ISH], F32, kind="ExternalInput")
    dr["sh_down"] = nc.dram_tensor("sh_down", [ISH, D], F32, kind="ExternalInput")
    dr["w_gate"] = nc.dram_tensor("w_gate", [NE, D, EI], F32, kind="ExternalInput")
    dr["w_up"] = nc.dram_tensor("w_up", [NE, D, EI], F32, kind="ExternalInput")
    dr["w_down"] = nc.dram_tensor("w_down", [NE, EI, D], F32, kind="ExternalInput")
    dr["wqkvT"] = nc.dram_tensor("wqkvT", [D, 1536], F32, kind="ExternalInput")
    dr["woutT"] = nc.dram_tensor("woutT", [512, D], F32, kind="ExternalInput")
    dr["ident"] = nc.dram_tensor("ident", [128, 128], BF16, kind="ExternalInput")
    dr["cs1"] = nc.dram_tensor("cs1", [128, T], BF16, kind="ExternalInput")
    dr["cs2"] = nc.dram_tensor("cs2", [128, T], BF16, kind="ExternalInput")
    dr["phi2"] = nc.dram_tensor("phi2", [128, T], BF16, kind="ExternalInput")
    dr["a4"] = nc.dram_tensor("a4", [128, NH], F32, kind="ExternalInput")
    dr["b4"] = nc.dram_tensor("b4", [128, NH], F32, kind="ExternalInput")
    dr["sgnv"] = nc.dram_tensor("sgnv", [128, 1], F32, kind="ExternalInput")
    dr["masks"] = nc.dram_tensor("masks", [128, 4*512], BF16, kind="ExternalInput")
    dr["out"] = nc.dram_tensor("out", [NTOK, D], F32, kind="ExternalOutput")
    dr["xspill"] = nc.dram_tensor("xspill", [NTOK, D], F32)
    dr["ag_in"] = nc.dram_tensor("ag_in", [D, NTOK], BF16)
    dr["ag_out"] = nc.dram_tensor("ag_out", [2*D, NTOK], BF16)
    dr["y_in"] = nc.dram_tensor("y_in", [T, D], F32)
    dr["y_out"] = nc.dram_tensor("y_out", [NTOK, D], F32)

    with ExitStack() as gctx:
        p = Prog(nc)
        gctx.enter_context(p.ctx)
        ident = gctx.enter_context(nc.sbuf_tensor("identsb", [128, 128], BF16))
        epscol = gctx.enter_context(nc.sbuf_tensor("epscol", [128, 1], F32))
        blk = gctx.enter_context(nc.Block())
        p.op("sp", lambda e: e.dma_start(out=ident[:], in_=dr["ident"][:, :]),
             inc="SPD", k=16)
        v_ident = p.cnt["SPD"]
        v_eps = p.op("dv", lambda e: e.memset(epscol[:], EPS), inc="DV")

        with ExitStack() as f1ctx:
            b = {"ident": ident, "eps": epscol}
            x1 = f1ctx.enter_context(nc.sbuf_tensor("x1", [128, TT*1024], F32))
            alloc_ffn_bufs(nc, f1ctx, b, "c0")
            b["x"] = x1
            p.op("sp", lambda e: e.dma_start(
                out=x1.ap().rearrange("p (t n) -> p t n", t=TT),
                in_=dr["x_own"].ap().rearrange("(t p) n -> p t n", p=128)),
                inc="SPD", k=16)
            v_x = p.cnt["SPD"]
            p.wait("te", "SPD", v_ident)
            p.wait("dv", "SPD", v_x)
            p.wait("ac", "SPD", v_x)
            if True:
                info1 = emit_ffn(p, b, dr, x1, 0)
                with ExitStack() as ps0:
                    tpA = ps0.enter_context(nc.psum_tensor("tpA_l2", [128, 128], BF16))
                    tpB = ps0.enter_context(nc.psum_tensor("tpB_l2", [128, 128], BF16))
                    p.fence_te()
                    v_xln2 = emit_ln(p, x1, b["ln"], b["xlnT"], ident, tpA, tpB)
                p.wait("sp", "DV", v_xln2)
                p.op("sp", lambda e: e.dma_start(
                    out=dr["xspill"].ap().rearrange("(t p) n -> p t n", p=128),
                    in_=x1.ap().rearrange("p (t n) -> p t n", t=TT)),
                    inc="SPD", k=16)
                p.op("sp", lambda e, _t=b["xlnT"]: e.dma_start(
                    out=r128(dr["ag_in"].ap()),
                    in_=_t.ap().rearrange("p (c n) -> p c n", c=KC)),
                    inc="SPD", k=16)
                v_agin = p.cnt["SPD"]
        emit_attention(p, nc, dr, ident, v_agin)
        with ExitStack() as f2ctx:
            b = {"ident": ident, "eps": epscol}
            x2 = f2ctx.enter_context(nc.sbuf_tensor("x2", [128, TT*1024], F32))
            alloc_ffn_bufs(nc, f2ctx, b, "c1")
            b["x"] = x2
            p.fence_dma("sp")
            p.wait("sp", "RSS", 1)
            p.op("sp", lambda e: e.dma_start(
                out=x2.ap().rearrange("p (t n) -> p t n", t=TT),
                in_=dr["y_out"].ap().rearrange("(t p) n -> p t n", p=128)),
                inc="SPD", k=16)
            v_y = p.cnt["SPD"]
            with nc.sbuf_tensor("stg", [128, 2*1024], F32) as stg:
                for tt in range(TT):
                    sl = tt % 2
                    p.op("sp", lambda e, tt=tt, sl=sl, stg=stg: e.dma_start(
                        out=stg[:, sl*1024:(sl+1)*1024],
                        in_=dr["xspill"].ap().rearrange("(t p) n -> p t n", p=128)[:, tt]),
                        inc="SPD", k=16)
                    v_s = p.cnt["SPD"]
                    p.wait("dv", "SPD", max(v_s, v_y))
                    v_add = p.op("dv", lambda e, tt=tt, sl=sl, stg=stg: e.tensor_tensor(
                        x2[:, tt*1024:(tt+1)*1024], x2[:, tt*1024:(tt+1)*1024],
                        stg[:, sl*1024:(sl+1)*1024], ALU.add), inc="DV")
                    p.wait("sp", "DV", v_add)
            p.wait("dv", "DV", p.cnt["DV"])
            p.fence_dma("gp")
            p.op("gp", lambda e: e.dma_start(
                out=b["rw"].ap().rearrange("p (c n) -> p c n", c=KC),
                in_=r128(dr["router_w"].ap())), inc="GPD", k=16)
            p.op("gp", lambda e: e.dma_start(
                out=b["sgw"].ap(), in_=dr["sh_gate_w"][:, :]), inc="GPD", k=16)
            b["v_wsmall"] = p.cnt["GPD"]
            info2 = emit_ffn(p, b, dr, x2, 1)
            p.wait("sp", "DV", info2["x_dv"])
            p.op("sp", lambda e: e.dma_start(
                out=dr["out"].ap().rearrange("(t p) n -> p t n", p=128),
                in_=x2.ap().rearrange("p (t n) -> p t n", t=TT)),
                inc="SPD", k=16)
            p.wait("sp", "SPD", p.cnt["SPD"])
        p.flush(blk)
    return nc


def make_core_inputs(core, full):
    b, hg = core // 2, core % 2
    X = np.asarray(full["L0_attn_x"], np.float32).reshape(4*T, D)
    rows = slice(b*T + hg*NTOK, b*T + (hg+1)*NTOK)
    Wqkv = np.asarray(full["L0_attn_Wqkv"], np.float32)
    heads = range(hg*NH, (hg+1)*NH)
    qrows = np.concatenate([np.arange(h*64, (h+1)*64) for h in heads])
    sel = np.concatenate([qrows, D + qrows, 2*D + qrows])
    wqkvT = np.ascontiguousarray(Wqkv[sel].T)
    Wout = np.asarray(full["L0_attn_Wout"], np.float32)
    woutT = np.ascontiguousarray(Wout[:, hg*512:(hg+1)*512].T)
    alpha = np.asarray(full["L0_attn_alpha"], np.float32)[hg*NH:(hg+1)*NH]
    beta = np.asarray(full["L0_attn_beta"], np.float32)[hg*NH:(hg+1)*NH]
    a4 = np.ascontiguousarray(np.tile(alpha.T, (4, 1)))
    b4 = np.ascontiguousarray(np.tile(beta.T, (4, 1)))
    consts = make_consts()
    return {
        "x_own": np.ascontiguousarray(X[rows]),
        "router_w": np.asarray(full["L0_ffn_router_w"], np.float32),
        "sh_gate_w": np.ascontiguousarray(
            np.asarray(full["L0_ffn_sh_gate_w"], np.float32).reshape(KC, 128).T),
        "sh_gate": np.asarray(full["L0_ffn_sh_gate"], np.float32),
        "sh_up": np.asarray(full["L0_ffn_sh_up"], np.float32),
        "sh_down": np.asarray(full["L0_ffn_sh_down"], np.float32),
        "w_gate": np.asarray(full["L0_ffn_w_gate"], np.float32),
        "w_up": np.asarray(full["L0_ffn_w_up"], np.float32),
        "w_down": np.asarray(full["L0_ffn_w_down"], np.float32),
        "wqkvT": wqkvT, "woutT": woutT, "a4": a4, "b4": b4,
        **consts,
    }


_CONSTS = None

def make_consts():
    global _CONSTS
    if _CONSTS is not None:
        return _CONSTS
    R = 32
    om = (10000.0 ** (-2.0 * np.arange(R, dtype=np.float64) / 64.0))
    k = np.arange(T, dtype=np.float64)
    C = np.cos(om[:, None] * k[None, :])
    S = np.sin(om[:, None] * k[None, :])
    cs1 = np.concatenate([C, S, C, S], axis=0).astype(BF)
    cs2 = np.concatenate([S, C, S, C], axis=0).astype(BF)
    phi = np.concatenate([C, S], axis=0)
    phi2 = np.concatenate([phi, phi], axis=0).astype(BF)
    sgn = np.concatenate([-np.ones(32), np.ones(32), -np.ones(32),
                          np.ones(32)]).astype(np.float32)[:, None]
    i = np.arange(128)[:, None]
    j = np.arange(512)[None, :]
    masks = np.concatenate(
        [(j >= i + d*128).astype(np.float32) for d in range(4)], axis=1).astype(BF)
    _CONSTS = {
        "ident": np.eye(128).astype(BF),
        "cs1": np.ascontiguousarray(cs1), "cs2": np.ascontiguousarray(cs2),
        "phi2": np.ascontiguousarray(phi2), "sgnv": sgn, "masks": masks,
    }
    return _CONSTS


def kernel(**inputs):
    from concourse.bass_utils import run_bass_kernel_spmd
    nc = build_program()
    in_maps = [make_core_inputs(c, inputs) for c in range(8)]
    res = run_bass_kernel_spmd(nc, in_maps, core_ids=list(range(8)))
    out = np.empty((4 * T, D), dtype=np.float32)
    for c in range(8):
        b, hg = c // 2, c % 2
        out[b*T + hg*NTOK: b*T + (hg+1)*NTOK] = res.results[c]["out"]
    return out.reshape(4, T, D)


# version 04
# speedup vs baseline: 1.0903x; 1.0054x over previous
import numpy as np
import ml_dtypes
from contextlib import ExitStack

import concourse.bass as bass
import concourse.mybir as mybir

F32 = mybir.dt.float32
BF16 = mybir.dt.bfloat16
AF = mybir.ActivationFunctionType
ALU = mybir.AluOpType
AXX = mybir.AxisListType.X

D = 1024
KC = 8
NTOK = 2048
TT = 16
TB = 4
NE = 32
EI = 768
EIC = 6
ISH = 2048
T = 4096
NH = 8
QB = 8
EPS = 1e-5

ENGS = ("te", "dv", "ac", "sp", "gp")
BF = ml_dtypes.bfloat16


class Prog:
    def __init__(self, nc):
        self.nc = nc
        self.q = {e: [] for e in ENGS}
        self.sems = {}
        self.cnt = {}
        self.ctx = ExitStack()
        for s in ("TE", "DV", "AC", "GPD", "SPD", "AGS", "RSS"):
            self.sems[s] = self.ctx.enter_context(nc.semaphore(s))
            self.cnt[s] = 0

    def op(self, eng, fn, inc=None, k=1):
        if inc is None:
            self.q[eng].append(lambda e, fn=fn: fn(e))
            return None
        self.cnt[inc] += k
        s = self.sems[inc]
        self.q[eng].append(lambda e, fn=fn, s=s, k=k: fn(e).then_inc(s, k))
        return self.cnt[inc]

    def wait(self, eng, sem, val):
        if val is None or val <= 0:
            return
        self.q[eng].append(lambda e, s=self.sems[sem], v=val: e.wait_ge(s, v))

    def fence_te(self):
        self.wait("te", "DV", self.cnt["DV"])
        self.wait("te", "AC", self.cnt["AC"])

    def fence_dma(self, eng):
        for s in ("TE", "DV", "AC"):
            self.wait(eng, s, self.cnt[s])

    def flush(self, block):
        def runner(fns):
            def go(e):
                for f in fns:
                    f(e)
            return go
        block.tensor(runner(self.q["te"]))
        block.vector(runner(self.q["dv"]))
        block.scalar(runner(self.q["ac"]))
        block.sync(runner(self.q["sp"]))
        block.gpsimd(runner(self.q["gp"]))


def r128(dram_ap):
    return dram_ap.rearrange("(c p) n -> p c n", p=128)


def emit_ln(p, x, ln, xlnT, ident, tpA, tpB, ln_fence=True):
    if ln_fence:
        p.wait("ac", "DV", p.cnt["DV"])
    for tt in range(TT):
        p.op("dv", lambda e, tt=tt: e.reduce_sum(
            ln["sums"][:, tt:tt+1], x[:, tt*1024:(tt+1)*1024], axis=AXX),
            inc="DV" if tt == TT - 1 else None)
    p.wait("dv", "DV", p.cnt["DV"])
    v_nm = p.op("dv", lambda e: e.tensor_scalar(
        ln["nm"][:, 0:TT], ln["sums"][:, 0:TT], -1.0 / D, None, ALU.mult), inc="DV")
    p.wait("ac", "DV", v_nm)
    v_red = {}
    for tt in range(TT):
        slot = tt % 2
        if tt >= 2:
            p.wait("ac", "DV", v_red[tt - 2])
        v_sq = p.op("ac", lambda e, tt=tt, slot=slot: e.activation(
            ln["sq"][:, slot*1024:(slot+1)*1024], x[:, tt*1024:(tt+1)*1024],
            AF.Square, bias=ln["nm"][:, tt:tt+1]), inc="AC")
        p.wait("dv", "AC", v_sq)
        v_red[tt] = p.op("dv", lambda e, tt=tt, slot=slot: e.reduce_sum(
            ln["vsums"][:, tt:tt+1], ln["sq"][:, slot*1024:(slot+1)*1024],
            axis=AXX), inc="DV")
    p.wait("ac", "DV", v_red[TT - 1])
    v_rstd = p.op("ac", lambda e: e.activation(
        ln["rstd"][:, 0:TT], ln["vsums"][:, 0:TT], AF.Sqrt,
        scale=1.0 / D, bias=ln["eps"][:, 0:1]), inc="AC")
    p.wait("dv", "AC", v_rstd)
    v_rcp = p.op("dv", lambda e: e.reciprocal(
        ln["rstd2"][:, 0:TT], ln["rstd"][:, 0:TT]), inc="DV")
    p.wait("dv", "DV", v_rcp)
    v_cp = {}
    v_tr_tile = {}
    for tt in range(TT):
        slot = tt % 2
        if tt >= 2:
            p.wait("dv", "TE", v_tr_tile[tt - 2])
        v_xln = p.op("dv", lambda e, tt=tt, slot=slot: e.tensor_scalar(
            ln["xln"][:, slot*1024:(slot+1)*1024], x[:, tt*1024:(tt+1)*1024],
            ln["nm"][:, tt:tt+1], ln["rstd2"][:, tt:tt+1], ALU.add, ALU.mult),
            inc="DV")
        p.wait("te", "DV", v_xln)
        for kc in range(KC):
            tp = tpA if kc % 2 == 0 else tpB
            prev = (tt, kc - 2) if kc >= 2 else ((tt - 1, KC + kc - 2) if tt > 0 else None)
            if prev is not None:
                p.wait("te", "DV", v_cp[prev])
            v_t = p.op("te", lambda e, tt=tt, kc=kc, slot=slot, tp=tp: e.transpose(
                tp[:], ln["xln"][:, slot*1024 + kc*128: slot*1024 + (kc+1)*128],
                ident[:]), inc="TE")
            p.wait("dv", "TE", v_t)
            v_cp[(tt, kc)] = p.op("dv", lambda e, tt=tt, kc=kc, tp=tp: e.tensor_copy(
                xlnT[:, kc*NTOK + tt*128: kc*NTOK + (tt+1)*128], tp[:]), inc="DV")
        v_tr_tile[tt] = p.cnt["TE"]
    return p.cnt["DV"]


def emit_router(p, b, v_xlnT_dv, v_w_gpd):
    nc = p.nc
    xlnT, rw, sgw = b["xlnT"], b["rw"], b["sgw"]
    with ExitStack() as ps:
        import itertools as _it
        _rid = getattr(emit_router, "_n", 0); emit_router._n = _rid + 1
        lg = ps.enter_context(nc.psum_tensor(f"lg{_rid}", [128, TT*32], F32))
        sg = ps.enter_context(nc.psum_tensor(f"sg{_rid}", [128, TT], F32))
        p.fence_te()
        p.wait("te", "DV", v_xlnT_dv)
        p.wait("te", "GPD", v_w_gpd)
        for tt in range(TT):
            for kc in range(KC):
                p.op("te", lambda e, tt=tt, kc=kc: e.matmul(
                    lg[:, tt*32:(tt+1)*32],
                    xlnT[:, kc*NTOK + tt*128: kc*NTOK + (tt+1)*128],
                    rw[:, kc*32:(kc+1)*32],
                    start=(tt == 0 and kc == 0), stop=(tt == TT-1 and kc == KC-1)))
        for tt in range(TT):
            for kc in range(KC):
                last = (tt == TT-1 and kc == KC-1)
                p.op("te", lambda e, tt=tt, kc=kc: e.matmul(
                    sg[:, tt:tt+1],
                    xlnT[:, kc*NTOK + tt*128: kc*NTOK + (tt+1)*128],
                    sgw[:, kc:kc+1],
                    start=(tt == 0 and kc == 0), stop=last),
                    inc="TE" if last else None)
        v_te = p.cnt["TE"]
        p.wait("dv", "TE", v_te)
        for tt in range(TT):
            p.op("dv", lambda e, tt=tt: e.reduce_max(
                b["mx"][:, tt:tt+1], lg[:, tt*32:(tt+1)*32], axis=AXX),
                inc="DV" if tt == TT - 1 else None)
        p.wait("dv", "DV", p.cnt["DV"])
        v_nmx = p.op("dv", lambda e: e.tensor_scalar(
            b["nmx"][:, 0:TT], b["mx"][:, 0:TT], -1.0, None, ALU.mult), inc="DV")
        p.wait("ac", "DV", v_nmx)
        for tt in range(TT):
            p.op("ac", lambda e, tt=tt: e.activation(
                b["expb"][:, tt*32:(tt+1)*32], lg[:, tt*32:(tt+1)*32], AF.Exp,
                bias=b["nmx"][:, tt:tt+1]))
        v_sig = p.op("ac", lambda e: e.activation(
            b["sig"][:, 0:TT], sg[:, 0:TT], AF.Sigmoid), inc="AC")
        p.wait("dv", "AC", v_sig)
        v_g2 = p.op("dv", lambda e: e.tensor_scalar(
            b["gate2"][:, 0:TT], b["sig"][:, 0:TT], 0.5, None, ALU.mult), inc="DV")
        expb, cur, wbuf, mb, comb = b["expb"], b["cur"], b["wbuf"], b["mb"], b["combine"]
        src = expb
        for it in range(4):
            for tt in range(TT):
                p.op("dv", lambda e, tt=tt, it=it, src=src: e.reduce_max(
                    mb[:, it*TT + tt: it*TT + tt + 1], src[:, tt*32:(tt+1)*32],
                    axis=AXX), inc="DV" if tt == TT - 1 else None)
            p.wait("dv", "DV", p.cnt["DV"])
            if it < 3:
                for tt in range(TT):
                    p.op("dv", lambda e, tt=tt, it=it, src=src: e.scalar_tensor_tensor(
                        cur[:, tt*32:(tt+1)*32], src[:, tt*32:(tt+1)*32],
                        mb[:, it*TT + tt: it*TT + tt + 1], src[:, tt*32:(tt+1)*32],
                        ALU.is_lt, ALU.mult), inc="DV" if tt == TT - 1 else None)
                p.wait("dv", "DV", p.cnt["DV"])
                src = cur
        for tt in range(TT):
            p.op("dv", lambda e, tt=tt: e.scalar_tensor_tensor(
                wbuf[:, tt*32:(tt+1)*32], expb[:, tt*32:(tt+1)*32],
                mb[:, 3*TT + tt: 3*TT + tt + 1], expb[:, tt*32:(tt+1)*32],
                ALU.is_ge, ALU.mult), inc="DV" if tt == TT - 1 else None)
        p.wait("dv", "DV", p.cnt["DV"])
        for tt in range(TT):
            p.op("dv", lambda e, tt=tt: e.reduce_sum(
                b["s4"][:, tt:tt+1], wbuf[:, tt*32:(tt+1)*32], axis=AXX),
                inc="DV" if tt == TT - 1 else None)
        p.wait("dv", "DV", p.cnt["DV"])
        v = p.op("dv", lambda e: e.tensor_scalar(
            b["s8"][:, 0:TT], b["s4"][:, 0:TT], 2.0, None, ALU.mult), inc="DV")
        p.wait("dv", "DV", v)
        v = p.op("dv", lambda e: e.reciprocal(b["r2"][:, 0:TT], b["s8"][:, 0:TT]),
                 inc="DV")
        p.wait("dv", "DV", v)
        for tt in range(TT):
            p.op("dv", lambda e, tt=tt: e.tensor_scalar(
                comb[:, tt*32:(tt+1)*32], wbuf[:, tt*32:(tt+1)*32],
                b["r2"][:, tt:tt+1], None, ALU.mult),
                inc="DV" if tt == TT - 1 else None)
    return {"combine_dv": p.cnt["DV"], "gate2_dv": v_g2,
            "lg_reader_ac": v_sig, "lg_reader_dv": v_nmx}


def emit_gu_act(p, st, lw_g, lw_u, xlnT, gpsl, upsl, act_dst, silu_sb, tok_off):
    i = st.get("gu_rot", 0) & 1
    st["gu_rot"] = i ^ 1
    gps, ups = gpsl[i], upsl[i]
    busy = st.get(("gu_busy", i))
    if busy is not None:
        p.wait("te", "AC", busy[0])
        p.wait("te", "DV", busy[1])
    for kc in range(KC):
        p.op("te", lambda e, kc=kc, gps=gps: e.matmul(
            gps[:], lw_g(kc), xlnT[:, kc*NTOK + tok_off: kc*NTOK + tok_off + 512],
            start=(kc == 0), stop=(kc == KC - 1)),
            inc="TE" if kc == KC - 1 else None)
    v_g = p.cnt["TE"]
    for kc in range(KC):
        p.op("te", lambda e, kc=kc, ups=ups: e.matmul(
            ups[:], lw_u(kc), xlnT[:, kc*NTOK + tok_off: kc*NTOK + tok_off + 512],
            start=(kc == 0), stop=(kc == KC - 1)),
            inc="TE" if kc == KC - 1 else None)
    v_u = p.cnt["TE"]
    slot = st["silu_parity"]
    st["silu_parity"] ^= 1
    p.wait("ac", "TE", v_g)
    if st.get(("silu_free", slot)):
        p.wait("ac", "DV", st[("silu_free", slot)])
    v_silu = p.op("ac", lambda e, slot=slot, gps=gps: e.activation(
        silu_sb[:, slot*512:(slot+1)*512], gps[:], AF.Silu), inc="AC")
    p.wait("dv", "AC", v_silu)
    p.wait("dv", "TE", v_u)
    v_act = p.op("dv", lambda e, slot=slot, ups=ups: e.tensor_tensor(
        act_dst, silu_sb[:, slot*512:(slot+1)*512], ups[:], ALU.mult), inc="DV")
    st[("gu_busy", i)] = (v_silu, v_act)
    st[("silu_free", slot)] = v_act
    return v_act


def emit_ffn(p, b, drams, x, call):
    nc = p.nc
    xlnT = b["xlnT"]
    if call == 0:
        p.fence_dma("gp")
        p.op("gp", lambda e: e.dma_start(
            out=b["rw"].ap().rearrange("p (c n) -> p c n", c=KC),
            in_=r128(drams["router_w"].ap())), inc="GPD", k=16)
        p.op("gp", lambda e: e.dma_start(
            out=b["sgw"].ap(), in_=drams["sh_gate_w"][:, :]), inc="GPD", k=16)
        b["v_wsmall"] = p.cnt["GPD"]
    def load_sh(hf):
        for nm, buf in (("sh_gate", b["shg"]), ("sh_up", b["shu"])):
            p.op("gp", lambda e, nm=nm, buf=buf, hf=hf: e.dma_start(
                out=buf.ap().rearrange("p (c n) -> p c n", c=KC),
                in_=r128(drams[nm][:, hf*1024:(hf+1)*1024])), inc="GPD", k=16)
        vA = p.cnt["GPD"]
        p.op("gp", lambda e, hf=hf: e.dma_start(
            out=b["shd"].ap().rearrange("p (c n) -> p c n", c=KC),
            in_=r128(drams["sh_down"][hf*1024:(hf+1)*1024, :])), inc="GPD", k=16)
        return (vA, p.cnt["GPD"])

    def load_expert(ei, sl):
        for nm, buf, cs, w in (("w_gate", b["wg"], KC, EI),
                               ("w_up", b["wu"], KC, EI),
                               ("w_down", b["wd"], EIC, 1024)):
            p.op("gp", lambda e, nm=nm, buf=buf, ei=ei, sl=sl, cs=cs, w=w: e.dma_start(
                out=buf[:, sl*cs*w:(sl+1)*cs*w].rearrange("p (c n) -> p c n", c=cs),
                in_=r128(drams[nm][ei])), inc="GPD", k=16)
        return p.cnt["GPD"]

    with ExitStack() as ps0:
        tpA = ps0.enter_context(nc.psum_tensor(f"tpA{call}", [128, 128], BF16))
        tpB = ps0.enter_context(nc.psum_tensor(f"tpB{call}", [128, 128], BF16))
        p.fence_te()
        v_xlnT = emit_ln(p, x, b["ln"], xlnT, b["ident"], tpA, tpB)
    rinfo = emit_router(p, b, v_xlnT, b["v_wsmall"])
    x_writable_dv = max(rinfo["combine_dv"], rinfo["gate2_dv"])

    with ExitStack() as ps:
        gpsl = [ps.enter_context(nc.psum_tensor(f"gps{call}_{i}", [128, 512], F32))
                for i in range(2)]
        upsl = [ps.enter_context(nc.psum_tensor(f"ups{call}_{i}", [128, 512], F32))
                for i in range(2)]
        routed = [ps.enter_context(nc.psum_tensor(f"rt{call}_{i}", [128, 1024], F32))
                  for i in range(2)]
        st = {"silu_parity": 0}
        pair_free_dv = {0: None, 1: None}
        ts_ctr = 0
        first_mm_fenced = [False]

        def down_group(lhs_fn, rhs_fn, tt, scalar_ap):
            nonlocal ts_ctr
            pr = ts_ctr % 2
            ts_ctr += 1
            if pair_free_dv[pr] is not None:
                p.wait("te", "DV", pair_free_dv[pr])
            nic = lhs_fn(None)
            for ic in range(nic):
                for nh2 in range(2):
                    last = (ic == nic - 1 and nh2 == 1)
                    p.op("te", lambda e, ic=ic, nh2=nh2, pr=pr: e.matmul(
                        routed[pr][:, nh2*512:(nh2+1)*512],
                        lhs_fn(ic), rhs_fn(ic, nh2),
                        start=(ic == 0), stop=(ic == nic - 1)),
                        inc="TE" if last else None)
            v_te = p.cnt["TE"]
            p.wait("dv", "TE", v_te)
            if not first_mm_fenced[0]:
                p.wait("dv", "DV", x_writable_dv)
                first_mm_fenced[0] = True
            pair_free_dv[pr] = p.op("dv", lambda e, tt=tt, pr=pr, s=scalar_ap: e.scalar_tensor_tensor(
                x[:, tt*1024:(tt+1)*1024], routed[pr][:], s,
                x[:, tt*1024:(tt+1)*1024], ALU.mult, ALU.add), inc="DV")

        with ExitStack() as shctx:
            alloc_ffn_shared(nc, shctx, b, f"f{call}")
            shg, shu, shd, actsh = b["shg"], b["shu"], b["shd"], b["actsh"]
            sh_gpd = [load_sh(0)]
            for hf in range(2):
                p.fence_te()
                p.wait("te", "GPD", sh_gpd[hf][0])
                for icc in range(KC):
                    for tb in range(TB):
                        emit_gu_act(
                            p, st,
                            (lambda kc, icc=icc: shg[:, kc*1024 + icc*128: kc*1024 + (icc+1)*128]),
                            (lambda kc, icc=icc: shu[:, kc*1024 + icc*128: kc*1024 + (icc+1)*128]),
                            xlnT, gpsl, upsl,
                            actsh[:, icc*NTOK + tb*512: icc*NTOK + (tb+1)*512],
                            b["silu_sb"], tb*512)
                v_actsh_dv = p.cnt["DV"]
                p.wait("te", "DV", v_actsh_dv)
                p.wait("te", "GPD", sh_gpd[hf][1])
                for tt in range(TT):
                    down_group(
                        (lambda ic, tt=tt: KC if ic is None else
                         actsh[:, ic*NTOK + tt*128: ic*NTOK + (tt+1)*128]),
                        (lambda ic, nh2: shd[:, ic*1024 + nh2*512: ic*1024 + (nh2+1)*512]),
                        tt, b["gate2"][:, tt:tt+1])
                if hf == 0:
                    p.wait("gp", "TE", p.cnt["TE"])
                    sh_gpd.append(load_sh(1))
            v_te_sh, v_dv_sh = p.cnt["TE"], p.cnt["DV"]

        alloc_ffn_experts(nc, ps, b, f"f{call}")
        wg, wu, wd, act_sb = b["wg"], b["wu"], b["wd"], b["act_sb"]
        p.wait("gp", "TE", v_te_sh)
        p.wait("gp", "DV", v_dv_sh)
        for ei in range(2):
            b["ew_gpd"][ei] = load_expert(ei, ei % 2)
        act_slot_reader_te = {0: None, 1: None}
        for ei in range(NE):
            sl = ei % 2
            p.wait("te", "GPD", b["ew_gpd"][ei])
            for tb in range(TB):
                asl = tb % 2
                if act_slot_reader_te[asl] is not None:
                    p.wait("dv", "TE", act_slot_reader_te[asl])
                for ic in range(EIC):
                    emit_gu_act(
                        p, st,
                        (lambda kc, ic=ic, sl=sl: wg[:, sl*KC*EI + kc*EI + ic*128: sl*KC*EI + kc*EI + (ic+1)*128]),
                        (lambda kc, ic=ic, sl=sl: wu[:, sl*KC*EI + kc*EI + ic*128: sl*KC*EI + kc*EI + (ic+1)*128]),
                        xlnT, gpsl, upsl,
                        act_sb[:, asl*EIC*512 + ic*512: asl*EIC*512 + (ic+1)*512],
                        b["silu_sb"], tb*512)
                v_act_dv = p.cnt["DV"]
                p.wait("te", "DV", v_act_dv)
                for ts4 in range(4):
                    tt = tb*4 + ts4
                    down_group(
                        (lambda ic, ts4=ts4, asl=asl: EIC if ic is None else
                         act_sb[:, asl*EIC*512 + ic*512 + ts4*128: asl*EIC*512 + ic*512 + (ts4+1)*128]),
                        (lambda ic, nh2, sl=sl: wd[:, sl*EIC*1024 + ic*1024 + nh2*512: sl*EIC*1024 + ic*1024 + (nh2+1)*512]),
                        tt, b["combine"][:, tt*32 + ei: tt*32 + ei + 1])
                act_slot_reader_te[asl] = p.cnt["TE"]
            if ei + 2 < NE:
                p.wait("gp", "TE", p.cnt["TE"])
                b["ew_gpd"][ei + 2] = load_expert(ei + 2, (ei + 2) % 2)
    return {"x_dv": p.cnt["DV"]}


def alloc_ffn_bufs(nc, ctx, b, tag=""):
    sb = lambda n, shape, dt=F32: ctx.enter_context(nc.sbuf_tensor(n + tag, shape, dt))
    b["xlnT"] = sb("xlnT", [128, KC*NTOK], BF16)
    ln = {}
    ln["sums"] = sb("ln_sums", [128, TT])
    ln["nm"] = sb("ln_nm", [128, TT])
    ln["vsums"] = sb("ln_vsums", [128, TT])
    ln["rstd"] = sb("ln_rstd", [128, TT])
    ln["rstd2"] = sb("ln_rstd2", [128, TT])
    ln["sq"] = sb("ln_sq", [128, 2*1024])
    ln["xln"] = sb("ln_xln", [128, 2*1024], BF16)
    ln["eps"] = b["eps"]
    b["ln"] = ln
    b["rw"] = sb("rw", [128, KC*32], BF16)
    b["sgw"] = sb("sgw", [128, KC], BF16)
    for n in ("mx", "nmx", "sig", "gate2", "s4", "s8", "r2"):
        b[n] = sb(n, [128, TT])
    for n in ("expb", "cur", "wbuf", "combine"):
        b[n] = sb(n, [128, TT*32])
    b["mb"] = sb("mb", [128, 4*TT])
    b["silu_sb"] = sb("silu_sb", [128, 2*512], BF16)
    return b


def alloc_ffn_shared(nc, ctx, b, tag):
    sb = lambda n, shape, dt=BF16: ctx.enter_context(nc.sbuf_tensor(n + tag, shape, dt))
    b["shg"] = sb("shg", [128, KC*1024])
    b["shu"] = sb("shu", [128, KC*1024])
    b["shd"] = sb("shd", [128, KC*1024])
    b["actsh"] = sb("actsh", [128, KC*NTOK])


def alloc_ffn_experts(nc, ctx, b, tag):
    sb = lambda n, shape, dt=BF16: ctx.enter_context(nc.sbuf_tensor(n + tag, shape, dt))
    b["wg"] = sb("wg", [128, 2*KC*EI])
    b["wu"] = sb("wu", [128, 2*KC*EI])
    b["wd"] = sb("wd", [128, 2*EIC*1024])
    b["act_sb"] = sb("act_sb", [128, 2*EIC*512])
    b["ew_gpd"] = [None]*NE


def emit_attention(p, nc, dr, ident, v_agin):
    GROUPS = [[0, 1], [2, 3], [4, 5], [6, 7]]
    with ExitStack() as actx:
        k_aug = actx.enter_context(nc.sbuf_tensor("k_aug", [128, NH*T], BF16))
        qT = actx.enter_context(nc.sbuf_tensor("qT", [128, 4*T], BF16))
        v_sb = actx.enter_context(nc.sbuf_tensor("v_sb", [128, 32*520], BF16))
        p.wait("gp", "SPD", v_agin)
        p.op("gp", lambda e: e.collective_compute(
            "AllGather", ALU.bypass, replica_groups=GROUPS,
            ins=[dr["ag_in"].ap().opt()], outs=[dr["ag_out"].ap().opt()]),
            inc="AGS", k=1)
        p.wait("dv", "TE", p.cnt["TE"])
        p.wait("dv", "AC", p.cnt["AC"])
        v_ones = p.op("dv", lambda e: e.memset(v_sb[:], 1.0), inc="DV")
        with ExitStack() as qctx:
            xch = qctx.enter_context(nc.sbuf_tensor("xch", [128, 2*KC*512], BF16))
            wqkv = qctx.enter_context(nc.sbuf_tensor("wqkv", [128, KC*1536], BF16))
            p.fence_dma("gp")
            p.op("gp", lambda e: e.dma_start(
                out=wqkv.ap().rearrange("p (c n) -> p c n", c=KC),
                in_=r128(dr["wqkvT"].ap())), inc="GPD", k=16)
            v_wqkv = p.cnt["GPD"]
            with ExitStack() as pctx:
                qkvps = [pctx.enter_context(nc.psum_tensor(f"qkvps{i}", [128, 512], F32))
                         for i in range(2)]
                p.fence_dma("sp")
                p.wait("sp", "AGS", 1)
                p.fence_te()
                p.wait("te", "GPD", v_wqkv)
                agv = dr["ag_out"].ap().rearrange("(r c p) t -> p r c t", p=128, c=KC)
                ps_free = {0: None, 1: None}
                chunk_reader_te = {0: None, 1: None}
                pp = [0]

                def grp(slx, lhs_fn, rhs_fn, evac_fn):
                    pi = pp[0] % 2
                    pp[0] += 1
                    ps = qkvps[pi]
                    if ps_free[pi] is not None:
                        p.wait("te", "DV", ps_free[pi])
                    for kc in range(KC):
                        p.op("te", lambda e, kc=kc, ps=ps: e.matmul(
                            ps[:], lhs_fn(kc), rhs_fn(kc),
                            start=(kc == 0), stop=(kc == KC - 1)),
                            inc="TE" if kc == KC - 1 else None)
                    p.wait("dv", "TE", p.cnt["TE"])
                    ps_free[pi] = evac_fn(ps)

                def xch_rhs(slx):
                    return lambda kc, slx=slx: xch[:, slx*KC*512 + kc*512: slx*KC*512 + (kc+1)*512]

                for tbg in range(QB):
                    slx = tbg % 2
                    if chunk_reader_te[slx] is not None:
                        p.wait("sp", "TE", chunk_reader_te[slx])
                    p.op("sp", lambda e, tbg=tbg, slx=slx: e.dma_start(
                        out=xch[:, slx*KC*512:(slx+1)*KC*512].rearrange(
                            "p (c n) -> p c n", c=KC),
                        in_=agv[:, tbg // TB, :, (tbg % TB)*512:(tbg % TB + 1)*512]),
                        inc="SPD", k=16)
                    p.wait("te", "SPD", p.cnt["SPD"])
                    for p4 in range(4):
                        grp(slx,
                            lambda kc, p4=p4: wqkv[:, kc*1536 + p4*128: kc*1536 + (p4+1)*128],
                            xch_rhs(slx),
                            lambda ps, p4=p4, tbg=tbg: p.op(
                                "dv", lambda e, ps=ps, p4=p4, tbg=tbg: e.tensor_scalar(
                                    qT[:, p4*T + tbg*512: p4*T + (tbg+1)*512],
                                    ps[:], 0.125, None, ALU.mult), inc="DV"))
                    for p4 in range(4):
                        def kev(ps, p4=p4, tbg=tbg):
                            p.op("dv", lambda e, ps=ps, p4=p4, tbg=tbg: e.tensor_copy(
                                k_aug[0:64, (2*p4)*T + tbg*512: (2*p4)*T + (tbg+1)*512],
                                ps[0:64, :]))
                            return p.op("dv", lambda e, ps=ps, p4=p4, tbg=tbg: e.tensor_copy(
                                k_aug[64:128, (2*p4+1)*T + tbg*512: (2*p4+1)*T + (tbg+1)*512],
                                ps[64:128, :]), inc="DV")
                        grp(slx,
                            lambda kc, p4=p4: wqkv[:, kc*1536 + 512 + p4*128: kc*1536 + 512 + (p4+1)*128],
                            xch_rhs(slx),
                            kev)
                    for ts4 in range(4):
                        kt = tbg*4 + ts4
                        def vev(ps, kt=kt):
                            p.wait("dv", "DV", v_ones)
                            return p.op("dv", lambda e, ps=ps, kt=kt: e.tensor_copy(
                                v_sb[:, kt*520: kt*520 + 520].rearrange(
                                    "p (h x) -> p h x", h=NH)[:, :, 0:64],
                                ps[:].rearrange("p (h x) -> p h x", h=NH)), inc="DV")
                        grp(slx,
                            lambda kc, ts4=ts4, slx=slx: xch[:, slx*KC*512 + kc*512 + ts4*128: slx*KC*512 + kc*512 + (ts4+1)*128],
                            lambda kc: wqkv[:, kc*1536 + 1024: kc*1536 + 1536],
                            vev)
                    chunk_reader_te[slx] = p.cnt["TE"]
        with ExitStack() as psictx:
            cs1 = psictx.enter_context(nc.sbuf_tensor("cs1sb", [128, T], BF16))
            cs2 = psictx.enter_context(nc.sbuf_tensor("cs2sb", [128, T], BF16))
            tpsi = psictx.enter_context(nc.sbuf_tensor("tpsi", [128, T], BF16))
            a4 = psictx.enter_context(nc.sbuf_tensor("a4sb", [128, NH], F32))
            b4 = psictx.enter_context(nc.sbuf_tensor("b4sb", [128, NH], F32))
            a2 = psictx.enter_context(nc.sbuf_tensor("a2sb", [128, NH], F32))
            sgn = psictx.enter_context(nc.sbuf_tensor("sgnsb", [128, 1], F32))
            p.fence_dma("sp")
            for nm, buf in (("cs1", cs1), ("cs2", cs2), ("a4", a4), ("b4", b4),
                            ("sgnv", sgn)):
                p.op("sp", lambda e, nm=nm, buf=buf: e.dma_start(
                    out=buf[:], in_=dr[nm][:, :]), inc="SPD", k=16)
            v_csld = p.cnt["SPD"]
            p.wait("dv", "SPD", v_csld)
            v_a2 = p.op("dv", lambda e: e.tensor_scalar(
                a2[:], b4[:], sgn[:, 0:1], None, ALU.mult), inc="DV")
            p.wait("dv", "DV", v_a2)
            for h in range(NH):
                Bb = (1 - (h % 2)) * 64
                v_t = p.op("dv", lambda e, h=h: e.tensor_scalar(
                    tpsi[:], cs1[:], a4[:, h:h+1], None, ALU.mult), inc="DV")
                p.wait("dv", "DV", v_t)
                p.op("dv", lambda e, h=h, Bb=Bb: e.scalar_tensor_tensor(
                    k_aug[Bb:Bb+64, h*T:(h+1)*T], cs2[Bb:Bb+64, :],
                    a2[Bb:Bb+64, h:h+1], tpsi[Bb:Bb+64, :], ALU.mult, ALU.add),
                    inc="DV")
            v_psi = p.cnt["DV"]
        with ExitStack() as fctx:
            phi2 = fctx.enter_context(nc.sbuf_tensor("phi2sb", [128, T], BF16))
            masks = fctx.enter_context(nc.sbuf_tensor("maskssb", [128, 4*512], BF16))
            wout = fctx.enter_context(nc.sbuf_tensor("woutsb", [128, 4*1024], BF16))
            q_aug = fctx.enter_context(nc.sbuf_tensor("q_aug", [128, 3*512], BF16))
            P_sb = fctx.enter_context(nc.sbuf_tensor("P_sb", [128, 3*512], BF16))
            O_sb = fctx.enter_context(nc.sbuf_tensor("O_sb", [128, 4*512], BF16))
            OT_sb = fctx.enter_context(nc.sbuf_tensor("OT_sb", [128, 4*128], BF16))
            r_sb = fctx.enter_context(nc.sbuf_tensor("r_sb", [128, 4], F32))
            yst = fctx.enter_context(nc.sbuf_tensor("yst", [128, 2*1024], F32))
            p.fence_dma("sp")
            for nm, buf in (("phi2", phi2), ("masks", masks)):
                p.op("sp", lambda e, nm=nm, buf=buf: e.dma_start(
                    out=buf[:], in_=dr[nm][:, :]), inc="SPD", k=16)
            v_fl = p.cnt["SPD"]
            p.fence_dma("gp")
            p.op("gp", lambda e: e.dma_start(
                out=wout.ap().rearrange("p (c n) -> p c n", c=4),
                in_=r128(dr["woutT"].ap())), inc="GPD", k=16)
            v_wout = p.cnt["GPD"]
            with ExitStack() as pctx:
                Sps = [pctx.enter_context(nc.psum_tensor(f"Sps{i}", [128, 512], F32))
                       for i in range(3)]
                Ops = [pctx.enter_context(nc.psum_tensor(f"Ops{i}", [128, 260], F32))
                       for i in range(2)]
                tpo = pctx.enter_context(nc.psum_tensor("tpo", [128, 128], BF16))
                yps = pctx.enter_context(nc.psum_tensor("yps", [128, 1024], F32))
                p.fence_te()
                p.wait("te", "DV", v_psi)
                p.wait("te", "SPD", v_fl)
                p.wait("te", "GPD", v_wout)
                q_free_te = {0: None, 1: None, 2: None}
                S_free_ac = {0: None, 1: None, 2: None}
                P_free_te = {0: None, 1: None, 2: None}
                P_free_dv = {0: None, 1: None, 2: None}
                O_free_dv = {0: None, 1: None}
                osb_reader_te = None
                yst_reader_spd = {0: None, 1: None}
                v_qa = {}

                def emit_qa(qb, h):
                    qsl = h % 3
                    pi = h % 2
                    if q_free_te[qsl] is not None:
                        p.wait("dv", "TE", q_free_te[qsl])
                    p.op("dv", lambda e, h=h, qb=qb, qsl=qsl, pi=pi: e.tensor_copy(
                        q_aug[pi*64:(pi+1)*64, qsl*512:(qsl+1)*512],
                        qT[pi*64:(pi+1)*64, (h//2)*T + qb*512: (h//2)*T + (qb+1)*512]))
                    v_qa[(qb, h)] = p.op("dv", lambda e, qb=qb, qsl=qsl, pi=pi: e.tensor_copy(
                        q_aug[(1-pi)*64:(2-pi)*64, qsl*512:(qsl+1)*512],
                        phi2[(1-pi)*64:(2-pi)*64, qb*512:(qb+1)*512]), inc="DV")

                for qb in range(QB):
                    if osb_reader_te is not None:
                        p.wait("dv", "TE", osb_reader_te)
                    emit_qa(qb, 0)
                    emit_qa(qb, 1)
                    for h in range(NH):
                        qsl = h % 3
                        ob = h % 2
                        if h + 2 < NH:
                            emit_qa(qb, h + 2)
                        p.wait("te", "DV", v_qa[(qb, h)])
                        if O_free_dv[ob] is not None:
                            p.wait("te", "DV", O_free_dv[ob])
                        nkt = qb*4 + 4
                        SKEW = 2
                        P_ready = {}
                        for it in range(nkt + SKEW):
                            kt = it
                            if kt < nkt:
                                ssl = kt % 3
                                if S_free_ac[ssl] is not None:
                                    p.wait("te", "AC", S_free_ac[ssl])
                                v_s = p.op("te", lambda e, h=h, kt=kt, ssl=ssl, qsl=qsl: e.matmul(
                                    Sps[ssl][:], k_aug[:, h*T + kt*128: h*T + (kt+1)*128],
                                    q_aug[:, qsl*512:(qsl+1)*512],
                                    start=True, stop=True), inc="TE")
                                p.wait("ac", "TE", v_s)
                                if P_free_te[ssl] is not None:
                                    p.wait("ac", "TE", P_free_te[ssl])
                                if P_free_dv[ssl] is not None:
                                    p.wait("ac", "DV", P_free_dv[ssl])
                                v_p = p.op("ac", lambda e, ssl=ssl: e.activation(
                                    P_sb[:, ssl*512:(ssl+1)*512], Sps[ssl][:], AF.Exp),
                                    inc="AC")
                                S_free_ac[ssl] = v_p
                                if kt >= qb*4:
                                    d = kt - qb*4
                                    p.wait("dv", "AC", v_p)
                                    v_m = p.op("dv", lambda e, ssl=ssl, d=d: e.tensor_tensor(
                                        P_sb[:, ssl*512:(ssl+1)*512],
                                        P_sb[:, ssl*512:(ssl+1)*512],
                                        masks[:, d*512:(d+1)*512], ALU.mult), inc="DV")
                                    P_free_dv[ssl] = v_m
                                    P_ready[kt] = ("DV", v_m)
                                else:
                                    P_free_dv[ssl] = None
                                    P_ready[kt] = ("AC", v_p)
                            ka = it - SKEW
                            if 0 <= ka < nkt:
                                ssl2 = ka % 3
                                if ka == 0 and O_free_dv[ob] is not None:
                                    p.wait("te", "DV", O_free_dv[ob])
                                p.wait("te", P_ready[ka][0], P_ready[ka][1])
                                for qs in range(4):
                                    last = (ka == nkt - 1 and qs == 3)
                                    p.op("te", lambda e, ka=ka, qs=qs, ssl2=ssl2, ob=ob, h=h: e.matmul(
                                        Ops[ob][:, qs*65:(qs+1)*65],
                                        P_sb[:, ssl2*512 + qs*128: ssl2*512 + (qs+1)*128],
                                        v_sb[:, ka*520 + h*65: ka*520 + (h+1)*65],
                                        start=(ka == 0 and qs == 0), stop=last),
                                        inc="TE" if last else None)
                                P_free_te[ssl2] = p.cnt["TE"]
                        q_free_te[qsl] = p.cnt["TE"]
                        v_av = p.cnt["TE"]
                        p.wait("dv", "TE", v_av)
                        for qs in range(4):
                            p.op("dv", lambda e, qs=qs, ob=ob: e.reciprocal(
                                r_sb[:, qs:qs+1], Ops[ob][:, qs*65 + 64: qs*65 + 65]),
                                inc="DV" if qs == 3 else None)
                        p.wait("dv", "DV", p.cnt["DV"])
                        for qs in range(4):
                            p.op("dv", lambda e, qs=qs, h=h, ob=ob: e.tensor_scalar(
                                O_sb[:, qs*512 + h*64: qs*512 + (h+1)*64],
                                Ops[ob][:, qs*65: qs*65 + 64],
                                r_sb[:, qs:qs+1], None, ALU.mult),
                                inc="DV" if qs == 3 else None)
                        O_free_dv[ob] = p.cnt["DV"]
                    v_osb = p.cnt["DV"]
                    p.wait("te", "DV", v_osb)
                    for qs in range(4):
                        ysl = qs % 2
                        for c4 in range(4):
                            v_t = p.op("te", lambda e, qs=qs, c4=c4: e.transpose(
                                tpo[:], O_sb[:, qs*512 + c4*128: qs*512 + (c4+1)*128],
                                ident[:]), inc="TE")
                            p.wait("dv", "TE", v_t)
                            v_otc = p.op("dv", lambda e, c4=c4: e.tensor_copy(
                                OT_sb[:, c4*128:(c4+1)*128], tpo[:]), inc="DV")
                            p.wait("te", "DV", v_otc)
                        for c4 in range(4):
                            for nh2 in range(2):
                                last = (c4 == 3 and nh2 == 1)
                                p.op("te", lambda e, c4=c4, nh2=nh2: e.matmul(
                                    yps[:, nh2*512:(nh2+1)*512],
                                    OT_sb[:, c4*128:(c4+1)*128],
                                    wout[:, c4*1024 + nh2*512: c4*1024 + (nh2+1)*512],
                                    start=(c4 == 0), stop=(c4 == 3)),
                                    inc="TE" if last else None)
                        v_y = p.cnt["TE"]
                        p.wait("dv", "TE", v_y)
                        if yst_reader_spd[ysl] is not None:
                            p.wait("dv", "SPD", yst_reader_spd[ysl])
                        v_ye = p.op("dv", lambda e, ysl=ysl: e.tensor_copy(
                            yst[:, ysl*1024:(ysl+1)*1024], yps[:]), inc="DV")
                        p.wait("sp", "DV", v_ye)
                        p.op("sp", lambda e, qb=qb, qs=qs, ysl=ysl: e.dma_start(
                            out=dr["y_in"].ap().rearrange("(t p) n -> p t n", p=128)[:, qb*4+qs],
                            in_=yst[:, ysl*1024:(ysl+1)*1024]), inc="SPD", k=16)
                        yst_reader_spd[ysl] = p.cnt["SPD"]
                    osb_reader_te = p.cnt["TE"]
                v_yall = p.cnt["SPD"]
        p.wait("gp", "SPD", v_yall)
        p.op("gp", lambda e: e.collective_compute(
            "ReduceScatter", ALU.add, replica_groups=GROUPS,
            ins=[dr["y_in"].ap().opt()], outs=[dr["y_out"].ap().opt()]),
            inc="RSS", k=1)


def build_program():
    nc = bass.Bass("TRN2", target_bir_lowering=False)
    dr = {}
    dr["x_own"] = nc.dram_tensor("x_own", [NTOK, D], F32, kind="ExternalInput")
    dr["router_w"] = nc.dram_tensor("router_w", [D, 32], F32, kind="ExternalInput")
    dr["sh_gate_w"] = nc.dram_tensor("sh_gate_w", [128, KC], F32, kind="ExternalInput")
    dr["sh_gate"] = nc.dram_tensor("sh_gate", [D, ISH], F32, kind="ExternalInput")
    dr["sh_up"] = nc.dram_tensor("sh_up", [D, ISH], F32, kind="ExternalInput")
    dr["sh_down"] = nc.dram_tensor("sh_down", [ISH, D], F32, kind="ExternalInput")
    dr["w_gate"] = nc.dram_tensor("w_gate", [NE, D, EI], F32, kind="ExternalInput")
    dr["w_up"] = nc.dram_tensor("w_up", [NE, D, EI], F32, kind="ExternalInput")
    dr["w_down"] = nc.dram_tensor("w_down", [NE, EI, D], F32, kind="ExternalInput")
    dr["wqkvT"] = nc.dram_tensor("wqkvT", [D, 1536], F32, kind="ExternalInput")
    dr["woutT"] = nc.dram_tensor("woutT", [512, D], F32, kind="ExternalInput")
    dr["ident"] = nc.dram_tensor("ident", [128, 128], BF16, kind="ExternalInput")
    dr["cs1"] = nc.dram_tensor("cs1", [128, T], BF16, kind="ExternalInput")
    dr["cs2"] = nc.dram_tensor("cs2", [128, T], BF16, kind="ExternalInput")
    dr["phi2"] = nc.dram_tensor("phi2", [128, T], BF16, kind="ExternalInput")
    dr["a4"] = nc.dram_tensor("a4", [128, NH], F32, kind="ExternalInput")
    dr["b4"] = nc.dram_tensor("b4", [128, NH], F32, kind="ExternalInput")
    dr["sgnv"] = nc.dram_tensor("sgnv", [128, 1], F32, kind="ExternalInput")
    dr["masks"] = nc.dram_tensor("masks", [128, 4*512], BF16, kind="ExternalInput")
    dr["out"] = nc.dram_tensor("out", [NTOK, D], F32, kind="ExternalOutput")
    dr["xspill"] = nc.dram_tensor("xspill", [NTOK, D], F32)
    dr["ag_in"] = nc.dram_tensor("ag_in", [D, NTOK], BF16)
    dr["ag_out"] = nc.dram_tensor("ag_out", [2*D, NTOK], BF16)
    dr["y_in"] = nc.dram_tensor("y_in", [T, D], F32)
    dr["y_out"] = nc.dram_tensor("y_out", [NTOK, D], F32)

    with ExitStack() as gctx:
        p = Prog(nc)
        gctx.enter_context(p.ctx)
        ident = gctx.enter_context(nc.sbuf_tensor("identsb", [128, 128], BF16))
        epscol = gctx.enter_context(nc.sbuf_tensor("epscol", [128, 1], F32))
        blk = gctx.enter_context(nc.Block())
        p.op("sp", lambda e: e.dma_start(out=ident[:], in_=dr["ident"][:, :]),
             inc="SPD", k=16)
        v_ident = p.cnt["SPD"]
        v_eps = p.op("dv", lambda e: e.memset(epscol[:], EPS), inc="DV")

        with ExitStack() as f1ctx:
            b = {"ident": ident, "eps": epscol}
            x1 = f1ctx.enter_context(nc.sbuf_tensor("x1", [128, TT*1024], F32))
            alloc_ffn_bufs(nc, f1ctx, b, "c0")
            b["x"] = x1
            p.op("sp", lambda e: e.dma_start(
                out=x1.ap().rearrange("p (t n) -> p t n", t=TT),
                in_=dr["x_own"].ap().rearrange("(t p) n -> p t n", p=128)),
                inc="SPD", k=16)
            v_x = p.cnt["SPD"]
            p.wait("te", "SPD", v_ident)
            p.wait("dv", "SPD", v_x)
            p.wait("ac", "SPD", v_x)
            if True:
                info1 = emit_ffn(p, b, dr, x1, 0)
                with ExitStack() as ps0:
                    tpA = ps0.enter_context(nc.psum_tensor("tpA_l2", [128, 128], BF16))
                    tpB = ps0.enter_context(nc.psum_tensor("tpB_l2", [128, 128], BF16))
                    p.fence_te()
                    v_xln2 = emit_ln(p, x1, b["ln"], b["xlnT"], ident, tpA, tpB)
                p.wait("sp", "DV", v_xln2)
                p.op("sp", lambda e: e.dma_start(
                    out=dr["xspill"].ap().rearrange("(t p) n -> p t n", p=128),
                    in_=x1.ap().rearrange("p (t n) -> p t n", t=TT)),
                    inc="SPD", k=16)
                p.op("sp", lambda e, _t=b["xlnT"]: e.dma_start(
                    out=r128(dr["ag_in"].ap()),
                    in_=_t.ap().rearrange("p (c n) -> p c n", c=KC)),
                    inc="SPD", k=16)
                v_agin = p.cnt["SPD"]
        emit_attention(p, nc, dr, ident, v_agin)
        with ExitStack() as f2ctx:
            b = {"ident": ident, "eps": epscol}
            x2 = f2ctx.enter_context(nc.sbuf_tensor("x2", [128, TT*1024], F32))
            alloc_ffn_bufs(nc, f2ctx, b, "c1")
            b["x"] = x2
            p.fence_dma("sp")
            p.wait("sp", "RSS", 1)
            p.op("sp", lambda e: e.dma_start(
                out=x2.ap().rearrange("p (t n) -> p t n", t=TT),
                in_=dr["y_out"].ap().rearrange("(t p) n -> p t n", p=128)),
                inc="SPD", k=16)
            v_y = p.cnt["SPD"]
            with nc.sbuf_tensor("stg", [128, 2*1024], F32) as stg:
                for tt in range(TT):
                    sl = tt % 2
                    p.op("sp", lambda e, tt=tt, sl=sl, stg=stg: e.dma_start(
                        out=stg[:, sl*1024:(sl+1)*1024],
                        in_=dr["xspill"].ap().rearrange("(t p) n -> p t n", p=128)[:, tt]),
                        inc="SPD", k=16)
                    v_s = p.cnt["SPD"]
                    p.wait("dv", "SPD", max(v_s, v_y))
                    v_add = p.op("dv", lambda e, tt=tt, sl=sl, stg=stg: e.tensor_tensor(
                        x2[:, tt*1024:(tt+1)*1024], x2[:, tt*1024:(tt+1)*1024],
                        stg[:, sl*1024:(sl+1)*1024], ALU.add), inc="DV")
                    p.wait("sp", "DV", v_add)
            p.wait("dv", "DV", p.cnt["DV"])
            p.fence_dma("gp")
            p.op("gp", lambda e: e.dma_start(
                out=b["rw"].ap().rearrange("p (c n) -> p c n", c=KC),
                in_=r128(dr["router_w"].ap())), inc="GPD", k=16)
            p.op("gp", lambda e: e.dma_start(
                out=b["sgw"].ap(), in_=dr["sh_gate_w"][:, :]), inc="GPD", k=16)
            b["v_wsmall"] = p.cnt["GPD"]
            info2 = emit_ffn(p, b, dr, x2, 1)
            p.wait("sp", "DV", info2["x_dv"])
            p.op("sp", lambda e: e.dma_start(
                out=dr["out"].ap().rearrange("(t p) n -> p t n", p=128),
                in_=x2.ap().rearrange("p (t n) -> p t n", t=TT)),
                inc="SPD", k=16)
            p.wait("sp", "SPD", p.cnt["SPD"])
        p.flush(blk)
    return nc


def make_core_inputs(core, full):
    b, hg = core // 2, core % 2
    X = np.asarray(full["L0_attn_x"], np.float32).reshape(4*T, D)
    rows = slice(b*T + hg*NTOK, b*T + (hg+1)*NTOK)
    Wqkv = np.asarray(full["L0_attn_Wqkv"], np.float32)
    heads = range(hg*NH, (hg+1)*NH)
    qrows = np.concatenate([np.arange(h*64, (h+1)*64) for h in heads])
    sel = np.concatenate([qrows, D + qrows, 2*D + qrows])
    wqkvT = np.ascontiguousarray(Wqkv[sel].T)
    Wout = np.asarray(full["L0_attn_Wout"], np.float32)
    woutT = np.ascontiguousarray(Wout[:, hg*512:(hg+1)*512].T)
    alpha = np.asarray(full["L0_attn_alpha"], np.float32)[hg*NH:(hg+1)*NH]
    beta = np.asarray(full["L0_attn_beta"], np.float32)[hg*NH:(hg+1)*NH]
    a4 = np.ascontiguousarray(np.tile(alpha.T, (4, 1)))
    b4 = np.ascontiguousarray(np.tile(beta.T, (4, 1)))
    consts = make_consts()
    return {
        "x_own": np.ascontiguousarray(X[rows]),
        "router_w": np.asarray(full["L0_ffn_router_w"], np.float32),
        "sh_gate_w": np.ascontiguousarray(
            np.asarray(full["L0_ffn_sh_gate_w"], np.float32).reshape(KC, 128).T),
        "sh_gate": np.asarray(full["L0_ffn_sh_gate"], np.float32),
        "sh_up": np.asarray(full["L0_ffn_sh_up"], np.float32),
        "sh_down": np.asarray(full["L0_ffn_sh_down"], np.float32),
        "w_gate": np.asarray(full["L0_ffn_w_gate"], np.float32),
        "w_up": np.asarray(full["L0_ffn_w_up"], np.float32),
        "w_down": np.asarray(full["L0_ffn_w_down"], np.float32),
        "wqkvT": wqkvT, "woutT": woutT, "a4": a4, "b4": b4,
        **consts,
    }


_CONSTS = None

def make_consts():
    global _CONSTS
    if _CONSTS is not None:
        return _CONSTS
    R = 32
    om = (10000.0 ** (-2.0 * np.arange(R, dtype=np.float64) / 64.0))
    k = np.arange(T, dtype=np.float64)
    C = np.cos(om[:, None] * k[None, :])
    S = np.sin(om[:, None] * k[None, :])
    cs1 = np.concatenate([C, S, C, S], axis=0).astype(BF)
    cs2 = np.concatenate([S, C, S, C], axis=0).astype(BF)
    phi = np.concatenate([C, S], axis=0)
    phi2 = np.concatenate([phi, phi], axis=0).astype(BF)
    sgn = np.concatenate([-np.ones(32), np.ones(32), -np.ones(32),
                          np.ones(32)]).astype(np.float32)[:, None]
    i = np.arange(128)[:, None]
    j = np.arange(512)[None, :]
    masks = np.concatenate(
        [(j >= i + d*128).astype(np.float32) for d in range(4)], axis=1).astype(BF)
    _CONSTS = {
        "ident": np.eye(128).astype(BF),
        "cs1": np.ascontiguousarray(cs1), "cs2": np.ascontiguousarray(cs2),
        "phi2": np.ascontiguousarray(phi2), "sgnv": sgn, "masks": masks,
    }
    return _CONSTS


def kernel(**inputs):
    from concourse.bass_utils import run_bass_kernel_spmd
    nc = build_program()
    in_maps = [make_core_inputs(c, inputs) for c in range(8)]
    res = run_bass_kernel_spmd(nc, in_maps, core_ids=list(range(8)))
    out = np.empty((4 * T, D), dtype=np.float32)
    for c in range(8):
        b, hg = c // 2, c % 2
        out[b*T + hg*NTOK: b*T + (hg+1)*NTOK] = res.results[c]["out"]
    return out.reshape(4, T, D)


# version 05
# speedup vs baseline: 1.1019x; 1.0106x over previous
import numpy as np
import ml_dtypes
from contextlib import ExitStack

import concourse.bass as bass
import concourse.mybir as mybir

F32 = mybir.dt.float32
BF16 = mybir.dt.bfloat16
AF = mybir.ActivationFunctionType
ALU = mybir.AluOpType
AXX = mybir.AxisListType.X

D = 1024
KC = 8
NTOK = 2048
TT = 16
TB = 4
NE = 32
EI = 768
EIC = 6
ISH = 2048
T = 4096
NH = 8
QB = 8
EPS = 1e-5

ENGS = ("te", "dv", "ac", "sp", "gp")
BF = ml_dtypes.bfloat16


class Prog:
    def __init__(self, nc):
        self.nc = nc
        self.q = {e: [] for e in ENGS}
        self.sems = {}
        self.cnt = {}
        self.ctx = ExitStack()
        for s in ("TE", "DV", "AC", "GPD", "SPD", "AGS", "RSS"):
            self.sems[s] = self.ctx.enter_context(nc.semaphore(s))
            self.cnt[s] = 0

    def op(self, eng, fn, inc=None, k=1):
        if inc is None:
            self.q[eng].append(lambda e, fn=fn: fn(e))
            return None
        self.cnt[inc] += k
        s = self.sems[inc]
        self.q[eng].append(lambda e, fn=fn, s=s, k=k: fn(e).then_inc(s, k))
        return self.cnt[inc]

    def wait(self, eng, sem, val):
        if val is None or val <= 0:
            return
        self.q[eng].append(lambda e, s=self.sems[sem], v=val: e.wait_ge(s, v))

    def fence_te(self):
        self.wait("te", "DV", self.cnt["DV"])
        self.wait("te", "AC", self.cnt["AC"])

    def fence_dma(self, eng):
        for s in ("TE", "DV", "AC"):
            self.wait(eng, s, self.cnt[s])

    def flush(self, block):
        def runner(fns):
            def go(e):
                for f in fns:
                    f(e)
            return go
        block.tensor(runner(self.q["te"]))
        block.vector(runner(self.q["dv"]))
        block.scalar(runner(self.q["ac"]))
        block.sync(runner(self.q["sp"]))
        block.gpsimd(runner(self.q["gp"]))


def r128(dram_ap):
    return dram_ap.rearrange("(c p) n -> p c n", p=128)


def emit_ln(p, x, ln, xlnT, ident, tpA, tpB, ln_fence=True, x_ready=None):
    if ln_fence:
        p.wait("ac", "DV", p.cnt["DV"])
    for tt in range(TT):
        if x_ready is not None:
            p.wait("dv", x_ready[tt][0], x_ready[tt][1])
            p.wait("ac", x_ready[tt][0], x_ready[tt][1])
        p.op("dv", lambda e, tt=tt: e.reduce_sum(
            ln["sums"][:, tt:tt+1], x[:, tt*1024:(tt+1)*1024], axis=AXX),
            inc="DV" if tt == TT - 1 else None)
    p.wait("dv", "DV", p.cnt["DV"])
    v_nm = p.op("dv", lambda e: e.tensor_scalar(
        ln["nm"][:, 0:TT], ln["sums"][:, 0:TT], -1.0 / D, None, ALU.mult), inc="DV")
    p.wait("ac", "DV", v_nm)
    v_red = {}
    for tt in range(TT):
        slot = tt % 2
        if tt >= 2:
            p.wait("ac", "DV", v_red[tt - 2])
        v_sq = p.op("ac", lambda e, tt=tt, slot=slot: e.activation(
            ln["sq"][:, slot*1024:(slot+1)*1024], x[:, tt*1024:(tt+1)*1024],
            AF.Square, bias=ln["nm"][:, tt:tt+1]), inc="AC")
        p.wait("dv", "AC", v_sq)
        v_red[tt] = p.op("dv", lambda e, tt=tt, slot=slot: e.reduce_sum(
            ln["vsums"][:, tt:tt+1], ln["sq"][:, slot*1024:(slot+1)*1024],
            axis=AXX), inc="DV")
    p.wait("ac", "DV", v_red[TT - 1])
    v_rstd = p.op("ac", lambda e: e.activation(
        ln["rstd"][:, 0:TT], ln["vsums"][:, 0:TT], AF.Sqrt,
        scale=1.0 / D, bias=ln["eps"][:, 0:1]), inc="AC")
    p.wait("dv", "AC", v_rstd)
    v_rcp = p.op("dv", lambda e: e.reciprocal(
        ln["rstd2"][:, 0:TT], ln["rstd"][:, 0:TT]), inc="DV")
    p.wait("dv", "DV", v_rcp)
    v_cp = {}
    v_tr_tile = {}
    for tt in range(TT):
        slot = tt % 2
        if tt >= 2:
            p.wait("dv", "TE", v_tr_tile[tt - 2])
        v_xln = p.op("dv", lambda e, tt=tt, slot=slot: e.tensor_scalar(
            ln["xln"][:, slot*1024:(slot+1)*1024], x[:, tt*1024:(tt+1)*1024],
            ln["nm"][:, tt:tt+1], ln["rstd2"][:, tt:tt+1], ALU.add, ALU.mult),
            inc="DV")
        p.wait("te", "DV", v_xln)
        for kc in range(KC):
            tp = tpA if kc % 2 == 0 else tpB
            prev = (tt, kc - 2) if kc >= 2 else ((tt - 1, KC + kc - 2) if tt > 0 else None)
            if prev is not None:
                p.wait("te", "DV", v_cp[prev])
            v_t = p.op("te", lambda e, tt=tt, kc=kc, slot=slot, tp=tp: e.transpose(
                tp[:], ln["xln"][:, slot*1024 + kc*128: slot*1024 + (kc+1)*128],
                ident[:]), inc="TE")
            p.wait("dv", "TE", v_t)
            v_cp[(tt, kc)] = p.op("dv", lambda e, tt=tt, kc=kc, tp=tp: e.tensor_copy(
                xlnT[:, kc*NTOK + tt*128: kc*NTOK + (tt+1)*128], tp[:]), inc="DV")
        v_tr_tile[tt] = p.cnt["TE"]
    return p.cnt["DV"]


def emit_router(p, b, v_xlnT_dv, v_w_gpd):
    nc = p.nc
    xlnT, rw, sgw = b["xlnT"], b["rw"], b["sgw"]
    with ExitStack() as ps:
        import itertools as _it
        _rid = getattr(emit_router, "_n", 0); emit_router._n = _rid + 1
        lg = ps.enter_context(nc.psum_tensor(f"lg{_rid}", [128, TT*32], F32))
        sg = ps.enter_context(nc.psum_tensor(f"sg{_rid}", [128, TT], F32))
        p.fence_te()
        p.wait("te", "DV", v_xlnT_dv)
        p.wait("te", "GPD", v_w_gpd)
        for tt in range(TT):
            for kc in range(KC):
                p.op("te", lambda e, tt=tt, kc=kc: e.matmul(
                    lg[:, tt*32:(tt+1)*32],
                    xlnT[:, kc*NTOK + tt*128: kc*NTOK + (tt+1)*128],
                    rw[:, kc*32:(kc+1)*32],
                    start=(tt == 0 and kc == 0), stop=(tt == TT-1 and kc == KC-1)))
        for tt in range(TT):
            for kc in range(KC):
                last = (tt == TT-1 and kc == KC-1)
                p.op("te", lambda e, tt=tt, kc=kc: e.matmul(
                    sg[:, tt:tt+1],
                    xlnT[:, kc*NTOK + tt*128: kc*NTOK + (tt+1)*128],
                    sgw[:, kc:kc+1],
                    start=(tt == 0 and kc == 0), stop=last),
                    inc="TE" if last else None)
        v_te = p.cnt["TE"]
        p.wait("dv", "TE", v_te)
        for tt in range(TT):
            p.op("dv", lambda e, tt=tt: e.reduce_max(
                b["mx"][:, tt:tt+1], lg[:, tt*32:(tt+1)*32], axis=AXX),
                inc="DV" if tt == TT - 1 else None)
        p.wait("dv", "DV", p.cnt["DV"])
        v_nmx = p.op("dv", lambda e: e.tensor_scalar(
            b["nmx"][:, 0:TT], b["mx"][:, 0:TT], -1.0, None, ALU.mult), inc="DV")
        p.wait("ac", "DV", v_nmx)
        for tt in range(TT):
            p.op("ac", lambda e, tt=tt: e.activation(
                b["expb"][:, tt*32:(tt+1)*32], lg[:, tt*32:(tt+1)*32], AF.Exp,
                bias=b["nmx"][:, tt:tt+1]))
        v_sig = p.op("ac", lambda e: e.activation(
            b["sig"][:, 0:TT], sg[:, 0:TT], AF.Sigmoid), inc="AC")
        p.wait("dv", "AC", v_sig)
        v_g2 = p.op("dv", lambda e: e.tensor_scalar(
            b["gate2"][:, 0:TT], b["sig"][:, 0:TT], 0.5, None, ALU.mult), inc="DV")
        expb, cur, wbuf, mb, comb = b["expb"], b["cur"], b["wbuf"], b["mb"], b["combine"]
        src = expb
        for it in range(4):
            for tt in range(TT):
                p.op("dv", lambda e, tt=tt, it=it, src=src: e.reduce_max(
                    mb[:, it*TT + tt: it*TT + tt + 1], src[:, tt*32:(tt+1)*32],
                    axis=AXX), inc="DV" if tt == TT - 1 else None)
            p.wait("dv", "DV", p.cnt["DV"])
            if it < 3:
                for tt in range(TT):
                    p.op("dv", lambda e, tt=tt, it=it, src=src: e.scalar_tensor_tensor(
                        cur[:, tt*32:(tt+1)*32], src[:, tt*32:(tt+1)*32],
                        mb[:, it*TT + tt: it*TT + tt + 1], src[:, tt*32:(tt+1)*32],
                        ALU.is_lt, ALU.mult), inc="DV" if tt == TT - 1 else None)
                p.wait("dv", "DV", p.cnt["DV"])
                src = cur
        for tt in range(TT):
            p.op("dv", lambda e, tt=tt: e.scalar_tensor_tensor(
                wbuf[:, tt*32:(tt+1)*32], expb[:, tt*32:(tt+1)*32],
                mb[:, 3*TT + tt: 3*TT + tt + 1], expb[:, tt*32:(tt+1)*32],
                ALU.is_ge, ALU.mult), inc="DV" if tt == TT - 1 else None)
        p.wait("dv", "DV", p.cnt["DV"])
        for tt in range(TT):
            p.op("dv", lambda e, tt=tt: e.reduce_sum(
                b["s4"][:, tt:tt+1], wbuf[:, tt*32:(tt+1)*32], axis=AXX),
                inc="DV" if tt == TT - 1 else None)
        p.wait("dv", "DV", p.cnt["DV"])
        v = p.op("dv", lambda e: e.tensor_scalar(
            b["s8"][:, 0:TT], b["s4"][:, 0:TT], 2.0, None, ALU.mult), inc="DV")
        p.wait("dv", "DV", v)
        v = p.op("dv", lambda e: e.reciprocal(b["r2"][:, 0:TT], b["s8"][:, 0:TT]),
                 inc="DV")
        p.wait("dv", "DV", v)
        for tt in range(TT):
            p.op("dv", lambda e, tt=tt: e.tensor_scalar(
                comb[:, tt*32:(tt+1)*32], wbuf[:, tt*32:(tt+1)*32],
                b["r2"][:, tt:tt+1], None, ALU.mult),
                inc="DV" if tt == TT - 1 else None)
    return {"combine_dv": p.cnt["DV"], "gate2_dv": v_g2,
            "lg_reader_ac": v_sig, "lg_reader_dv": v_nmx}


def emit_gu_act(p, st, lw_g, lw_u, xlnT, gpsl, upsl, act_dst, silu_sb, tok_off):
    i = st.get("gu_rot", 0) & 1
    st["gu_rot"] = i ^ 1
    gps, ups = gpsl[i], upsl[i]
    busy = st.get(("gu_busy", i))
    if busy is not None:
        p.wait("te", "AC", busy[0])
        p.wait("te", "DV", busy[1])
    for kc in range(KC):
        p.op("te", lambda e, kc=kc, gps=gps: e.matmul(
            gps[:], lw_g(kc), xlnT[:, kc*NTOK + tok_off: kc*NTOK + tok_off + 512],
            start=(kc == 0), stop=(kc == KC - 1)),
            inc="TE" if kc == KC - 1 else None)
    v_g = p.cnt["TE"]
    for kc in range(KC):
        p.op("te", lambda e, kc=kc, ups=ups: e.matmul(
            ups[:], lw_u(kc), xlnT[:, kc*NTOK + tok_off: kc*NTOK + tok_off + 512],
            start=(kc == 0), stop=(kc == KC - 1)),
            inc="TE" if kc == KC - 1 else None)
    v_u = p.cnt["TE"]
    slot = st["silu_parity"]
    st["silu_parity"] ^= 1
    p.wait("ac", "TE", v_g)
    if st.get(("silu_free", slot)):
        p.wait("ac", "DV", st[("silu_free", slot)])
    v_silu = p.op("ac", lambda e, slot=slot, gps=gps: e.activation(
        silu_sb[:, slot*512:(slot+1)*512], gps[:], AF.Silu), inc="AC")
    p.wait("dv", "AC", v_silu)
    p.wait("dv", "TE", v_u)
    v_act = p.op("dv", lambda e, slot=slot, ups=ups: e.tensor_tensor(
        act_dst, silu_sb[:, slot*512:(slot+1)*512], ups[:], ALU.mult), inc="DV")
    st[("gu_busy", i)] = (v_silu, v_act)
    st[("silu_free", slot)] = v_act
    return v_act


def emit_ffn(p, b, drams, x, call):
    nc = p.nc
    xlnT = b["xlnT"]
    if call == 0:
        p.fence_dma("gp")
        p.op("gp", lambda e: e.dma_start(
            out=b["rw"].ap().rearrange("p (c n) -> p c n", c=KC),
            in_=r128(drams["router_w"].ap())), inc="GPD", k=16)
        p.op("gp", lambda e: e.dma_start(
            out=b["sgw"].ap(), in_=drams["sh_gate_w"][:, :]), inc="GPD", k=16)
        b["v_wsmall"] = p.cnt["GPD"]
    def load_sh(hf):
        for nm, buf in (("sh_gate", b["shg"]), ("sh_up", b["shu"])):
            p.op("gp", lambda e, nm=nm, buf=buf, hf=hf: e.dma_start(
                out=buf.ap().rearrange("p (c n) -> p c n", c=KC),
                in_=r128(drams[nm][:, hf*1024:(hf+1)*1024])), inc="GPD", k=16)
        vA = p.cnt["GPD"]
        p.op("gp", lambda e, hf=hf: e.dma_start(
            out=b["shd"].ap().rearrange("p (c n) -> p c n", c=KC),
            in_=r128(drams["sh_down"][hf*1024:(hf+1)*1024, :])), inc="GPD", k=16)
        return (vA, p.cnt["GPD"])

    def load_expert(ei, sl):
        for nm, buf, cs, w in (("w_gate", b["wg"], KC, EI),
                               ("w_up", b["wu"], KC, EI),
                               ("w_down", b["wd"], EIC, 1024)):
            p.op("gp", lambda e, nm=nm, buf=buf, ei=ei, sl=sl, cs=cs, w=w: e.dma_start(
                out=buf[:, sl*cs*w:(sl+1)*cs*w].rearrange("p (c n) -> p c n", c=cs),
                in_=r128(drams[nm][ei])), inc="GPD", k=16)
        return p.cnt["GPD"]

    with ExitStack() as ps0:
        tpA = ps0.enter_context(nc.psum_tensor(f"tpA{call}", [128, 128], BF16))
        tpB = ps0.enter_context(nc.psum_tensor(f"tpB{call}", [128, 128], BF16))
        p.fence_te()
        v_xlnT = emit_ln(p, x, b["ln"], xlnT, b["ident"], tpA, tpB,
                         x_ready=b.get("x_ready"))
    rinfo = emit_router(p, b, v_xlnT, b["v_wsmall"])
    x_writable_dv = max(rinfo["combine_dv"], rinfo["gate2_dv"])

    with ExitStack() as ps:
        gpsl = [ps.enter_context(nc.psum_tensor(f"gps{call}_{i}", [128, 512], F32))
                for i in range(2)]
        upsl = [ps.enter_context(nc.psum_tensor(f"ups{call}_{i}", [128, 512], F32))
                for i in range(2)]
        routed = [ps.enter_context(nc.psum_tensor(f"rt{call}_{i}", [128, 1024], F32))
                  for i in range(2)]
        st = {"silu_parity": 0}
        pair_free_dv = {0: None, 1: None}
        ts_ctr = 0
        first_mm_fenced = [False]

        def down_group(lhs_fn, rhs_fn, tt, scalar_ap):
            nonlocal ts_ctr
            pr = ts_ctr % 2
            ts_ctr += 1
            if pair_free_dv[pr] is not None:
                p.wait("te", "DV", pair_free_dv[pr])
            nic = lhs_fn(None)
            for ic in range(nic):
                for nh2 in range(2):
                    last = (ic == nic - 1 and nh2 == 1)
                    p.op("te", lambda e, ic=ic, nh2=nh2, pr=pr: e.matmul(
                        routed[pr][:, nh2*512:(nh2+1)*512],
                        lhs_fn(ic), rhs_fn(ic, nh2),
                        start=(ic == 0), stop=(ic == nic - 1)),
                        inc="TE" if last else None)
            v_te = p.cnt["TE"]
            p.wait("dv", "TE", v_te)
            if not first_mm_fenced[0]:
                p.wait("dv", "DV", x_writable_dv)
                first_mm_fenced[0] = True
            pair_free_dv[pr] = p.op("dv", lambda e, tt=tt, pr=pr, s=scalar_ap: e.scalar_tensor_tensor(
                x[:, tt*1024:(tt+1)*1024], routed[pr][:], s,
                x[:, tt*1024:(tt+1)*1024], ALU.mult, ALU.add), inc="DV")

        with ExitStack() as shctx:
            alloc_ffn_shared(nc, shctx, b, f"f{call}")
            shg, shu, shd, actsh = b["shg"], b["shu"], b["shd"], b["actsh"]
            sh_gpd = [load_sh(0)]
            for hf in range(2):
                p.fence_te()
                p.wait("te", "GPD", sh_gpd[hf][0])
                for icc in range(KC):
                    for tb in range(TB):
                        emit_gu_act(
                            p, st,
                            (lambda kc, icc=icc: shg[:, kc*1024 + icc*128: kc*1024 + (icc+1)*128]),
                            (lambda kc, icc=icc: shu[:, kc*1024 + icc*128: kc*1024 + (icc+1)*128]),
                            xlnT, gpsl, upsl,
                            actsh[:, icc*NTOK + tb*512: icc*NTOK + (tb+1)*512],
                            b["silu_sb"], tb*512)
                v_actsh_dv = p.cnt["DV"]
                p.wait("te", "DV", v_actsh_dv)
                p.wait("te", "GPD", sh_gpd[hf][1])
                for tt in range(TT):
                    down_group(
                        (lambda ic, tt=tt: KC if ic is None else
                         actsh[:, ic*NTOK + tt*128: ic*NTOK + (tt+1)*128]),
                        (lambda ic, nh2: shd[:, ic*1024 + nh2*512: ic*1024 + (nh2+1)*512]),
                        tt, b["gate2"][:, tt:tt+1])
                if hf == 0:
                    p.wait("gp", "TE", p.cnt["TE"])
                    sh_gpd.append(load_sh(1))
            v_te_sh, v_dv_sh = p.cnt["TE"], p.cnt["DV"]

        alloc_ffn_experts(nc, ps, b, f"f{call}")
        wg, wu, wd, act_sb = b["wg"], b["wu"], b["wd"], b["act_sb"]
        p.wait("gp", "TE", v_te_sh)
        p.wait("gp", "DV", v_dv_sh)
        for ei in range(2):
            b["ew_gpd"][ei] = load_expert(ei, ei % 2)
        act_slot_reader_te = {0: None, 1: None}
        for ei in range(NE):
            sl = ei % 2
            p.wait("te", "GPD", b["ew_gpd"][ei])
            for tb in range(TB):
                asl = tb % 2
                if act_slot_reader_te[asl] is not None:
                    p.wait("dv", "TE", act_slot_reader_te[asl])
                for ic in range(EIC):
                    emit_gu_act(
                        p, st,
                        (lambda kc, ic=ic, sl=sl: wg[:, sl*KC*EI + kc*EI + ic*128: sl*KC*EI + kc*EI + (ic+1)*128]),
                        (lambda kc, ic=ic, sl=sl: wu[:, sl*KC*EI + kc*EI + ic*128: sl*KC*EI + kc*EI + (ic+1)*128]),
                        xlnT, gpsl, upsl,
                        act_sb[:, asl*EIC*512 + ic*512: asl*EIC*512 + (ic+1)*512],
                        b["silu_sb"], tb*512)
                v_act_dv = p.cnt["DV"]
                p.wait("te", "DV", v_act_dv)
                for ts4 in range(4):
                    tt = tb*4 + ts4
                    down_group(
                        (lambda ic, ts4=ts4, asl=asl: EIC if ic is None else
                         act_sb[:, asl*EIC*512 + ic*512 + ts4*128: asl*EIC*512 + ic*512 + (ts4+1)*128]),
                        (lambda ic, nh2, sl=sl: wd[:, sl*EIC*1024 + ic*1024 + nh2*512: sl*EIC*1024 + ic*1024 + (nh2+1)*512]),
                        tt, b["combine"][:, tt*32 + ei: tt*32 + ei + 1])
                act_slot_reader_te[asl] = p.cnt["TE"]
            if ei + 2 < NE:
                p.wait("gp", "TE", p.cnt["TE"])
                b["ew_gpd"][ei + 2] = load_expert(ei + 2, (ei + 2) % 2)
    return {"x_dv": p.cnt["DV"]}


def alloc_ffn_bufs(nc, ctx, b, tag=""):
    sb = lambda n, shape, dt=F32: ctx.enter_context(nc.sbuf_tensor(n + tag, shape, dt))
    b["xlnT"] = sb("xlnT", [128, KC*NTOK], BF16)
    ln = {}
    ln["sums"] = sb("ln_sums", [128, TT])
    ln["nm"] = sb("ln_nm", [128, TT])
    ln["vsums"] = sb("ln_vsums", [128, TT])
    ln["rstd"] = sb("ln_rstd", [128, TT])
    ln["rstd2"] = sb("ln_rstd2", [128, TT])
    ln["sq"] = sb("ln_sq", [128, 2*1024])
    ln["xln"] = sb("ln_xln", [128, 2*1024], BF16)
    ln["eps"] = b["eps"]
    b["ln"] = ln
    b["rw"] = sb("rw", [128, KC*32], BF16)
    b["sgw"] = sb("sgw", [128, KC], BF16)
    for n in ("mx", "nmx", "sig", "gate2", "s4", "s8", "r2"):
        b[n] = sb(n, [128, TT])
    for n in ("expb", "cur", "wbuf", "combine"):
        b[n] = sb(n, [128, TT*32])
    b["mb"] = sb("mb", [128, 4*TT])
    b["silu_sb"] = sb("silu_sb", [128, 2*512], BF16)
    return b


def alloc_ffn_shared(nc, ctx, b, tag):
    sb = lambda n, shape, dt=BF16: ctx.enter_context(nc.sbuf_tensor(n + tag, shape, dt))
    b["shg"] = sb("shg", [128, KC*1024])
    b["shu"] = sb("shu", [128, KC*1024])
    b["shd"] = sb("shd", [128, KC*1024])
    b["actsh"] = sb("actsh", [128, KC*NTOK])


def alloc_ffn_experts(nc, ctx, b, tag):
    sb = lambda n, shape, dt=BF16: ctx.enter_context(nc.sbuf_tensor(n + tag, shape, dt))
    b["wg"] = sb("wg", [128, 2*KC*EI])
    b["wu"] = sb("wu", [128, 2*KC*EI])
    b["wd"] = sb("wd", [128, 2*EIC*1024])
    b["act_sb"] = sb("act_sb", [128, 2*EIC*512])
    b["ew_gpd"] = [None]*NE


def emit_attention(p, nc, dr, ident, v_agin):
    GROUPS = [[0, 1], [2, 3], [4, 5], [6, 7]]
    with ExitStack() as actx:
        k_aug = actx.enter_context(nc.sbuf_tensor("k_aug", [128, NH*T], BF16))
        qT = actx.enter_context(nc.sbuf_tensor("qT", [128, 4*T], BF16))
        v_sb = actx.enter_context(nc.sbuf_tensor("v_sb", [128, 32*520], BF16))
        p.wait("gp", "SPD", v_agin)
        p.op("gp", lambda e: e.collective_compute(
            "AllGather", ALU.bypass, replica_groups=GROUPS,
            ins=[dr["ag_in"].ap().opt()], outs=[dr["ag_out"].ap().opt()]),
            inc="AGS", k=1)
        p.wait("dv", "TE", p.cnt["TE"])
        p.wait("dv", "AC", p.cnt["AC"])
        v_ones = p.op("dv", lambda e: e.memset(v_sb[:], 1.0), inc="DV")
        with ExitStack() as qctx:
            xch = qctx.enter_context(nc.sbuf_tensor("xch", [128, 2*KC*512], BF16))
            wqkv = qctx.enter_context(nc.sbuf_tensor("wqkv", [128, KC*1536], BF16))
            p.fence_dma("gp")
            p.op("gp", lambda e: e.dma_start(
                out=wqkv.ap().rearrange("p (c n) -> p c n", c=KC),
                in_=r128(dr["wqkvT"].ap())), inc="GPD", k=16)
            v_wqkv = p.cnt["GPD"]
            with ExitStack() as pctx:
                qkvps = [pctx.enter_context(nc.psum_tensor(f"qkvps{i}", [128, 512], F32))
                         for i in range(2)]
                p.fence_dma("sp")
                p.wait("sp", "AGS", 1)
                p.fence_te()
                p.wait("te", "GPD", v_wqkv)
                agv = dr["ag_out"].ap().rearrange("(r c p) t -> p r c t", p=128, c=KC)
                ps_free = {0: None, 1: None}
                chunk_reader_te = {0: None, 1: None}
                pp = [0]

                def grp(slx, lhs_fn, rhs_fn, evac_fn):
                    pi = pp[0] % 2
                    pp[0] += 1
                    ps = qkvps[pi]
                    if ps_free[pi] is not None:
                        p.wait("te", "DV", ps_free[pi])
                    for kc in range(KC):
                        p.op("te", lambda e, kc=kc, ps=ps: e.matmul(
                            ps[:], lhs_fn(kc), rhs_fn(kc),
                            start=(kc == 0), stop=(kc == KC - 1)),
                            inc="TE" if kc == KC - 1 else None)
                    p.wait("dv", "TE", p.cnt["TE"])
                    ps_free[pi] = evac_fn(ps)

                def xch_rhs(slx):
                    return lambda kc, slx=slx: xch[:, slx*KC*512 + kc*512: slx*KC*512 + (kc+1)*512]

                for tbg in range(QB):
                    slx = tbg % 2
                    if chunk_reader_te[slx] is not None:
                        p.wait("sp", "TE", chunk_reader_te[slx])
                    p.op("sp", lambda e, tbg=tbg, slx=slx: e.dma_start(
                        out=xch[:, slx*KC*512:(slx+1)*KC*512].rearrange(
                            "p (c n) -> p c n", c=KC),
                        in_=agv[:, tbg // TB, :, (tbg % TB)*512:(tbg % TB + 1)*512]),
                        inc="SPD", k=16)
                    p.wait("te", "SPD", p.cnt["SPD"])
                    for p4 in range(4):
                        grp(slx,
                            lambda kc, p4=p4: wqkv[:, kc*1536 + p4*128: kc*1536 + (p4+1)*128],
                            xch_rhs(slx),
                            lambda ps, p4=p4, tbg=tbg: p.op(
                                "dv", lambda e, ps=ps, p4=p4, tbg=tbg: e.tensor_scalar(
                                    qT[:, p4*T + tbg*512: p4*T + (tbg+1)*512],
                                    ps[:], 0.125, None, ALU.mult), inc="DV"))
                    for p4 in range(4):
                        def kev(ps, p4=p4, tbg=tbg):
                            p.op("dv", lambda e, ps=ps, p4=p4, tbg=tbg: e.tensor_copy(
                                k_aug[0:64, (2*p4)*T + tbg*512: (2*p4)*T + (tbg+1)*512],
                                ps[0:64, :]))
                            return p.op("dv", lambda e, ps=ps, p4=p4, tbg=tbg: e.tensor_copy(
                                k_aug[64:128, (2*p4+1)*T + tbg*512: (2*p4+1)*T + (tbg+1)*512],
                                ps[64:128, :]), inc="DV")
                        grp(slx,
                            lambda kc, p4=p4: wqkv[:, kc*1536 + 512 + p4*128: kc*1536 + 512 + (p4+1)*128],
                            xch_rhs(slx),
                            kev)
                    for ts4 in range(4):
                        kt = tbg*4 + ts4
                        def vev(ps, kt=kt):
                            p.wait("dv", "DV", v_ones)
                            return p.op("dv", lambda e, ps=ps, kt=kt: e.tensor_copy(
                                v_sb[:, kt*520: kt*520 + 520].rearrange(
                                    "p (h x) -> p h x", h=NH)[:, :, 0:64],
                                ps[:].rearrange("p (h x) -> p h x", h=NH)), inc="DV")
                        grp(slx,
                            lambda kc, ts4=ts4, slx=slx: xch[:, slx*KC*512 + kc*512 + ts4*128: slx*KC*512 + kc*512 + (ts4+1)*128],
                            lambda kc: wqkv[:, kc*1536 + 1024: kc*1536 + 1536],
                            vev)
                    chunk_reader_te[slx] = p.cnt["TE"]
        with ExitStack() as psictx:
            cs1 = psictx.enter_context(nc.sbuf_tensor("cs1sb", [128, T], BF16))
            cs2 = psictx.enter_context(nc.sbuf_tensor("cs2sb", [128, T], BF16))
            tpsi = psictx.enter_context(nc.sbuf_tensor("tpsi", [128, T], BF16))
            a4 = psictx.enter_context(nc.sbuf_tensor("a4sb", [128, NH], F32))
            b4 = psictx.enter_context(nc.sbuf_tensor("b4sb", [128, NH], F32))
            a2 = psictx.enter_context(nc.sbuf_tensor("a2sb", [128, NH], F32))
            sgn = psictx.enter_context(nc.sbuf_tensor("sgnsb", [128, 1], F32))
            p.fence_dma("sp")
            for nm, buf in (("cs1", cs1), ("cs2", cs2), ("a4", a4), ("b4", b4),
                            ("sgnv", sgn)):
                p.op("sp", lambda e, nm=nm, buf=buf: e.dma_start(
                    out=buf[:], in_=dr[nm][:, :]), inc="SPD", k=16)
            v_csld = p.cnt["SPD"]
            p.wait("dv", "SPD", v_csld)
            v_a2 = p.op("dv", lambda e: e.tensor_scalar(
                a2[:], b4[:], sgn[:, 0:1], None, ALU.mult), inc="DV")
            p.wait("dv", "DV", v_a2)
            for h in range(NH):
                Bb = (1 - (h % 2)) * 64
                v_t = p.op("dv", lambda e, h=h: e.tensor_scalar(
                    tpsi[:], cs1[:], a4[:, h:h+1], None, ALU.mult), inc="DV")
                p.wait("dv", "DV", v_t)
                p.op("dv", lambda e, h=h, Bb=Bb: e.scalar_tensor_tensor(
                    k_aug[Bb:Bb+64, h*T:(h+1)*T], cs2[Bb:Bb+64, :],
                    a2[Bb:Bb+64, h:h+1], tpsi[Bb:Bb+64, :], ALU.mult, ALU.add),
                    inc="DV")
            v_psi = p.cnt["DV"]
        with ExitStack() as fctx:
            phi2 = fctx.enter_context(nc.sbuf_tensor("phi2sb", [128, T], BF16))
            masks = fctx.enter_context(nc.sbuf_tensor("maskssb", [128, 4*512], BF16))
            wout = fctx.enter_context(nc.sbuf_tensor("woutsb", [128, 4*1024], BF16))
            q_aug = fctx.enter_context(nc.sbuf_tensor("q_aug", [128, 3*512], BF16))
            P_sb = fctx.enter_context(nc.sbuf_tensor("P_sb", [128, 3*512], BF16))
            O_sb = fctx.enter_context(nc.sbuf_tensor("O_sb", [128, 4*512], BF16))
            OT_sb = fctx.enter_context(nc.sbuf_tensor("OT_sb", [128, 4*128], BF16))
            r_sb = fctx.enter_context(nc.sbuf_tensor("r_sb", [128, 4], F32))
            yst = fctx.enter_context(nc.sbuf_tensor("yst", [128, 2*1024], BF16))
            p.fence_dma("sp")
            for nm, buf in (("phi2", phi2), ("masks", masks)):
                p.op("sp", lambda e, nm=nm, buf=buf: e.dma_start(
                    out=buf[:], in_=dr[nm][:, :]), inc="SPD", k=16)
            v_fl = p.cnt["SPD"]
            p.fence_dma("gp")
            p.op("gp", lambda e: e.dma_start(
                out=wout.ap().rearrange("p (c n) -> p c n", c=4),
                in_=r128(dr["woutT"].ap())), inc="GPD", k=16)
            v_wout = p.cnt["GPD"]
            with ExitStack() as pctx:
                Sps = [pctx.enter_context(nc.psum_tensor(f"Sps{i}", [128, 512], F32))
                       for i in range(3)]
                Ops = [pctx.enter_context(nc.psum_tensor(f"Ops{i}", [128, 260], F32))
                       for i in range(2)]
                tpo = pctx.enter_context(nc.psum_tensor("tpo", [128, 128], BF16))
                yps = pctx.enter_context(nc.psum_tensor("yps", [128, 1024], F32))
                p.fence_te()
                p.wait("te", "DV", v_psi)
                p.wait("te", "SPD", v_fl)
                p.wait("te", "GPD", v_wout)
                q_free_te = {0: None, 1: None, 2: None}
                S_free_ac = {0: None, 1: None, 2: None}
                P_free_te = {0: None, 1: None, 2: None}
                P_free_dv = {0: None, 1: None, 2: None}
                O_free_dv = {0: None, 1: None}
                osb_reader_te = None
                yst_reader_spd = {0: None, 1: None}
                v_qa = {}

                def emit_qa(qb, h):
                    qsl = h % 3
                    pi = h % 2
                    if q_free_te[qsl] is not None:
                        p.wait("dv", "TE", q_free_te[qsl])
                    p.op("dv", lambda e, h=h, qb=qb, qsl=qsl, pi=pi: e.tensor_copy(
                        q_aug[pi*64:(pi+1)*64, qsl*512:(qsl+1)*512],
                        qT[pi*64:(pi+1)*64, (h//2)*T + qb*512: (h//2)*T + (qb+1)*512]))
                    v_qa[(qb, h)] = p.op("dv", lambda e, qb=qb, qsl=qsl, pi=pi: e.tensor_copy(
                        q_aug[(1-pi)*64:(2-pi)*64, qsl*512:(qsl+1)*512],
                        phi2[(1-pi)*64:(2-pi)*64, qb*512:(qb+1)*512]), inc="DV")

                for qb in range(QB):
                    if osb_reader_te is not None:
                        p.wait("dv", "TE", osb_reader_te)
                    emit_qa(qb, 0)
                    emit_qa(qb, 1)
                    for h in range(NH):
                        qsl = h % 3
                        ob = h % 2
                        if h + 2 < NH:
                            emit_qa(qb, h + 2)
                        p.wait("te", "DV", v_qa[(qb, h)])
                        if O_free_dv[ob] is not None:
                            p.wait("te", "DV", O_free_dv[ob])
                        nkt = qb*4 + 4
                        SKEW = 2
                        P_ready = {}
                        for it in range(nkt + SKEW):
                            kt = it
                            if kt < nkt:
                                ssl = kt % 3
                                if S_free_ac[ssl] is not None:
                                    p.wait("te", "AC", S_free_ac[ssl])
                                v_s = p.op("te", lambda e, h=h, kt=kt, ssl=ssl, qsl=qsl: e.matmul(
                                    Sps[ssl][:], k_aug[:, h*T + kt*128: h*T + (kt+1)*128],
                                    q_aug[:, qsl*512:(qsl+1)*512],
                                    start=True, stop=True), inc="TE")
                                p.wait("ac", "TE", v_s)
                                if P_free_te[ssl] is not None:
                                    p.wait("ac", "TE", P_free_te[ssl])
                                if P_free_dv[ssl] is not None:
                                    p.wait("ac", "DV", P_free_dv[ssl])
                                v_p = p.op("ac", lambda e, ssl=ssl: e.activation(
                                    P_sb[:, ssl*512:(ssl+1)*512], Sps[ssl][:], AF.Exp),
                                    inc="AC")
                                S_free_ac[ssl] = v_p
                                if kt >= qb*4:
                                    d = kt - qb*4
                                    p.wait("dv", "AC", v_p)
                                    v_m = p.op("dv", lambda e, ssl=ssl, d=d: e.tensor_tensor(
                                        P_sb[:, ssl*512:(ssl+1)*512],
                                        P_sb[:, ssl*512:(ssl+1)*512],
                                        masks[:, d*512:(d+1)*512], ALU.mult), inc="DV")
                                    P_free_dv[ssl] = v_m
                                    P_ready[kt] = ("DV", v_m)
                                else:
                                    P_free_dv[ssl] = None
                                    P_ready[kt] = ("AC", v_p)
                            ka = it - SKEW
                            if 0 <= ka < nkt:
                                ssl2 = ka % 3
                                if ka == 0 and O_free_dv[ob] is not None:
                                    p.wait("te", "DV", O_free_dv[ob])
                                p.wait("te", P_ready[ka][0], P_ready[ka][1])
                                for qs in range(4):
                                    last = (ka == nkt - 1 and qs == 3)
                                    p.op("te", lambda e, ka=ka, qs=qs, ssl2=ssl2, ob=ob, h=h: e.matmul(
                                        Ops[ob][:, qs*65:(qs+1)*65],
                                        P_sb[:, ssl2*512 + qs*128: ssl2*512 + (qs+1)*128],
                                        v_sb[:, ka*520 + h*65: ka*520 + (h+1)*65],
                                        start=(ka == 0 and qs == 0), stop=last),
                                        inc="TE" if last else None)
                                P_free_te[ssl2] = p.cnt["TE"]
                        q_free_te[qsl] = p.cnt["TE"]
                        v_av = p.cnt["TE"]
                        p.wait("dv", "TE", v_av)
                        for qs in range(4):
                            p.op("dv", lambda e, qs=qs, ob=ob: e.reciprocal(
                                r_sb[:, qs:qs+1], Ops[ob][:, qs*65 + 64: qs*65 + 65]),
                                inc="DV" if qs == 3 else None)
                        p.wait("dv", "DV", p.cnt["DV"])
                        for qs in range(4):
                            p.op("dv", lambda e, qs=qs, h=h, ob=ob: e.tensor_scalar(
                                O_sb[:, qs*512 + h*64: qs*512 + (h+1)*64],
                                Ops[ob][:, qs*65: qs*65 + 64],
                                r_sb[:, qs:qs+1], None, ALU.mult),
                                inc="DV" if qs == 3 else None)
                        O_free_dv[ob] = p.cnt["DV"]
                    v_osb = p.cnt["DV"]
                    p.wait("te", "DV", v_osb)
                    for qs in range(4):
                        ysl = qs % 2
                        for c4 in range(4):
                            v_t = p.op("te", lambda e, qs=qs, c4=c4: e.transpose(
                                tpo[:], O_sb[:, qs*512 + c4*128: qs*512 + (c4+1)*128],
                                ident[:]), inc="TE")
                            p.wait("dv", "TE", v_t)
                            v_otc = p.op("dv", lambda e, c4=c4: e.tensor_copy(
                                OT_sb[:, c4*128:(c4+1)*128], tpo[:]), inc="DV")
                            p.wait("te", "DV", v_otc)
                        for c4 in range(4):
                            for nh2 in range(2):
                                last = (c4 == 3 and nh2 == 1)
                                p.op("te", lambda e, c4=c4, nh2=nh2: e.matmul(
                                    yps[:, nh2*512:(nh2+1)*512],
                                    OT_sb[:, c4*128:(c4+1)*128],
                                    wout[:, c4*1024 + nh2*512: c4*1024 + (nh2+1)*512],
                                    start=(c4 == 0), stop=(c4 == 3)),
                                    inc="TE" if last else None)
                        v_y = p.cnt["TE"]
                        p.wait("dv", "TE", v_y)
                        if yst_reader_spd[ysl] is not None:
                            p.wait("dv", "SPD", yst_reader_spd[ysl])
                        v_ye = p.op("dv", lambda e, ysl=ysl: e.tensor_copy(
                            yst[:, ysl*1024:(ysl+1)*1024], yps[:]), inc="DV")
                        p.wait("sp", "DV", v_ye)
                        row = (qb*512 + qs*128) // 128
                        p.op("sp", lambda e, row=row, ysl=ysl: e.dma_start(
                            out=dr["y_in"].ap().rearrange("(t p) n -> p t n", p=128)[:, row],
                            in_=yst[:, ysl*1024:(ysl+1)*1024]), inc="SPD", k=16)
                        yst_reader_spd[ysl] = p.cnt["SPD"]
                    osb_reader_te = p.cnt["TE"]
                    if qb == 5:
                        v_y_halfA = p.cnt["SPD"]
                v_yall = p.cnt["SPD"]
        p.wait("gp", "SPD", v_yall)
        p.op("gp", lambda e: e.collective_compute(
            "ReduceScatter", ALU.add, replica_groups=GROUPS,
            ins=[dr["y_in"].ap().opt()], outs=[dr["y_out"].ap().opt()]),
            inc="RSS", k=1)


def build_program():
    nc = bass.Bass("TRN2", target_bir_lowering=False)
    dr = {}
    dr["x_own"] = nc.dram_tensor("x_own", [NTOK, D], F32, kind="ExternalInput")
    dr["router_w"] = nc.dram_tensor("router_w", [D, 32], F32, kind="ExternalInput")
    dr["sh_gate_w"] = nc.dram_tensor("sh_gate_w", [128, KC], F32, kind="ExternalInput")
    dr["sh_gate"] = nc.dram_tensor("sh_gate", [D, ISH], F32, kind="ExternalInput")
    dr["sh_up"] = nc.dram_tensor("sh_up", [D, ISH], F32, kind="ExternalInput")
    dr["sh_down"] = nc.dram_tensor("sh_down", [ISH, D], F32, kind="ExternalInput")
    dr["w_gate"] = nc.dram_tensor("w_gate", [NE, D, EI], F32, kind="ExternalInput")
    dr["w_up"] = nc.dram_tensor("w_up", [NE, D, EI], F32, kind="ExternalInput")
    dr["w_down"] = nc.dram_tensor("w_down", [NE, EI, D], F32, kind="ExternalInput")
    dr["wqkvT"] = nc.dram_tensor("wqkvT", [D, 1536], F32, kind="ExternalInput")
    dr["woutT"] = nc.dram_tensor("woutT", [512, D], F32, kind="ExternalInput")
    dr["ident"] = nc.dram_tensor("ident", [128, 128], BF16, kind="ExternalInput")
    dr["cs1"] = nc.dram_tensor("cs1", [128, T], BF16, kind="ExternalInput")
    dr["cs2"] = nc.dram_tensor("cs2", [128, T], BF16, kind="ExternalInput")
    dr["phi2"] = nc.dram_tensor("phi2", [128, T], BF16, kind="ExternalInput")
    dr["a4"] = nc.dram_tensor("a4", [128, NH], F32, kind="ExternalInput")
    dr["b4"] = nc.dram_tensor("b4", [128, NH], F32, kind="ExternalInput")
    dr["sgnv"] = nc.dram_tensor("sgnv", [128, 1], F32, kind="ExternalInput")
    dr["masks"] = nc.dram_tensor("masks", [128, 4*512], BF16, kind="ExternalInput")
    dr["out"] = nc.dram_tensor("out", [NTOK, D], F32, kind="ExternalOutput")
    dr["xspill"] = nc.dram_tensor("xspill", [NTOK, D], F32)
    dr["ag_in"] = nc.dram_tensor("ag_in", [D, NTOK], BF16)
    dr["ag_out"] = nc.dram_tensor("ag_out", [2*D, NTOK], BF16)
    dr["y_in"] = nc.dram_tensor("y_in", [T, D], BF16)
    dr["y_out"] = nc.dram_tensor("y_out", [NTOK, D], BF16)

    with ExitStack() as gctx:
        p = Prog(nc)
        gctx.enter_context(p.ctx)
        ident = gctx.enter_context(nc.sbuf_tensor("identsb", [128, 128], BF16))
        epscol = gctx.enter_context(nc.sbuf_tensor("epscol", [128, 1], F32))
        blk = gctx.enter_context(nc.Block())
        p.op("sp", lambda e: e.dma_start(out=ident[:], in_=dr["ident"][:, :]),
             inc="SPD", k=16)
        v_ident = p.cnt["SPD"]
        v_eps = p.op("dv", lambda e: e.memset(epscol[:], EPS), inc="DV")

        with ExitStack() as f1ctx:
            b = {"ident": ident, "eps": epscol}
            x1 = f1ctx.enter_context(nc.sbuf_tensor("x1", [128, TT*1024], F32))
            alloc_ffn_bufs(nc, f1ctx, b, "c0")
            b["x"] = x1
            p.op("sp", lambda e: e.dma_start(
                out=x1.ap().rearrange("p (t n) -> p t n", t=TT),
                in_=dr["x_own"].ap().rearrange("(t p) n -> p t n", p=128)),
                inc="SPD", k=16)
            v_x = p.cnt["SPD"]
            x_rdy = [("SPD", v_x)] * TT
            p.wait("te", "SPD", v_ident)
            b["x_ready"] = x_rdy
            if True:
                info1 = emit_ffn(p, b, dr, x1, 0)
                b.pop("x_ready", None)
                with ExitStack() as ps0:
                    tpA = ps0.enter_context(nc.psum_tensor("tpA_l2", [128, 128], BF16))
                    tpB = ps0.enter_context(nc.psum_tensor("tpB_l2", [128, 128], BF16))
                    p.fence_te()
                    v_xln2 = emit_ln(p, x1, b["ln"], b["xlnT"], ident, tpA, tpB)
                p.wait("sp", "DV", v_xln2)
                p.op("sp", lambda e: e.dma_start(
                    out=dr["xspill"].ap().rearrange("(t p) n -> p t n", p=128),
                    in_=x1.ap().rearrange("p (t n) -> p t n", t=TT)),
                    inc="SPD", k=16)
                p.op("sp", lambda e, _t=b["xlnT"]: e.dma_start(
                    out=r128(dr["ag_in"].ap()),
                    in_=_t.ap().rearrange("p (c n) -> p c n", c=KC)),
                    inc="SPD", k=16)
                v_agin = p.cnt["SPD"]
        emit_attention(p, nc, dr, ident, v_agin)
        with ExitStack() as f2ctx:
            b = {"ident": ident, "eps": epscol}
            x2 = f2ctx.enter_context(nc.sbuf_tensor("x2", [128, TT*1024], F32))
            alloc_ffn_bufs(nc, f2ctx, b, "c1")
            b["x"] = x2
            p.fence_dma("sp")
            p.op("sp", lambda e: e.dma_start(
                out=x2.ap().rearrange("p (t n) -> p t n", t=TT),
                in_=dr["xspill"].ap().rearrange("(t p) n -> p t n", p=128)),
                inc="SPD", k=16)
            v_xs = p.cnt["SPD"]
            x_rdy2 = []
            with nc.sbuf_tensor("ystg", [128, 2*1024], BF16) as ystg:
                v_adds = {}
                for tt in range(TT):
                    sl = tt % 2
                    if tt >= 2:
                        p.wait("sp", "DV", v_adds[tt - 2])
                    p.wait("sp", "RSS", 1)
                    p.op("sp", lambda e, tt=tt, sl=sl, ystg=ystg: e.dma_start(
                        out=ystg[:, sl*1024:(sl+1)*1024],
                        in_=dr["y_out"].ap().rearrange("(t p) n -> p t n", p=128)[:, tt]),
                        inc="SPD", k=16)
                    v_s = p.cnt["SPD"]
                    p.wait("dv", "SPD", max(v_s, v_xs))
                    v_adds[tt] = p.op("dv", lambda e, tt=tt, sl=sl, ystg=ystg: e.tensor_tensor(
                        x2[:, tt*1024:(tt+1)*1024], x2[:, tt*1024:(tt+1)*1024],
                        ystg[:, sl*1024:(sl+1)*1024], ALU.add), inc="DV")
                    x_rdy2.append(("DV", v_adds[tt]))
            b["x_ready"] = x_rdy2
            p.fence_dma("gp")
            p.op("gp", lambda e: e.dma_start(
                out=b["rw"].ap().rearrange("p (c n) -> p c n", c=KC),
                in_=r128(dr["router_w"].ap())), inc="GPD", k=16)
            p.op("gp", lambda e: e.dma_start(
                out=b["sgw"].ap(), in_=dr["sh_gate_w"][:, :]), inc="GPD", k=16)
            b["v_wsmall"] = p.cnt["GPD"]
            info2 = emit_ffn(p, b, dr, x2, 1)
            p.wait("sp", "DV", info2["x_dv"])
            p.op("sp", lambda e: e.dma_start(
                out=dr["out"].ap().rearrange("(t p) n -> p t n", p=128),
                in_=x2.ap().rearrange("p (t n) -> p t n", t=TT)),
                inc="SPD", k=16)
            p.wait("sp", "SPD", p.cnt["SPD"])
        p.flush(blk)
    return nc


def make_core_inputs(core, full):
    b, hg = core // 2, core % 2
    X = np.asarray(full["L0_attn_x"], np.float32).reshape(4*T, D)
    rows = slice(b*T + hg*NTOK, b*T + (hg+1)*NTOK)
    Wqkv = np.asarray(full["L0_attn_Wqkv"], np.float32)
    heads = range(hg*NH, (hg+1)*NH)
    qrows = np.concatenate([np.arange(h*64, (h+1)*64) for h in heads])
    sel = np.concatenate([qrows, D + qrows, 2*D + qrows])
    wqkvT = np.ascontiguousarray(Wqkv[sel].T)
    Wout = np.asarray(full["L0_attn_Wout"], np.float32)
    woutT = np.ascontiguousarray(Wout[:, hg*512:(hg+1)*512].T)
    alpha = np.asarray(full["L0_attn_alpha"], np.float32)[hg*NH:(hg+1)*NH]
    beta = np.asarray(full["L0_attn_beta"], np.float32)[hg*NH:(hg+1)*NH]
    a4 = np.ascontiguousarray(np.tile(alpha.T, (4, 1)))
    b4 = np.ascontiguousarray(np.tile(beta.T, (4, 1)))
    consts = make_consts()
    return {
        "x_own": np.ascontiguousarray(X[rows]),
        "router_w": np.asarray(full["L0_ffn_router_w"], np.float32),
        "sh_gate_w": np.ascontiguousarray(
            np.asarray(full["L0_ffn_sh_gate_w"], np.float32).reshape(KC, 128).T),
        "sh_gate": np.asarray(full["L0_ffn_sh_gate"], np.float32),
        "sh_up": np.asarray(full["L0_ffn_sh_up"], np.float32),
        "sh_down": np.asarray(full["L0_ffn_sh_down"], np.float32),
        "w_gate": np.asarray(full["L0_ffn_w_gate"], np.float32),
        "w_up": np.asarray(full["L0_ffn_w_up"], np.float32),
        "w_down": np.asarray(full["L0_ffn_w_down"], np.float32),
        "wqkvT": wqkvT, "woutT": woutT, "a4": a4, "b4": b4,
        **consts,
    }


_CONSTS = None

def make_consts():
    global _CONSTS
    if _CONSTS is not None:
        return _CONSTS
    R = 32
    om = (10000.0 ** (-2.0 * np.arange(R, dtype=np.float64) / 64.0))
    k = np.arange(T, dtype=np.float64)
    C = np.cos(om[:, None] * k[None, :])
    S = np.sin(om[:, None] * k[None, :])
    cs1 = np.concatenate([C, S, C, S], axis=0).astype(BF)
    cs2 = np.concatenate([S, C, S, C], axis=0).astype(BF)
    phi = np.concatenate([C, S], axis=0)
    phi2 = np.concatenate([phi, phi], axis=0).astype(BF)
    sgn = np.concatenate([-np.ones(32), np.ones(32), -np.ones(32),
                          np.ones(32)]).astype(np.float32)[:, None]
    i = np.arange(128)[:, None]
    j = np.arange(512)[None, :]
    masks = np.concatenate(
        [(j >= i + d*128).astype(np.float32) for d in range(4)], axis=1).astype(BF)
    _CONSTS = {
        "ident": np.eye(128).astype(BF),
        "cs1": np.ascontiguousarray(cs1), "cs2": np.ascontiguousarray(cs2),
        "phi2": np.ascontiguousarray(phi2), "sgnv": sgn, "masks": masks,
    }
    return _CONSTS


def kernel(**inputs):
    from concourse.bass_utils import run_bass_kernel_spmd
    nc = build_program()
    in_maps = [make_core_inputs(c, inputs) for c in range(8)]
    res = run_bass_kernel_spmd(nc, in_maps, core_ids=list(range(8)))
    out = np.empty((4 * T, D), dtype=np.float32)
    for c in range(8):
        b, hg = c // 2, c % 2
        out[b*T + hg*NTOK: b*T + (hg+1)*NTOK] = res.results[c]["out"]
    return out.reshape(4, T, D)


# version 06
# speedup vs baseline: 1.1890x; 1.0790x over previous
import numpy as np
import ml_dtypes
from contextlib import ExitStack

import concourse.bass as bass
import concourse.mybir as mybir

F32 = mybir.dt.float32
BF16 = mybir.dt.bfloat16
AF = mybir.ActivationFunctionType
ALU = mybir.AluOpType
AXX = mybir.AxisListType.X

D = 1024
KC = 8
NTOK = 2048
TT = 16
TB = 4
NE = 32
EI = 768
EIC = 6
ISH = 2048
T = 4096
NH = 8
QB = 8
EPS = 1e-5

ENGS = ("te", "dv", "ac", "sp", "gp")
BF = ml_dtypes.bfloat16


class Prog:
    def __init__(self, nc):
        self.nc = nc
        self.q = {e: [] for e in ENGS}
        self.sems = {}
        self.cnt = {}
        self.ctx = ExitStack()
        for s in ("TE", "DV", "AC", "GPD", "SPD", "AGS", "RSS"):
            self.sems[s] = self.ctx.enter_context(nc.semaphore(s))
            self.cnt[s] = 0

    def op(self, eng, fn, inc=None, k=1):
        if inc is None:
            self.q[eng].append(lambda e, fn=fn: fn(e))
            return None
        self.cnt[inc] += k
        s = self.sems[inc]
        self.q[eng].append(lambda e, fn=fn, s=s, k=k: fn(e).then_inc(s, k))
        return self.cnt[inc]

    def wait(self, eng, sem, val):
        if val is None or val <= 0:
            return
        self.q[eng].append(lambda e, s=self.sems[sem], v=val: e.wait_ge(s, v))

    def fence_te(self):
        self.wait("te", "DV", self.cnt["DV"])
        self.wait("te", "AC", self.cnt["AC"])

    def fence_dma(self, eng):
        for s in ("TE", "DV", "AC"):
            self.wait(eng, s, self.cnt[s])

    def flush(self, block):
        def runner(fns):
            def go(e):
                for f in fns:
                    f(e)
            return go
        block.tensor(runner(self.q["te"]))
        block.vector(runner(self.q["dv"]))
        block.scalar(runner(self.q["ac"]))
        block.sync(runner(self.q["sp"]))
        block.gpsimd(runner(self.q["gp"]))


def r128(dram_ap):
    return dram_ap.rearrange("(c p) n -> p c n", p=128)


def emit_ln(p, x, ln, xlnT, ident, tpA, tpB, ln_fence=True, x_ready=None):
    if ln_fence:
        p.wait("ac", "DV", p.cnt["DV"])
    for tt in range(TT):
        if x_ready is not None:
            p.wait("dv", x_ready[tt][0], x_ready[tt][1])
            p.wait("ac", x_ready[tt][0], x_ready[tt][1])
        p.op("dv", lambda e, tt=tt: e.reduce_sum(
            ln["sums"][:, tt:tt+1], x[:, tt*1024:(tt+1)*1024], axis=AXX),
            inc="DV" if tt == TT - 1 else None)
    p.wait("dv", "DV", p.cnt["DV"])
    v_nm = p.op("dv", lambda e: e.tensor_scalar(
        ln["nm"][:, 0:TT], ln["sums"][:, 0:TT], -1.0 / D, None, ALU.mult), inc="DV")
    p.wait("ac", "DV", v_nm)
    v_red = {}
    for tt in range(TT):
        slot = tt % 2
        if tt >= 2:
            p.wait("ac", "DV", v_red[tt - 2])
        v_sq = p.op("ac", lambda e, tt=tt, slot=slot: e.activation(
            ln["sq"][:, slot*1024:(slot+1)*1024], x[:, tt*1024:(tt+1)*1024],
            AF.Square, bias=ln["nm"][:, tt:tt+1]), inc="AC")
        p.wait("dv", "AC", v_sq)
        v_red[tt] = p.op("dv", lambda e, tt=tt, slot=slot: e.reduce_sum(
            ln["vsums"][:, tt:tt+1], ln["sq"][:, slot*1024:(slot+1)*1024],
            axis=AXX), inc="DV")
    p.wait("ac", "DV", v_red[TT - 1])
    v_rstd = p.op("ac", lambda e: e.activation(
        ln["rstd"][:, 0:TT], ln["vsums"][:, 0:TT], AF.Sqrt,
        scale=1.0 / D, bias=ln["eps"][:, 0:1]), inc="AC")
    p.wait("dv", "AC", v_rstd)
    v_rcp = p.op("dv", lambda e: e.reciprocal(
        ln["rstd2"][:, 0:TT], ln["rstd"][:, 0:TT]), inc="DV")
    p.wait("dv", "DV", v_rcp)
    v_cp = {}
    v_tr_tile = {}
    for tt in range(TT):
        slot = tt % 2
        if tt >= 2:
            p.wait("dv", "TE", v_tr_tile[tt - 2])
        v_xln = p.op("dv", lambda e, tt=tt, slot=slot: e.tensor_scalar(
            ln["xln"][:, slot*1024:(slot+1)*1024], x[:, tt*1024:(tt+1)*1024],
            ln["nm"][:, tt:tt+1], ln["rstd2"][:, tt:tt+1], ALU.add, ALU.mult),
            inc="DV")
        p.wait("te", "DV", v_xln)
        for kc in range(KC):
            tp = tpA if kc % 2 == 0 else tpB
            prev = (tt, kc - 2) if kc >= 2 else ((tt - 1, KC + kc - 2) if tt > 0 else None)
            if prev is not None:
                p.wait("te", "DV", v_cp[prev])
            v_t = p.op("te", lambda e, tt=tt, kc=kc, slot=slot, tp=tp: e.transpose(
                tp[:], ln["xln"][:, slot*1024 + kc*128: slot*1024 + (kc+1)*128],
                ident[:]), inc="TE")
            p.wait("dv", "TE", v_t)
            v_cp[(tt, kc)] = p.op("dv", lambda e, tt=tt, kc=kc, tp=tp: e.tensor_copy(
                xlnT[:, kc*NTOK + tt*128: kc*NTOK + (tt+1)*128], tp[:]), inc="DV")
        v_tr_tile[tt] = p.cnt["TE"]
    return p.cnt["DV"]


def emit_router(p, b, v_xlnT_dv, v_w_gpd):
    nc = p.nc
    xlnT, rw, sgw = b["xlnT"], b["rw"], b["sgw"]
    with ExitStack() as ps:
        import itertools as _it
        _rid = getattr(emit_router, "_n", 0); emit_router._n = _rid + 1
        lg = ps.enter_context(nc.psum_tensor(f"lg{_rid}", [128, TT*32], F32))
        sg = ps.enter_context(nc.psum_tensor(f"sg{_rid}", [128, TT], F32))
        p.fence_te()
        p.wait("te", "DV", v_xlnT_dv)
        p.wait("te", "GPD", v_w_gpd)
        for tt in range(TT):
            for kc in range(KC):
                p.op("te", lambda e, tt=tt, kc=kc: e.matmul(
                    lg[:, tt*32:(tt+1)*32],
                    xlnT[:, kc*NTOK + tt*128: kc*NTOK + (tt+1)*128],
                    rw[:, kc*32:(kc+1)*32],
                    start=(tt == 0 and kc == 0), stop=(tt == TT-1 and kc == KC-1)))
        for tt in range(TT):
            for kc in range(KC):
                last = (tt == TT-1 and kc == KC-1)
                p.op("te", lambda e, tt=tt, kc=kc: e.matmul(
                    sg[:, tt:tt+1],
                    xlnT[:, kc*NTOK + tt*128: kc*NTOK + (tt+1)*128],
                    sgw[:, kc:kc+1],
                    start=(tt == 0 and kc == 0), stop=last),
                    inc="TE" if last else None)
        v_te = p.cnt["TE"]
        p.wait("dv", "TE", v_te)
        for tt in range(TT):
            p.op("dv", lambda e, tt=tt: e.reduce_max(
                b["mx"][:, tt:tt+1], lg[:, tt*32:(tt+1)*32], axis=AXX),
                inc="DV" if tt == TT - 1 else None)
        p.wait("dv", "DV", p.cnt["DV"])
        v_nmx = p.op("dv", lambda e: e.tensor_scalar(
            b["nmx"][:, 0:TT], b["mx"][:, 0:TT], -1.0, None, ALU.mult), inc="DV")
        p.wait("ac", "DV", v_nmx)
        for tt in range(TT):
            p.op("ac", lambda e, tt=tt: e.activation(
                b["expb"][:, tt*32:(tt+1)*32], lg[:, tt*32:(tt+1)*32], AF.Exp,
                bias=b["nmx"][:, tt:tt+1]))
        v_sig = p.op("ac", lambda e: e.activation(
            b["sig"][:, 0:TT], sg[:, 0:TT], AF.Sigmoid), inc="AC")
        p.wait("dv", "AC", v_sig)
        v_g2 = p.op("dv", lambda e: e.tensor_scalar(
            b["gate2"][:, 0:TT], b["sig"][:, 0:TT], 0.5, None, ALU.mult), inc="DV")
        expb, cur, wbuf, mb, comb = b["expb"], b["cur"], b["wbuf"], b["mb"], b["combine"]
        src = expb
        for it in range(4):
            for tt in range(TT):
                p.op("dv", lambda e, tt=tt, it=it, src=src: e.reduce_max(
                    mb[:, it*TT + tt: it*TT + tt + 1], src[:, tt*32:(tt+1)*32],
                    axis=AXX), inc="DV" if tt == TT - 1 else None)
            p.wait("dv", "DV", p.cnt["DV"])
            if it < 3:
                for tt in range(TT):
                    p.op("dv", lambda e, tt=tt, it=it, src=src: e.scalar_tensor_tensor(
                        cur[:, tt*32:(tt+1)*32], src[:, tt*32:(tt+1)*32],
                        mb[:, it*TT + tt: it*TT + tt + 1], src[:, tt*32:(tt+1)*32],
                        ALU.is_lt, ALU.mult), inc="DV" if tt == TT - 1 else None)
                p.wait("dv", "DV", p.cnt["DV"])
                src = cur
        for tt in range(TT):
            p.op("dv", lambda e, tt=tt: e.scalar_tensor_tensor(
                wbuf[:, tt*32:(tt+1)*32], expb[:, tt*32:(tt+1)*32],
                mb[:, 3*TT + tt: 3*TT + tt + 1], expb[:, tt*32:(tt+1)*32],
                ALU.is_ge, ALU.mult), inc="DV" if tt == TT - 1 else None)
        p.wait("dv", "DV", p.cnt["DV"])
        for tt in range(TT):
            p.op("dv", lambda e, tt=tt: e.reduce_sum(
                b["s4"][:, tt:tt+1], wbuf[:, tt*32:(tt+1)*32], axis=AXX),
                inc="DV" if tt == TT - 1 else None)
        p.wait("dv", "DV", p.cnt["DV"])
        v = p.op("dv", lambda e: e.tensor_scalar(
            b["s8"][:, 0:TT], b["s4"][:, 0:TT], 2.0, None, ALU.mult), inc="DV")
        p.wait("dv", "DV", v)
        v = p.op("dv", lambda e: e.reciprocal(b["r2"][:, 0:TT], b["s8"][:, 0:TT]),
                 inc="DV")
        p.wait("dv", "DV", v)
        for tt in range(TT):
            p.op("dv", lambda e, tt=tt: e.tensor_scalar(
                comb[:, tt*32:(tt+1)*32], wbuf[:, tt*32:(tt+1)*32],
                b["r2"][:, tt:tt+1], None, ALU.mult),
                inc="DV" if tt == TT - 1 else None)
    return {"combine_dv": p.cnt["DV"], "gate2_dv": v_g2,
            "lg_reader_ac": v_sig, "lg_reader_dv": v_nmx}


def emit_gu_act(p, st, lw_g, lw_u, xlnT, gpsl, upsl, act_dst, silu_sb, tok_off):
    i = st.get("gu_rot", 0) & 1
    st["gu_rot"] = i ^ 1
    gps, ups = gpsl[i], upsl[i]
    busy = st.get(("gu_busy", i))
    if busy is not None:
        p.wait("te", "AC", busy[0])
        p.wait("te", "DV", busy[1])
    for kc in range(KC):
        p.op("te", lambda e, kc=kc, gps=gps: e.matmul(
            gps[:], lw_g(kc), xlnT[:, kc*NTOK + tok_off: kc*NTOK + tok_off + 512],
            start=(kc == 0), stop=(kc == KC - 1)),
            inc="TE" if kc == KC - 1 else None)
    v_g = p.cnt["TE"]
    for kc in range(KC):
        p.op("te", lambda e, kc=kc, ups=ups: e.matmul(
            ups[:], lw_u(kc), xlnT[:, kc*NTOK + tok_off: kc*NTOK + tok_off + 512],
            start=(kc == 0), stop=(kc == KC - 1)),
            inc="TE" if kc == KC - 1 else None)
    v_u = p.cnt["TE"]
    slot = st["silu_parity"]
    st["silu_parity"] ^= 1
    p.wait("ac", "TE", v_g)
    if st.get(("silu_free", slot)):
        p.wait("ac", "DV", st[("silu_free", slot)])
    v_silu = p.op("ac", lambda e, slot=slot, gps=gps: e.activation(
        silu_sb[:, slot*512:(slot+1)*512], gps[:], AF.Silu), inc="AC")
    p.wait("dv", "AC", v_silu)
    p.wait("dv", "TE", v_u)
    v_act = p.op("dv", lambda e, slot=slot, ups=ups: e.tensor_tensor(
        act_dst, silu_sb[:, slot*512:(slot+1)*512], ups[:], ALU.mult), inc="DV")
    st[("gu_busy", i)] = (v_silu, v_act)
    st[("silu_free", slot)] = v_act
    return v_act


def emit_ffn(p, b, drams, x, call):
    nc = p.nc
    xlnT = b["xlnT"]
    if call == 0:
        p.fence_dma("gp")
        p.op("gp", lambda e: e.dma_start(
            out=b["rw"].ap().rearrange("p (c n) -> p c n", c=KC),
            in_=r128(drams["router_w"].ap())), inc="GPD", k=16)
        p.op("gp", lambda e: e.dma_start(
            out=b["sgw"].ap(), in_=drams["sh_gate_w"][:, :]), inc="GPD", k=16)
        b["v_wsmall"] = p.cnt["GPD"]
    def load_sh(hf):
        for nm, buf in (("sh_gate", b["shg"]), ("sh_up", b["shu"])):
            p.op("gp", lambda e, nm=nm, buf=buf, hf=hf: e.dma_start(
                out=buf.ap().rearrange("p (c n) -> p c n", c=KC),
                in_=r128(drams[nm][:, hf*1024:(hf+1)*1024])), inc="GPD", k=16)
        vA = p.cnt["GPD"]
        p.op("gp", lambda e, hf=hf: e.dma_start(
            out=b["shd"].ap().rearrange("p (c n) -> p c n", c=KC),
            in_=r128(drams["sh_down"][hf*1024:(hf+1)*1024, :])), inc="GPD", k=16)
        return (vA, p.cnt["GPD"])

    def load_expert(ei, sl):
        for nm, buf, cs, w in (("w_gate", b["wg"], KC, EI),
                               ("w_up", b["wu"], KC, EI),
                               ("w_down", b["wd"], EIC, 1024)):
            p.op("gp", lambda e, nm=nm, buf=buf, ei=ei, sl=sl, cs=cs, w=w: e.dma_start(
                out=buf[:, sl*cs*w:(sl+1)*cs*w].rearrange("p (c n) -> p c n", c=cs),
                in_=r128(drams[nm][ei])), inc="GPD", k=16)
        return p.cnt["GPD"]

    with ExitStack() as ps0:
        tpA = ps0.enter_context(nc.psum_tensor(f"tpA{call}", [128, 128], BF16))
        tpB = ps0.enter_context(nc.psum_tensor(f"tpB{call}", [128, 128], BF16))
        p.fence_te()
        v_xlnT = emit_ln(p, x, b["ln"], xlnT, b["ident"], tpA, tpB,
                         x_ready=b.get("x_ready"))
    rinfo = emit_router(p, b, v_xlnT, b["v_wsmall"])
    x_writable_dv = max(rinfo["combine_dv"], rinfo["gate2_dv"])

    with ExitStack() as ps:
        gpsl = [ps.enter_context(nc.psum_tensor(f"gps{call}_{i}", [128, 512], F32))
                for i in range(2)]
        upsl = [ps.enter_context(nc.psum_tensor(f"ups{call}_{i}", [128, 512], F32))
                for i in range(2)]
        routed = [ps.enter_context(nc.psum_tensor(f"rt{call}_{i}", [128, 1024], F32))
                  for i in range(2)]
        st = {"silu_parity": 0}
        pair_free_dv = {0: None, 1: None}
        ts_ctr = 0
        first_mm_fenced = [False]

        def down_group(lhs_fn, rhs_fn, tt, scalar_ap):
            nonlocal ts_ctr
            pr = ts_ctr % 2
            ts_ctr += 1
            if pair_free_dv[pr] is not None:
                p.wait("te", "DV", pair_free_dv[pr])
            nic = lhs_fn(None)
            for ic in range(nic):
                for nh2 in range(2):
                    last = (ic == nic - 1 and nh2 == 1)
                    p.op("te", lambda e, ic=ic, nh2=nh2, pr=pr: e.matmul(
                        routed[pr][:, nh2*512:(nh2+1)*512],
                        lhs_fn(ic), rhs_fn(ic, nh2),
                        start=(ic == 0), stop=(ic == nic - 1)),
                        inc="TE" if last else None)
            v_te = p.cnt["TE"]
            p.wait("dv", "TE", v_te)
            if not first_mm_fenced[0]:
                p.wait("dv", "DV", x_writable_dv)
                first_mm_fenced[0] = True
            pair_free_dv[pr] = p.op("dv", lambda e, tt=tt, pr=pr, s=scalar_ap: e.scalar_tensor_tensor(
                x[:, tt*1024:(tt+1)*1024], routed[pr][:], s,
                x[:, tt*1024:(tt+1)*1024], ALU.mult, ALU.add), inc="DV")

        with ExitStack() as shctx:
            alloc_ffn_shared(nc, shctx, b, f"f{call}")
            shg, shu, shd, actsh = b["shg"], b["shu"], b["shd"], b["actsh"]
            sh_gpd = [load_sh(0)]
            for hf in range(2):
                p.fence_te()
                p.wait("te", "GPD", sh_gpd[hf][0])
                for icc in range(KC):
                    for tb in range(TB):
                        emit_gu_act(
                            p, st,
                            (lambda kc, icc=icc: shg[:, kc*1024 + icc*128: kc*1024 + (icc+1)*128]),
                            (lambda kc, icc=icc: shu[:, kc*1024 + icc*128: kc*1024 + (icc+1)*128]),
                            xlnT, gpsl, upsl,
                            actsh[:, icc*NTOK + tb*512: icc*NTOK + (tb+1)*512],
                            b["silu_sb"], tb*512)
                v_actsh_dv = p.cnt["DV"]
                p.wait("te", "DV", v_actsh_dv)
                p.wait("te", "GPD", sh_gpd[hf][1])
                for tt in range(TT):
                    down_group(
                        (lambda ic, tt=tt: KC if ic is None else
                         actsh[:, ic*NTOK + tt*128: ic*NTOK + (tt+1)*128]),
                        (lambda ic, nh2: shd[:, ic*1024 + nh2*512: ic*1024 + (nh2+1)*512]),
                        tt, b["gate2"][:, tt:tt+1])
                if hf == 0:
                    p.wait("gp", "TE", p.cnt["TE"])
                    sh_gpd.append(load_sh(1))
            v_te_sh, v_dv_sh = p.cnt["TE"], p.cnt["DV"]

        alloc_ffn_experts(nc, ps, b, f"f{call}")
        wg, wu, wd, act_sb = b["wg"], b["wu"], b["wd"], b["act_sb"]
        p.wait("gp", "TE", v_te_sh)
        p.wait("gp", "DV", v_dv_sh)
        for ei in range(2):
            b["ew_gpd"][ei] = load_expert(ei, ei % 2)
        act_slot_reader_te = {0: None, 1: None}
        for ei in range(NE):
            sl = ei % 2
            p.wait("te", "GPD", b["ew_gpd"][ei])
            for tb in range(TB):
                asl = tb % 2
                if act_slot_reader_te[asl] is not None:
                    p.wait("dv", "TE", act_slot_reader_te[asl])
                for ic in range(EIC):
                    emit_gu_act(
                        p, st,
                        (lambda kc, ic=ic, sl=sl: wg[:, sl*KC*EI + kc*EI + ic*128: sl*KC*EI + kc*EI + (ic+1)*128]),
                        (lambda kc, ic=ic, sl=sl: wu[:, sl*KC*EI + kc*EI + ic*128: sl*KC*EI + kc*EI + (ic+1)*128]),
                        xlnT, gpsl, upsl,
                        act_sb[:, asl*EIC*512 + ic*512: asl*EIC*512 + (ic+1)*512],
                        b["silu_sb"], tb*512)
                v_act_dv = p.cnt["DV"]
                p.wait("te", "DV", v_act_dv)
                for ts4 in range(4):
                    tt = tb*4 + ts4
                    down_group(
                        (lambda ic, ts4=ts4, asl=asl: EIC if ic is None else
                         act_sb[:, asl*EIC*512 + ic*512 + ts4*128: asl*EIC*512 + ic*512 + (ts4+1)*128]),
                        (lambda ic, nh2, sl=sl: wd[:, sl*EIC*1024 + ic*1024 + nh2*512: sl*EIC*1024 + ic*1024 + (nh2+1)*512]),
                        tt, b["combine"][:, tt*32 + ei: tt*32 + ei + 1])
                act_slot_reader_te[asl] = p.cnt["TE"]
            if ei + 2 < NE:
                p.wait("gp", "TE", p.cnt["TE"])
                b["ew_gpd"][ei + 2] = load_expert(ei + 2, (ei + 2) % 2)
    return {"x_dv": p.cnt["DV"]}


def alloc_ffn_bufs(nc, ctx, b, tag=""):
    sb = lambda n, shape, dt=F32: ctx.enter_context(nc.sbuf_tensor(n + tag, shape, dt))
    b["xlnT"] = sb("xlnT", [128, KC*NTOK], BF16)
    ln = {}
    ln["sums"] = sb("ln_sums", [128, TT])
    ln["nm"] = sb("ln_nm", [128, TT])
    ln["vsums"] = sb("ln_vsums", [128, TT])
    ln["rstd"] = sb("ln_rstd", [128, TT])
    ln["rstd2"] = sb("ln_rstd2", [128, TT])
    ln["sq"] = sb("ln_sq", [128, 2*1024])
    ln["xln"] = sb("ln_xln", [128, 2*1024], BF16)
    ln["eps"] = b["eps"]
    b["ln"] = ln
    b["rw"] = sb("rw", [128, KC*32], BF16)
    b["sgw"] = sb("sgw", [128, KC], BF16)
    for n in ("mx", "nmx", "sig", "gate2", "s4", "s8", "r2"):
        b[n] = sb(n, [128, TT])
    for n in ("expb", "cur", "wbuf", "combine"):
        b[n] = sb(n, [128, TT*32])
    b["mb"] = sb("mb", [128, 4*TT])
    b["silu_sb"] = sb("silu_sb", [128, 2*512], BF16)
    return b


def alloc_ffn_shared(nc, ctx, b, tag):
    sb = lambda n, shape, dt=BF16: ctx.enter_context(nc.sbuf_tensor(n + tag, shape, dt))
    b["shg"] = sb("shg", [128, KC*1024])
    b["shu"] = sb("shu", [128, KC*1024])
    b["shd"] = sb("shd", [128, KC*1024])
    b["actsh"] = sb("actsh", [128, KC*NTOK])


def alloc_ffn_experts(nc, ctx, b, tag):
    sb = lambda n, shape, dt=BF16: ctx.enter_context(nc.sbuf_tensor(n + tag, shape, dt))
    b["wg"] = sb("wg", [128, 2*KC*EI])
    b["wu"] = sb("wu", [128, 2*KC*EI])
    b["wd"] = sb("wd", [128, 2*EIC*1024])
    b["act_sb"] = sb("act_sb", [128, 2*EIC*512])
    b["ew_gpd"] = [None]*NE


def emit_attention(p, nc, dr, ident, v_agin):
    GROUPS = [[0, 1], [2, 3], [4, 5], [6, 7]]
    with ExitStack() as actx:
        k_aug = actx.enter_context(nc.sbuf_tensor("k_aug", [128, NH*T], BF16))
        qT = actx.enter_context(nc.sbuf_tensor("qT", [128, 4*T], BF16))
        v_sb = actx.enter_context(nc.sbuf_tensor("v_sb", [128, 32*520], BF16))
        p.wait("gp", "SPD", v_agin)
        p.op("gp", lambda e: e.collective_compute(
            "AllGather", ALU.bypass, replica_groups=GROUPS,
            ins=[dr["ag_in"].ap().opt()], outs=[dr["ag_out"].ap().opt()]),
            inc="AGS", k=1)
        p.wait("dv", "TE", p.cnt["TE"])
        p.wait("dv", "AC", p.cnt["AC"])
        v_ones = p.op("dv", lambda e: e.memset(v_sb[:], 1.0), inc="DV")
        ones1_hook = []
        with ExitStack() as qctx:
            xch = qctx.enter_context(nc.sbuf_tensor("xch", [128, 2*KC*512], BF16))
            wqkv = qctx.enter_context(nc.sbuf_tensor("wqkv", [128, KC*1536], BF16))
            p.fence_dma("gp")
            p.op("gp", lambda e: e.dma_start(
                out=wqkv.ap().rearrange("p (c n) -> p c n", c=KC),
                in_=r128(dr["wqkvT"].ap())), inc="GPD", k=16)
            v_wqkv = p.cnt["GPD"]
            with ExitStack() as pctx:
                qkvps = [pctx.enter_context(nc.psum_tensor(f"qkvps{i}", [128, 512], F32))
                         for i in range(2)]
                p.fence_dma("sp")
                p.wait("sp", "AGS", 1)
                p.fence_te()
                p.wait("te", "GPD", v_wqkv)
                agv = dr["ag_out"].ap().rearrange("(r c p) t -> p r c t", p=128, c=KC)
                ps_free = {0: None, 1: None}
                chunk_reader_te = {0: None, 1: None}
                pp = [0]

                def grp(slx, lhs_fn, rhs_fn, evac_fn):
                    pi = pp[0] % 2
                    pp[0] += 1
                    ps = qkvps[pi]
                    if ps_free[pi] is not None:
                        p.wait("te", "DV", ps_free[pi])
                    for kc in range(KC):
                        p.op("te", lambda e, kc=kc, ps=ps: e.matmul(
                            ps[:], lhs_fn(kc), rhs_fn(kc),
                            start=(kc == 0), stop=(kc == KC - 1)),
                            inc="TE" if kc == KC - 1 else None)
                    p.wait("dv", "TE", p.cnt["TE"])
                    ps_free[pi] = evac_fn(ps)

                def xch_rhs(slx):
                    return lambda kc, slx=slx: xch[:, slx*KC*512 + kc*512: slx*KC*512 + (kc+1)*512]

                for tbg in range(QB):
                    slx = tbg % 2
                    if chunk_reader_te[slx] is not None:
                        p.wait("sp", "TE", chunk_reader_te[slx])
                    p.op("sp", lambda e, tbg=tbg, slx=slx: e.dma_start(
                        out=xch[:, slx*KC*512:(slx+1)*KC*512].rearrange(
                            "p (c n) -> p c n", c=KC),
                        in_=agv[:, tbg // TB, :, (tbg % TB)*512:(tbg % TB + 1)*512]),
                        inc="SPD", k=16)
                    p.wait("te", "SPD", p.cnt["SPD"])
                    for p4 in range(4):
                        grp(slx,
                            lambda kc, p4=p4: wqkv[:, kc*1536 + p4*128: kc*1536 + (p4+1)*128],
                            xch_rhs(slx),
                            lambda ps, p4=p4, tbg=tbg: p.op(
                                "dv", lambda e, ps=ps, p4=p4, tbg=tbg: e.tensor_scalar(
                                    qT[:, p4*T + tbg*512: p4*T + (tbg+1)*512],
                                    ps[:], 0.125, None, ALU.mult), inc="DV"))
                    for p4 in range(4):
                        def kev(ps, p4=p4, tbg=tbg):
                            p.op("dv", lambda e, ps=ps, p4=p4, tbg=tbg: e.tensor_copy(
                                k_aug[0:64, (2*p4)*T + tbg*512: (2*p4)*T + (tbg+1)*512],
                                ps[0:64, :]))
                            return p.op("dv", lambda e, ps=ps, p4=p4, tbg=tbg: e.tensor_copy(
                                k_aug[64:128, (2*p4+1)*T + tbg*512: (2*p4+1)*T + (tbg+1)*512],
                                ps[64:128, :]), inc="DV")
                        grp(slx,
                            lambda kc, p4=p4: wqkv[:, kc*1536 + 512 + p4*128: kc*1536 + 512 + (p4+1)*128],
                            xch_rhs(slx),
                            kev)
                    for ts4 in range(4):
                        kt = tbg*4 + ts4
                        def vev(ps, kt=kt):
                            p.wait("dv", "DV", v_ones)
                            return p.op("dv", lambda e, ps=ps, kt=kt: e.tensor_copy(
                                v_sb[:, kt*520: kt*520 + 520].rearrange(
                                    "p (h x) -> p h x", h=NH)[:, :, 0:64],
                                ps[:].rearrange("p (h x) -> p h x", h=NH)), inc="DV")
                        grp(slx,
                            lambda kc, ts4=ts4, slx=slx: xch[:, slx*KC*512 + kc*512 + ts4*128: slx*KC*512 + kc*512 + (ts4+1)*128],
                            lambda kc: wqkv[:, kc*1536 + 1024: kc*1536 + 1536],
                            vev)
                    chunk_reader_te[slx] = p.cnt["TE"]
        with ExitStack() as psictx:
            cs1 = psictx.enter_context(nc.sbuf_tensor("cs1sb", [128, T], BF16))
            cs2 = psictx.enter_context(nc.sbuf_tensor("cs2sb", [128, T], BF16))
            tpsi = psictx.enter_context(nc.sbuf_tensor("tpsi", [128, T], BF16))
            a4 = psictx.enter_context(nc.sbuf_tensor("a4sb", [128, NH], F32))
            b4 = psictx.enter_context(nc.sbuf_tensor("b4sb", [128, NH], F32))
            a2 = psictx.enter_context(nc.sbuf_tensor("a2sb", [128, NH], F32))
            sgn = psictx.enter_context(nc.sbuf_tensor("sgnsb", [128, 1], F32))
            p.fence_dma("sp")
            for nm, buf in (("cs1", cs1), ("cs2", cs2), ("a4", a4), ("b4", b4),
                            ("sgnv", sgn)):
                p.op("sp", lambda e, nm=nm, buf=buf: e.dma_start(
                    out=buf[:], in_=dr[nm][:, :]), inc="SPD", k=16)
            v_csld = p.cnt["SPD"]
            p.wait("dv", "SPD", v_csld)
            v_a2 = p.op("dv", lambda e: e.tensor_scalar(
                a2[:], b4[:], sgn[:, 0:1], None, ALU.mult), inc="DV")
            p.wait("dv", "DV", v_a2)
            for h in range(NH):
                Bb = (1 - (h % 2)) * 64
                v_t = p.op("dv", lambda e, h=h: e.tensor_scalar(
                    tpsi[:], cs1[:], a4[:, h:h+1], None, ALU.mult), inc="DV")
                p.wait("dv", "DV", v_t)
                p.op("dv", lambda e, h=h, Bb=Bb: e.scalar_tensor_tensor(
                    k_aug[Bb:Bb+64, h*T:(h+1)*T], cs2[Bb:Bb+64, :],
                    a2[Bb:Bb+64, h:h+1], tpsi[Bb:Bb+64, :], ALU.mult, ALU.add),
                    inc="DV")
            v_psi = p.cnt["DV"]
        with ExitStack() as fctx:
            phi2 = fctx.enter_context(nc.sbuf_tensor("phi2sb", [128, T], BF16))
            masks = fctx.enter_context(nc.sbuf_tensor("maskssb", [128, 4*512], BF16))
            wout = fctx.enter_context(nc.sbuf_tensor("woutsb", [64, NH*1024], BF16))
            q_aug = fctx.enter_context(nc.sbuf_tensor("q_aug", [128, 3*512], BF16))
            P_sb = fctx.enter_context(nc.sbuf_tensor("P_sb", [128, 3*512], BF16))
            OT_sb = fctx.enter_context(nc.sbuf_tensor("OT_sb", [64, NH*512], BF16))
            rcp_sb = fctx.enter_context(nc.sbuf_tensor("rcp_sb", [128, 512], F32))
            ones1 = fctx.enter_context(nc.sbuf_tensor("ones1", [128, 64], F32))
            bc_sb = fctx.enter_context(nc.sbuf_tensor("bc_sb", [64, 2*512], F32))
            yst = fctx.enter_context(nc.sbuf_tensor("yst", [128, 2*1024], BF16))
            p.fence_dma("sp")
            for nm, buf in (("phi2", phi2), ("masks", masks)):
                p.op("sp", lambda e, nm=nm, buf=buf: e.dma_start(
                    out=buf[:], in_=dr[nm][:, :]), inc="SPD", k=16)
            v_fl = p.cnt["SPD"]
            p.fence_dma("gp")
            p.op("gp", lambda e: e.dma_start(
                out=wout.ap().rearrange("p (c n) -> p c n", c=NH),
                in_=dr["woutT"].ap().rearrange("(c p) n -> p c n", p=64)), inc="GPD", k=16)
            v_wout = p.cnt["GPD"]
            with ExitStack() as pctx:
                Sps = [pctx.enter_context(nc.psum_tensor(f"Sps{i}", [128, 512], F32))
                       for i in range(3)]
                Ops = [pctx.enter_context(nc.psum_tensor(f"Ops{i}", [65, 512], F32))
                       for i in range(2)]
                bcps = pctx.enter_context(nc.psum_tensor("bcps", [64, 512], F32))
                yps = pctx.enter_context(nc.psum_tensor("yps", [128, 1024], F32))
                p.wait("dv", "TE", p.cnt["TE"])
                p.wait("dv", "AC", p.cnt["AC"])
                v_ones1 = p.op("dv", lambda e: e.memset(ones1[:], 1.0), inc="DV")
                p.fence_te()
                p.wait("te", "DV", v_psi)
                p.wait("te", "DV", v_ones1)
                p.wait("te", "SPD", v_fl)
                p.wait("te", "GPD", v_wout)
                q_free_te = {0: None, 1: None, 2: None}
                S_free_ac = {0: None, 1: None, 2: None}
                P_free_te = {0: None, 1: None, 2: None}
                P_free_dv = {0: None, 1: None, 2: None}
                O_free_dv = {0: None, 1: None}
                bc_free_dv = {0: None, 1: None}
                rcp_reader_te = [None]
                yps_reader_dv = [None]
                ot_done = {}
                osb_reader_te = None
                yst_reader_spd = {0: None, 1: None}
                v_qa = {}

                def emit_qa(qb, h):
                    qsl = h % 3
                    pi = h % 2
                    if q_free_te[qsl] is not None:
                        p.wait("dv", "TE", q_free_te[qsl])
                    p.op("dv", lambda e, h=h, qb=qb, qsl=qsl, pi=pi: e.tensor_copy(
                        q_aug[pi*64:(pi+1)*64, qsl*512:(qsl+1)*512],
                        qT[pi*64:(pi+1)*64, (h//2)*T + qb*512: (h//2)*T + (qb+1)*512]))
                    v_qa[(qb, h)] = p.op("dv", lambda e, qb=qb, qsl=qsl, pi=pi: e.tensor_copy(
                        q_aug[(1-pi)*64:(2-pi)*64, qsl*512:(qsl+1)*512],
                        phi2[(1-pi)*64:(2-pi)*64, qb*512:(qb+1)*512]), inc="DV")

                for qb in range(QB):
                    if osb_reader_te is not None:
                        p.wait("dv", "TE", osb_reader_te)
                    emit_qa(qb, 0)
                    emit_qa(qb, 1)
                    for h in range(NH):
                        qsl = h % 3
                        ob = h % 2
                        if h + 2 < NH:
                            emit_qa(qb, h + 2)
                        p.wait("te", "DV", v_qa[(qb, h)])
                        if O_free_dv[ob] is not None:
                            p.wait("te", "DV", O_free_dv[ob])
                        nkt = qb*4 + 4
                        SKEW = 2
                        P_ready = {}
                        for it in range(nkt + SKEW):
                            kt = it
                            if kt < nkt:
                                ssl = kt % 3
                                if S_free_ac[ssl] is not None:
                                    p.wait("te", "AC", S_free_ac[ssl])
                                v_s = p.op("te", lambda e, h=h, kt=kt, ssl=ssl, qsl=qsl: e.matmul(
                                    Sps[ssl][:], k_aug[:, h*T + kt*128: h*T + (kt+1)*128],
                                    q_aug[:, qsl*512:(qsl+1)*512],
                                    start=True, stop=True), inc="TE")
                                p.wait("ac", "TE", v_s)
                                if P_free_te[ssl] is not None:
                                    p.wait("ac", "TE", P_free_te[ssl])
                                if P_free_dv[ssl] is not None:
                                    p.wait("ac", "DV", P_free_dv[ssl])
                                v_p = p.op("ac", lambda e, ssl=ssl: e.activation(
                                    P_sb[:, ssl*512:(ssl+1)*512], Sps[ssl][:], AF.Exp),
                                    inc="AC")
                                S_free_ac[ssl] = v_p
                                if kt >= qb*4:
                                    d = kt - qb*4
                                    p.wait("dv", "AC", v_p)
                                    v_m = p.op("dv", lambda e, ssl=ssl, d=d: e.tensor_tensor(
                                        P_sb[:, ssl*512:(ssl+1)*512],
                                        P_sb[:, ssl*512:(ssl+1)*512],
                                        masks[:, d*512:(d+1)*512], ALU.mult), inc="DV")
                                    P_free_dv[ssl] = v_m
                                    P_ready[kt] = ("DV", v_m)
                                else:
                                    P_free_dv[ssl] = None
                                    P_ready[kt] = ("AC", v_p)
                            ka = it - SKEW
                            if 0 <= ka < nkt:
                                ssl2 = ka % 3
                                if ka == 0 and O_free_dv[ob] is not None:
                                    p.wait("te", "DV", O_free_dv[ob])
                                p.wait("te", P_ready[ka][0], P_ready[ka][1])
                                p.op("te", lambda e, ka=ka, ssl2=ssl2, ob=ob, h=h: e.matmul(
                                    Ops[ob][:],
                                    v_sb[:, ka*520 + h*65: ka*520 + (h+1)*65],
                                    P_sb[:, ssl2*512:(ssl2+1)*512],
                                    start=(ka == 0), stop=(ka == nkt - 1)),
                                    inc="TE" if ka == nkt - 1 else None)
                                P_free_te[ssl2] = p.cnt["TE"]
                        q_free_te[qsl] = p.cnt["TE"]
                        v_av = p.cnt["TE"]
                        p.wait("dv", "TE", v_av)
                        if rcp_reader_te[0] is not None:
                            p.wait("dv", "TE", rcp_reader_te[0])
                        v_rcp = p.op("dv", lambda e, ob=ob: e.reciprocal(
                            rcp_sb[64:65, :], Ops[ob][64:65, :]), inc="DV")
                        p.wait("te", "DV", v_rcp)
                        bsl = h % 2
                        if bc_free_dv[bsl] is not None:
                            p.wait("te", "DV", bc_free_dv[bsl])
                        v_bc = p.op("te", lambda e: e.matmul(
                            bcps[:], ones1[64:65, :], rcp_sb[64:65, :],
                            start=True, stop=True), inc="TE")
                        rcp_reader_te[0] = p.cnt["TE"]
                        p.wait("dv", "TE", v_bc)
                        v_bcc = p.op("dv", lambda e, bsl=bsl: e.tensor_copy(
                            bc_sb[:, bsl*512:(bsl+1)*512], bcps[:]), inc="DV")
                        p.wait("dv", "DV", v_bcc)
                        bc_ready_te = p.cnt["TE"]
                        v_ot = p.op("dv", lambda e, h=h, ob=ob, bsl=bsl: e.tensor_tensor(
                            OT_sb[:, h*512:(h+1)*512], Ops[ob][0:64, :],
                            bc_sb[:, bsl*512:(bsl+1)*512], ALU.mult), inc="DV")
                        O_free_dv[ob] = v_ot
                        bc_free_dv[bsl] = v_ot
                        ot_done[h] = v_ot
                    p.wait("te", "DV", ot_done[NH - 1])
                    for qs in range(4):
                        ysl = qs % 2
                        if yps_reader_dv[0] is not None:
                            p.wait("te", "DV", yps_reader_dv[0])
                        for h2 in range(NH):
                            for nh2 in range(2):
                                last = (h2 == NH - 1 and nh2 == 1)
                                p.op("te", lambda e, qs=qs, h2=h2, nh2=nh2: e.matmul(
                                    yps[:, nh2*512:(nh2+1)*512],
                                    OT_sb[:, h2*512 + qs*128: h2*512 + (qs+1)*128],
                                    wout[0:64, h2*1024 + nh2*512: h2*1024 + (nh2+1)*512],
                                    start=(h2 == 0), stop=(h2 == NH - 1)),
                                    inc="TE" if last else None)
                        v_y = p.cnt["TE"]
                        p.wait("dv", "TE", v_y)
                        if yst_reader_spd[ysl] is not None:
                            p.wait("dv", "SPD", yst_reader_spd[ysl])
                        v_ye = p.op("dv", lambda e, ysl=ysl: e.tensor_copy(
                            yst[:, ysl*1024:(ysl+1)*1024], yps[:]), inc="DV")
                        yps_reader_dv[0] = v_ye
                        p.wait("sp", "DV", v_ye)
                        row = (qb*512 + qs*128) // 128
                        p.op("sp", lambda e, row=row, ysl=ysl: e.dma_start(
                            out=dr["y_in"].ap().rearrange("(t p) n -> p t n", p=128)[:, row],
                            in_=yst[:, ysl*1024:(ysl+1)*1024]), inc="SPD", k=16)
                        yst_reader_spd[ysl] = p.cnt["SPD"]
                    osb_reader_te = p.cnt["TE"]
                    if qb == 5:
                        v_y_halfA = p.cnt["SPD"]
                v_yall = p.cnt["SPD"]
        p.wait("gp", "SPD", v_yall)
        p.op("gp", lambda e: e.collective_compute(
            "ReduceScatter", ALU.add, replica_groups=GROUPS,
            ins=[dr["y_in"].ap().opt()], outs=[dr["y_out"].ap().opt()]),
            inc="RSS", k=1)


def build_program():
    nc = bass.Bass("TRN2", target_bir_lowering=False)
    dr = {}
    dr["x_own"] = nc.dram_tensor("x_own", [NTOK, D], F32, kind="ExternalInput")
    dr["router_w"] = nc.dram_tensor("router_w", [D, 32], F32, kind="ExternalInput")
    dr["sh_gate_w"] = nc.dram_tensor("sh_gate_w", [128, KC], F32, kind="ExternalInput")
    dr["sh_gate"] = nc.dram_tensor("sh_gate", [D, ISH], F32, kind="ExternalInput")
    dr["sh_up"] = nc.dram_tensor("sh_up", [D, ISH], F32, kind="ExternalInput")
    dr["sh_down"] = nc.dram_tensor("sh_down", [ISH, D], F32, kind="ExternalInput")
    dr["w_gate"] = nc.dram_tensor("w_gate", [NE, D, EI], F32, kind="ExternalInput")
    dr["w_up"] = nc.dram_tensor("w_up", [NE, D, EI], F32, kind="ExternalInput")
    dr["w_down"] = nc.dram_tensor("w_down", [NE, EI, D], F32, kind="ExternalInput")
    dr["wqkvT"] = nc.dram_tensor("wqkvT", [D, 1536], F32, kind="ExternalInput")
    dr["woutT"] = nc.dram_tensor("woutT", [512, D], F32, kind="ExternalInput")
    dr["ident"] = nc.dram_tensor("ident", [128, 128], BF16, kind="ExternalInput")
    dr["cs1"] = nc.dram_tensor("cs1", [128, T], BF16, kind="ExternalInput")
    dr["cs2"] = nc.dram_tensor("cs2", [128, T], BF16, kind="ExternalInput")
    dr["phi2"] = nc.dram_tensor("phi2", [128, T], BF16, kind="ExternalInput")
    dr["a4"] = nc.dram_tensor("a4", [128, NH], F32, kind="ExternalInput")
    dr["b4"] = nc.dram_tensor("b4", [128, NH], F32, kind="ExternalInput")
    dr["sgnv"] = nc.dram_tensor("sgnv", [128, 1], F32, kind="ExternalInput")
    dr["masks"] = nc.dram_tensor("masks", [128, 4*512], BF16, kind="ExternalInput")
    dr["out"] = nc.dram_tensor("out", [NTOK, D], F32, kind="ExternalOutput")
    dr["xspill"] = nc.dram_tensor("xspill", [NTOK, D], F32)
    dr["ag_in"] = nc.dram_tensor("ag_in", [D, NTOK], BF16)
    dr["ag_out"] = nc.dram_tensor("ag_out", [2*D, NTOK], BF16)
    dr["y_in"] = nc.dram_tensor("y_in", [T, D], BF16)
    dr["y_out"] = nc.dram_tensor("y_out", [NTOK, D], BF16)

    with ExitStack() as gctx:
        p = Prog(nc)
        gctx.enter_context(p.ctx)
        ident = gctx.enter_context(nc.sbuf_tensor("identsb", [128, 128], BF16))
        epscol = gctx.enter_context(nc.sbuf_tensor("epscol", [128, 1], F32))
        blk = gctx.enter_context(nc.Block())
        p.op("sp", lambda e: e.dma_start(out=ident[:], in_=dr["ident"][:, :]),
             inc="SPD", k=16)
        v_ident = p.cnt["SPD"]
        v_eps = p.op("dv", lambda e: e.memset(epscol[:], EPS), inc="DV")

        with ExitStack() as f1ctx:
            b = {"ident": ident, "eps": epscol}
            x1 = f1ctx.enter_context(nc.sbuf_tensor("x1", [128, TT*1024], F32))
            alloc_ffn_bufs(nc, f1ctx, b, "c0")
            b["x"] = x1
            p.op("sp", lambda e: e.dma_start(
                out=x1.ap().rearrange("p (t n) -> p t n", t=TT),
                in_=dr["x_own"].ap().rearrange("(t p) n -> p t n", p=128)),
                inc="SPD", k=16)
            v_x = p.cnt["SPD"]
            x_rdy = [("SPD", v_x)] * TT
            p.wait("te", "SPD", v_ident)
            b["x_ready"] = x_rdy
            if True:
                info1 = emit_ffn(p, b, dr, x1, 0)
                b.pop("x_ready", None)
                with ExitStack() as ps0:
                    tpA = ps0.enter_context(nc.psum_tensor("tpA_l2", [128, 128], BF16))
                    tpB = ps0.enter_context(nc.psum_tensor("tpB_l2", [128, 128], BF16))
                    p.fence_te()
                    v_xln2 = emit_ln(p, x1, b["ln"], b["xlnT"], ident, tpA, tpB)
                p.wait("sp", "DV", v_xln2)
                p.op("sp", lambda e: e.dma_start(
                    out=dr["xspill"].ap().rearrange("(t p) n -> p t n", p=128),
                    in_=x1.ap().rearrange("p (t n) -> p t n", t=TT)),
                    inc="SPD", k=16)
                p.op("sp", lambda e, _t=b["xlnT"]: e.dma_start(
                    out=r128(dr["ag_in"].ap()),
                    in_=_t.ap().rearrange("p (c n) -> p c n", c=KC)),
                    inc="SPD", k=16)
                v_agin = p.cnt["SPD"]
        emit_attention(p, nc, dr, ident, v_agin)
        with ExitStack() as f2ctx:
            b = {"ident": ident, "eps": epscol}
            x2 = f2ctx.enter_context(nc.sbuf_tensor("x2", [128, TT*1024], F32))
            alloc_ffn_bufs(nc, f2ctx, b, "c1")
            b["x"] = x2
            p.fence_dma("sp")
            p.op("sp", lambda e: e.dma_start(
                out=x2.ap().rearrange("p (t n) -> p t n", t=TT),
                in_=dr["xspill"].ap().rearrange("(t p) n -> p t n", p=128)),
                inc="SPD", k=16)
            v_xs = p.cnt["SPD"]
            x_rdy2 = []
            with nc.sbuf_tensor("ystg", [128, 2*1024], BF16) as ystg:
                v_adds = {}
                for tt in range(TT):
                    sl = tt % 2
                    if tt >= 2:
                        p.wait("sp", "DV", v_adds[tt - 2])
                    p.wait("sp", "RSS", 1)
                    p.op("sp", lambda e, tt=tt, sl=sl, ystg=ystg: e.dma_start(
                        out=ystg[:, sl*1024:(sl+1)*1024],
                        in_=dr["y_out"].ap().rearrange("(t p) n -> p t n", p=128)[:, tt]),
                        inc="SPD", k=16)
                    v_s = p.cnt["SPD"]
                    p.wait("dv", "SPD", max(v_s, v_xs))
                    v_adds[tt] = p.op("dv", lambda e, tt=tt, sl=sl, ystg=ystg: e.tensor_tensor(
                        x2[:, tt*1024:(tt+1)*1024], x2[:, tt*1024:(tt+1)*1024],
                        ystg[:, sl*1024:(sl+1)*1024], ALU.add), inc="DV")
                    x_rdy2.append(("DV", v_adds[tt]))
            b["x_ready"] = x_rdy2
            p.fence_dma("gp")
            p.op("gp", lambda e: e.dma_start(
                out=b["rw"].ap().rearrange("p (c n) -> p c n", c=KC),
                in_=r128(dr["router_w"].ap())), inc="GPD", k=16)
            p.op("gp", lambda e: e.dma_start(
                out=b["sgw"].ap(), in_=dr["sh_gate_w"][:, :]), inc="GPD", k=16)
            b["v_wsmall"] = p.cnt["GPD"]
            info2 = emit_ffn(p, b, dr, x2, 1)
            p.wait("sp", "DV", info2["x_dv"])
            p.op("sp", lambda e: e.dma_start(
                out=dr["out"].ap().rearrange("(t p) n -> p t n", p=128),
                in_=x2.ap().rearrange("p (t n) -> p t n", t=TT)),
                inc="SPD", k=16)
            p.wait("sp", "SPD", p.cnt["SPD"])
        p.flush(blk)
    return nc


def make_core_inputs(core, full):
    b, hg = core // 2, core % 2
    X = np.asarray(full["L0_attn_x"], np.float32).reshape(4*T, D)
    rows = slice(b*T + hg*NTOK, b*T + (hg+1)*NTOK)
    Wqkv = np.asarray(full["L0_attn_Wqkv"], np.float32)
    heads = range(hg*NH, (hg+1)*NH)
    qrows = np.concatenate([np.arange(h*64, (h+1)*64) for h in heads])
    sel = np.concatenate([qrows, D + qrows, 2*D + qrows])
    wqkvT = np.ascontiguousarray(Wqkv[sel].T)
    Wout = np.asarray(full["L0_attn_Wout"], np.float32)
    woutT = np.ascontiguousarray(Wout[:, hg*512:(hg+1)*512].T)
    alpha = np.asarray(full["L0_attn_alpha"], np.float32)[hg*NH:(hg+1)*NH]
    beta = np.asarray(full["L0_attn_beta"], np.float32)[hg*NH:(hg+1)*NH]
    a4 = np.ascontiguousarray(np.tile(alpha.T, (4, 1)))
    b4 = np.ascontiguousarray(np.tile(beta.T, (4, 1)))
    consts = make_consts()
    return {
        "x_own": np.ascontiguousarray(X[rows]),
        "router_w": np.asarray(full["L0_ffn_router_w"], np.float32),
        "sh_gate_w": np.ascontiguousarray(
            np.asarray(full["L0_ffn_sh_gate_w"], np.float32).reshape(KC, 128).T),
        "sh_gate": np.asarray(full["L0_ffn_sh_gate"], np.float32),
        "sh_up": np.asarray(full["L0_ffn_sh_up"], np.float32),
        "sh_down": np.asarray(full["L0_ffn_sh_down"], np.float32),
        "w_gate": np.asarray(full["L0_ffn_w_gate"], np.float32),
        "w_up": np.asarray(full["L0_ffn_w_up"], np.float32),
        "w_down": np.asarray(full["L0_ffn_w_down"], np.float32),
        "wqkvT": wqkvT, "woutT": woutT, "a4": a4, "b4": b4,
        **consts,
    }


_CONSTS = None

def make_consts():
    global _CONSTS
    if _CONSTS is not None:
        return _CONSTS
    R = 32
    om = (10000.0 ** (-2.0 * np.arange(R, dtype=np.float64) / 64.0))
    k = np.arange(T, dtype=np.float64)
    C = np.cos(om[:, None] * k[None, :])
    S = np.sin(om[:, None] * k[None, :])
    cs1 = np.concatenate([C, S, C, S], axis=0).astype(BF)
    cs2 = np.concatenate([S, C, S, C], axis=0).astype(BF)
    phi = np.concatenate([C, S], axis=0)
    phi2 = np.concatenate([phi, phi], axis=0).astype(BF)
    sgn = np.concatenate([-np.ones(32), np.ones(32), -np.ones(32),
                          np.ones(32)]).astype(np.float32)[:, None]
    i = np.arange(128)[:, None]
    j = np.arange(512)[None, :]
    masks = np.concatenate(
        [(j >= i + d*128).astype(np.float32) for d in range(4)], axis=1).astype(BF)
    _CONSTS = {
        "ident": np.eye(128).astype(BF),
        "cs1": np.ascontiguousarray(cs1), "cs2": np.ascontiguousarray(cs2),
        "phi2": np.ascontiguousarray(phi2), "sgnv": sgn, "masks": masks,
    }
    return _CONSTS


def kernel(**inputs):
    from concourse.bass_utils import run_bass_kernel_spmd
    nc = build_program()
    in_maps = [make_core_inputs(c, inputs) for c in range(8)]
    res = run_bass_kernel_spmd(nc, in_maps, core_ids=list(range(8)))
    out = np.empty((4 * T, D), dtype=np.float32)
    for c in range(8):
        b, hg = c // 2, c % 2
        out[b*T + hg*NTOK: b*T + (hg+1)*NTOK] = res.results[c]["out"]
    return out.reshape(4, T, D)
